# version 02
# baseline (speedup 1.0000x reference)
import numpy as np
import ml_dtypes
import concourse.bass as bass
import concourse.mybir as mybir
from concourse.bass_utils import run_bass_kernel_spmd
from contextlib import ExitStack

F32 = mybir.dt.float32
BF16 = mybir.dt.bfloat16
AF = mybir.ActivationFunctionType
ALU = mybir.AluOpType
AX = mybir.AxisListType.X
BF = ml_dtypes.bfloat16

NCORES = 8
NTOK = 2048
DM = 2048
EIN = 4096
NH = 64
HD = 64
DS = 128
CH = 64
NT = NTOK // 128
KD = DM // 128
KE = EIN // 128
EPROJ = 8512
NE = 16
SL = 128
HH = 512

ENGINES = ("tensor", "vector", "scalar", "gpsimd", "sync")
CONST_VALS = (1e-5, 1e-12, 0.0, 1.0, -1.0, 0.5, -0.5, 2.0, 0.0078125)


class Sem:
    __slots__ = ("h", "n", "step", "_cm")

    def __init__(self, nc, name, step=1):
        self._cm = nc.semaphore(name)
        self.h = self._cm.__enter__()
        self.n = 0
        self.step = step

    def inc(self, k=None):
        self.n += self.step if k is None else k
        return self.n

    def tok(self):
        return (self, self.n)


class Prog:

    def __init__(self, name=""):
        self.name = name
        self.items = []

    def op(self, eng, fn, waits=(), inc=None, inck=None):
        ws = []

        def add(w):
            if w is None:
                return
            if isinstance(w, list):
                for x in w:
                    add(x)
                return
            sem, tk = w
            if tk > 0:
                ws.append((sem, tk))

        for w in waits:
            add(w)
        incs = []
        tok = None
        if inc is not None:
            inc.inc(inck)
            tok = (inc, inc.n)
            incs = [(inc, inc.step if inck is None else inck)]
        self.items.append((eng, fn, tuple(ws), incs))
        return tok

    def emit(self, nc):
        by_eng = {}
        for it in self.items:
            by_eng.setdefault(it[0], []).append(it)
        with nc.Block() as b:
            for eng_name in ENGINES:
                items = by_eng.get(eng_name)
                if not items:
                    continue

                def body(e, items=items):
                    waited = {}
                    for engn, fn, waits, incs in items:
                        for (sem, tk) in waits:
                            if waited.get(id(sem), 0) < tk:
                                e.wait_ge(sem.h, tk)
                                waited[id(sem)] = tk
                        instr = fn(e)
                        for (sem, k) in incs:
                            instr.then_inc(sem.h, k)

                getattr(b, eng_name)(body)


def build(dbg=(), phases=99, m1_parts=("zx", "bc", "dt"), dt_stop=9):
    nc = bass.Bass("TRN2", target_bir_lowering=False)
    dkind = lambda n: "ExternalOutput" if n in dbg else "Internal"

    x_in = nc.dram_tensor("x_in", [NTOK, DM], F32, kind="ExternalInput")
    w_in = nc.dram_tensor("w_in", [EPROJ, DM], F32, kind="ExternalInput")
    w_out = nc.dram_tensor("w_out", [DM, EIN], F32, kind="ExternalInput")
    p_D = nc.dram_tensor("p_D", [1, NH], F32, kind="ExternalInput")
    p_dtb = nc.dram_tensor("p_dtb", [1, NH], F32, kind="ExternalInput")
    p_alog = nc.dram_tensor("p_alog", [1, NH], F32, kind="ExternalInput")
    p_bnw = nc.dram_tensor("p_bnw", [1, DS], F32, kind="ExternalInput")
    p_cnw = nc.dram_tensor("p_cnw", [1, DS], F32, kind="ExternalInput")
    p_lnw = nc.dram_tensor("p_lnw", [1, DM], F32, kind="ExternalInput")
    p_lnb = nc.dram_tensor("p_lnb", [1, DM], F32, kind="ExternalInput")
    p_gam = nc.dram_tensor("p_gam", [1, DM], F32, kind="ExternalInput")
    p_bet = nc.dram_tensor("p_bet", [1, DM], F32, kind="ExternalInput")
    p_scale = nc.dram_tensor("p_scale", [1, 1], F32, kind="ExternalInput")
    mu = nc.dram_tensor("mu", [DM, NE * SL], F32, kind="ExternalInput")
    w1 = nc.dram_tensor("w1", [NE, DM, HH], F32, kind="ExternalInput")
    b1 = nc.dram_tensor("b1", [NE, HH], F32, kind="ExternalInput")
    w2 = nc.dram_tensor("w2", [NE, HH, DM], F32, kind="ExternalInput")
    b2 = nc.dram_tensor("b2", [NE, DM], F32, kind="ExternalInput")

    c_idf = nc.dram_tensor("c_idf", [128, 128], F32, kind="ExternalInput")
    c_idb = nc.dram_tensor("c_idb", [128, 128], BF16, kind="ExternalInput")
    c_u2 = nc.dram_tensor("c_u2", [128, 128], F32, kind="ExternalInput")
    c_slow = nc.dram_tensor("c_slow", [128, 128], F32, kind="ExternalInput")
    c_tri2 = nc.dram_tensor("c_tri2", [128, CH], F32, kind="ExternalInput")
    c_half = nc.dram_tensor("c_half", [128, 2], F32, kind="ExternalInput")
    c_o1f = nc.dram_tensor("c_o1f", [1, 128], F32, kind="ExternalInput")
    c_o1b = nc.dram_tensor("c_o1b", [1, 128], BF16, kind="ExternalInput")
    c_ocb = nc.dram_tensor("c_ocb", [128, 1], BF16, kind="ExternalInput")
    c_ocf = nc.dram_tensor("c_ocf", [128, 1], F32, kind="ExternalInput")
    c_eps = nc.dram_tensor("c_eps", [128, 16], F32, kind="ExternalInput")

    out = nc.dram_tensor("out", [NTOK, DM], F32, kind="ExternalOutput")

    d_xn = nc.dram_tensor("d_xn", [NTOK, DM], BF16, kind=dkind("d_xn"))
    d_xln = nc.dram_tensor("d_xln", [NTOK, DM], BF16, kind=dkind("d_xln"))
    d_xnorm = nc.dram_tensor("d_xnorm", [DM, NTOK], BF16, kind=dkind("d_xnorm"))
    d_winb = nc.dram_tensor("d_winb", [EPROJ, DM], BF16, kind=dkind("d_winb"))
    d_woutb = nc.dram_tensor("d_woutb", [DM, EIN], BF16, kind=dkind("d_woutb"))
    d_x = nc.dram_tensor("d_x", [NTOK, EIN], BF16, kind=dkind("d_x"))
    d_sz = nc.dram_tensor("d_sz", [NTOK, EIN], BF16, kind=dkind("d_sz"))
    d_yg = nc.dram_tensor("d_yg", [EIN, NTOK], BF16, kind=dkind("d_yg"))
    d_mam = nc.dram_tensor("d_mam", [NTOK, DM], BF16, kind=dkind("d_mam"))
    d_slotin = nc.dram_tensor("d_slotin", [DM, NE * SL], BF16, kind=dkind("d_slotin"))
    d_expl = nc.dram_tensor("d_expl", [NE * SL, NTOK], BF16, kind=dkind("d_expl"))
    d_bc = nc.dram_tensor("d_bc", [2 * DS, NTOK], BF16, kind="ExternalOutput" if ("d_bc" in dbg or "d_m1" in dbg) else "Internal")
    d_un = nc.dram_tensor("d_un", [DM, NTOK], BF16, kind=dkind("d_un")) if "d_un" in dbg else None
    d_m1 = nc.dram_tensor("d_m1", [128, 5 * 1024 + 2048], F32, kind="ExternalOutput") if "d_m1" in dbg else None
    d_m2 = nc.dram_tensor("d_m2", [128, 3 * EIN + CH * NT], F32, kind="ExternalOutput") if "d_m2" in dbg else None

    with ExitStack() as TOP:
        ent = TOP.enter_context

        s_ld = Sem(nc, "s_ld", 16)
        s_wc = Sem(nc, "s_wc", 16)
        s_wo = Sem(nc, "s_wo", 16)
        s_pe = Sem(nc, "s_pe")
        s_ve = Sem(nc, "s_ve")
        s_se = Sem(nc, "s_se")
        s_po = Sem(nc, "s_po")
        _dma_sems = {}

        def dsem(buf, sw=False):
            k = (buf if isinstance(buf, str) else id(buf), sw)
            if k not in _dma_sems:
                _dma_sems[k] = Sem(nc, f"sd{len(_dma_sems)}", 16)
            return _dma_sems[k]

        idf = ent(nc.sbuf_tensor("idf", [128, 128], F32))
        idb = ent(nc.sbuf_tensor("idb", [128, 128], BF16))
        u2 = ent(nc.sbuf_tensor("u2", [128, 128], F32))
        slow = ent(nc.sbuf_tensor("slow", [128, 128], F32))
        tri2 = ent(nc.sbuf_tensor("tri2", [128, CH], F32))
        half = ent(nc.sbuf_tensor("half", [128, 2], F32))
        o1f = ent(nc.sbuf_tensor("o1f", [1, 128], F32))
        o1b = ent(nc.sbuf_tensor("o1b", [1, 128], BF16))
        ocb = ent(nc.sbuf_tensor("ocb", [128, 1], BF16))
        ocf = ent(nc.sbuf_tensor("ocf", [128, 1], F32))
        epsb = ent(nc.sbuf_tensor("epsb", [128, 16], F32))
        lnw_c = ent(nc.sbuf_tensor("lnw_c", [128, KD], F32))
        lnb_c = ent(nc.sbuf_tensor("lnb_c", [128, KD], F32))
        gam_c = ent(nc.sbuf_tensor("gam_c", [128, KD], F32))
        bet_c = ent(nc.sbuf_tensor("bet_c", [128, KD], F32))
        bnw_c = ent(nc.sbuf_tensor("bnw_c", [128, 1], F32))
        cnw_c = ent(nc.sbuf_tensor("cnw_c", [128, 1], F32))
        sc_rep = ent(nc.sbuf_tensor("sc_rep", [128, 1], F32))
        a_rep = ent(nc.sbuf_tensor("a_rep", [128, NH], F32))
        dtb_rep = ent(nc.sbuf_tensor("dtb_rep", [128, NH], F32))
        dd_rep = ent(nc.sbuf_tensor("dd_rep", [128, NH], F32))
        M2S = ExitStack()
        b_fm = M2S.enter_context(nc.sbuf_tensor("b_fm", [128, NTOK], BF16))
        c_fm = M2S.enter_context(nc.sbuf_tensor("c_fm", [128, NTOK], BF16))
        dA = M2S.enter_context(nc.sbuf_tensor("dA", [128, NH * NT], F32))
        eacs = M2S.enter_context(nc.sbuf_tensor("eacs", [128, NH * NT], F32))
        dec = M2S.enter_context(nc.sbuf_tensor("dec", [128, NH * NT], F32))
        stm = M2S.enter_context(nc.sbuf_tensor("stm", [128, CH * NT], BF16))
        cdrep = M2S.enter_context(nc.sbuf_tensor("cdrep", [128, 2 * NH * NT], F32))

        PS = [ent(nc.psum_tensor(f"ps{i}", [128, 512], F32)) for i in range(8)]

        UNS = ExitStack()
        un_fm = [UNS.enter_context(nc.sbuf_tensor(f"un{k}", [128, NTOK], BF16)) for k in range(KD)]

        free = {}

        def FREE(buf):
            v = free.get(id(buf), [])
            return v if isinstance(v, list) else [v]

        def SETFREE(buf, *toks):
            free[id(buf)] = [t for t in toks if t is not None]

        def guard(P):
            toks = [sm.tok() for sm in _dma_sems.values() if sm.n > 0]
            for eng in ENGINES:
                P.op(eng, lambda e: e.nop(), waits=toks)

        def rep_fill(P, row_dram, width, dst, buf, ps_base, dep0, negate=False, do_exp=False):
            t_ld = P.op("sync", lambda e, b_=buf, rd=row_dram, w=width: e.dma_start(
                b_[0:1, 0:w], rd[0:1, 0:w]), waits=[FREE(buf), dep0], inc=dsem(buf))
            dep = t_ld
            if do_exp:
                dep = P.op("scalar", lambda e, b_=buf, w=width: e.activation(
                    b_[0:1, 0:w], b_[0:1, 0:w], AF.Exp), waits=[t_ld], inc=s_se)
            last = None
            for j in range((width + 511) // 512):
                w = min(512, width - j * 512)
                pk = PS[ps_base + (j % 2)]
                tt = P.op("tensor", lambda e, j=j, w=w, pk=pk, b_=buf: e.matmul(
                    pk[:, 0:w], o1f[0:1, :], b_[0:1, 512 * j:512 * j + w],
                    start=True, stop=True), waits=[dep, FREE(pk)], inc=s_pe)
                if negate:
                    tv = P.op("vector", lambda e, j=j, w=w, pk=pk: e.tensor_scalar_mul(
                        dst[:, 512 * j:512 * j + w], pk[:, 0:w], -1.0), waits=[tt], inc=s_ve)
                else:
                    tv = P.op("vector", lambda e, j=j, w=w, pk=pk: e.tensor_copy(
                        dst[:, 512 * j:512 * j + w], pk[:, 0:w]), waits=[tt], inc=s_ve)
                SETFREE(pk, tv)
                last = tv
            SETFREE(buf, last)
            return last

        with ExitStack() as S:
            rowbuf = S.enter_context(nc.sbuf_tensor("rowbuf", [1, DM], F32, side="right"))
            rowbuf2 = S.enter_context(nc.sbuf_tensor("rowbuf2", [1, DM], F32, side="right"))
            P = Prog("B0")
            for dst, src in [(idf, c_idf), (idb, c_idb), (u2, c_u2), (slow, c_slow),
                             (tri2, c_tri2), (half, c_half), (o1f, c_o1f),
                             (o1b, c_o1b), (ocb, c_ocb), (ocf, c_ocf), (epsb, c_eps)]:
                P.op("sync", lambda e, d=dst, s=src: e.dma_start(d[:, :], s[:, :]), inc=s_ld)
            t_consts = s_ld.tok()
            for ci, cv in enumerate(CONST_VALS):
                nc.const_aps.register(F32, cv, epsb[:, ci:ci + 1])

            for i in range(8):
                r0, r1 = i * 1064, (i + 1) * 1064
                P.op("gpsimd", lambda e, r0=r0, r1=r1: e.dma_start(
                    d_winb[r0:r1, :], w_in[r0:r1, :]), inc=s_wc)
            t_wc_in = s_wc.tok()
            for i in range(4):
                r0, r1 = i * 512, (i + 1) * 512
                P.op("gpsimd", lambda e, r0=r0, r1=r1: e.dma_start(
                    d_woutb[r0:r1, :], w_out[r0:r1, :]), inc=s_wo)
            t_wc_out = s_wo.tok()

            def col_fill(row_dram, ncols, dst_col, buf, ps_base):
                t_ld = P.op("sync", lambda e, b_=buf, rd=row_dram, n=ncols: e.dma_start(
                    b_[0:1, 0:n * 128], rd[0:1, 0:n * 128]),
                    waits=[FREE(buf), t_consts], inc=dsem(buf))
                last = None
                for k in range(ncols):
                    pk = PS[ps_base + (k % 2)]
                    tt = P.op("tensor", lambda e, k=k, pk=pk, b_=buf: e.transpose(
                        pk[:, 0:1], b_[0:1, 128 * k:128 * (k + 1)], o1f[0:1, 0:1]),
                        waits=[t_ld, FREE(pk)], inc=s_pe)
                    tv = P.op("vector", lambda e, k=k, pk=pk: e.tensor_copy(
                        dst_col[:, k:k + 1], pk[:, 0:1]), waits=[tt], inc=s_ve)
                    SETFREE(pk, tv)
                    last = tv
                SETFREE(buf, last)

            col_fill(p_lnw, KD, lnw_c, rowbuf, 0)
            col_fill(p_lnb, KD, lnb_c, rowbuf2, 2)
            col_fill(p_gam, KD, gam_c, rowbuf, 0)
            col_fill(p_bet, KD, bet_c, rowbuf2, 2)
            col_fill(p_bnw, 1, bnw_c, rowbuf, 0)
            col_fill(p_cnw, 1, cnw_c, rowbuf2, 2)

            rep_fill(P, p_scale, 1, sc_rep, rowbuf, 4, t_consts)
            rep_fill(P, p_dtb, NH, dtb_rep, rowbuf2, 4, t_consts)
            rep_fill(P, p_D, NH, dd_rep, rowbuf, 4, t_consts)
            rep_fill(P, p_alog, NH, a_rep, rowbuf2, 4, t_consts, negate=True, do_exp=True)
            P.emit(nc)

        if phases >= 1:
            with ExitStack() as S:
                e0 = S.enter_context
                gam_rep = e0(nc.sbuf_tensor("gam_rep", [128, DM], F32, side="right"))
                bet_rep = e0(nc.sbuf_tensor("bet_rep", [128, DM], F32, side="right"))
                rowb = e0(nc.sbuf_tensor("rowb", [1, DM], F32, side="right"))
                xbuf = [e0(nc.sbuf_tensor(f"xb{i}", [128, DM], F32, side="right")) for i in range(2)]
                xn32 = [e0(nc.sbuf_tensor(f"xn32{i}", [128, DM], F32, side="right")) for i in range(2)]
                z232 = [e0(nc.sbuf_tensor(f"z232{i}", [128, DM], F32, side="right")) for i in range(2)]
                xl32 = [e0(nc.sbuf_tensor(f"xl32{i}", [128, DM], F32, side="right")) for i in range(2)]
                xstg = [e0(nc.sbuf_tensor(f"xstg{i}", [128, DM], BF16, side="right")) for i in range(2)]
                ustg = [e0(nc.sbuf_tensor(f"ustg{i}", [128, 128], F32, side="right")) for i in range(2)]
                l2row = e0(nc.sbuf_tensor("l2row", [1, 128], F32, side="right"))
                st1 = e0(nc.sbuf_tensor("st1", [128, 8], F32, side="right"))
                st2 = e0(nc.sbuf_tensor("st2", [128, 8], F32, side="right"))

                P = Prog("P0")
                rep_fill(P, p_gam, DM, gam_rep, rowb, 4, None)
                rep_fill(P, p_bet, DM, bet_rep, rowb, 4, None)

                t_l2prev = None
                for r in range(NT):
                    bi = r % 2
                    xb, xn_t, z2_t, xl_t = xbuf[bi], xn32[bi], z232[bi], xl32[bi]
                    xst = xstg[bi]

                    t_ld = P.op("sync", lambda e, r=r, xb=xb: e.dma_start(
                        xb[:, :], x_in[128 * r:128 * (r + 1), :]), waits=[FREE(xb)], inc=dsem(xb))
                    t_s1 = P.op("vector", lambda e, xb=xb: e.reduce_sum(
                        st1[:, 0:1], xb[:, :], axis=AX), waits=[t_ld, FREE(st1)], inc=s_ve)
                    t_s2 = P.op("scalar", lambda e, xb=xb, xn_t=xn_t: e.activation(
                        xn_t[:, :], xb[:, :], AF.Square, accum_out=st1[:, 1:2]),
                        waits=[t_ld, FREE(xn_t), t_s1], inc=s_se)
                    t_m = P.op("vector", lambda e: e.tensor_scalar_mul(
                        st1[:, 2:3], st1[:, 0:1], 1.0 / DM), waits=[t_s1, t_s2], inc=s_ve)
                    t_v = P.op("vector", lambda e: e.tensor_scalar_mul(
                        st1[:, 3:4], st1[:, 1:2], 1.0 / DM), waits=[t_m], inc=s_ve)
                    t_msq = P.op("vector", lambda e: e.tensor_tensor(
                        st1[:, 4:5], st1[:, 2:3], st1[:, 2:3], ALU.mult), waits=[t_v], inc=s_ve)
                    t_v2 = P.op("vector", lambda e: e.tensor_tensor(
                        st1[:, 5:6], st1[:, 3:4], st1[:, 4:5], ALU.subtract), waits=[t_msq], inc=s_ve)
                    t_lnv = P.op("scalar", lambda e: e.activation(
                        st1[:, 6:7], st1[:, 5:6], AF.Ln, bias=1e-5), waits=[t_v2], inc=s_se)
                    t_rs = P.op("scalar", lambda e: e.activation(
                        st1[:, 7:8], st1[:, 6:7], AF.Exp, scale=-0.5), waits=[t_lnv], inc=s_se)
                    t_xn = P.op("vector", lambda e, xb=xb, xn_t=xn_t: e.tensor_scalar(
                        xn_t[:, :], xb[:, :], st1[:, 2:3], st1[:, 7:8], ALU.subtract, ALU.mult),
                        waits=[t_rs], inc=s_ve)
                    t_st1 = P.op("gpsimd", lambda e, r=r, xn_t=xn_t: e.dma_start(
                        d_xn[128 * r:128 * (r + 1), :], xn_t[:, :]), waits=[t_xn], inc=dsem(xn_t, True))
                    t2_s1 = P.op("vector", lambda e, xn_t=xn_t: e.reduce_sum(
                        st2[:, 0:1], xn_t[:, :], axis=AX), waits=[t_xn, FREE(st2)], inc=s_ve)
                    t2_s2 = P.op("scalar", lambda e, xn_t=xn_t, z2_t=z2_t: e.activation(
                        z2_t[:, :], xn_t[:, :], AF.Square, accum_out=st2[:, 1:2]),
                        waits=[t_rs, t2_s1, FREE(z2_t)], inc=s_se)
                    t2_m = P.op("vector", lambda e: e.tensor_scalar_mul(
                        st2[:, 2:3], st2[:, 0:1], 1.0 / DM), waits=[t2_s1, t2_s2], inc=s_ve)
                    t2_v = P.op("vector", lambda e: e.tensor_scalar_mul(
                        st2[:, 3:4], st2[:, 1:2], 1.0 / DM), waits=[t2_m], inc=s_ve)
                    t2_msq = P.op("vector", lambda e: e.tensor_tensor(
                        st2[:, 4:5], st2[:, 2:3], st2[:, 2:3], ALU.mult), waits=[t2_v], inc=s_ve)
                    t2_v2 = P.op("vector", lambda e: e.tensor_tensor(
                        st2[:, 5:6], st2[:, 3:4], st2[:, 4:5], ALU.subtract), waits=[t2_msq], inc=s_ve)
                    t2_lnv = P.op("scalar", lambda e: e.activation(
                        st2[:, 6:7], st2[:, 5:6], AF.Ln, bias=1e-5), waits=[t2_v2], inc=s_se)
                    t2_rs = P.op("scalar", lambda e: e.activation(
                        st2[:, 7:8], st2[:, 6:7], AF.Exp, scale=-0.5), waits=[t2_lnv], inc=s_se)
                    t_z2 = P.op("vector", lambda e, xn_t=xn_t, z2_t=z2_t: e.tensor_scalar(
                        z2_t[:, :], xn_t[:, :], st2[:, 2:3], st2[:, 7:8], ALU.subtract, ALU.mult),
                        waits=[t2_rs], inc=s_ve)
                    SETFREE(xn_t, t_z2, t_st1)
                    SETFREE(st2, t_z2)
                    t_g1 = P.op("gpsimd", lambda e, z2_t=z2_t, xl_t=xl_t: e.tensor_tensor(
                        xl_t[:, :], z2_t[:, :], gam_rep[:, :], ALU.mult),
                        waits=[t_z2, FREE(xl_t)], inc=s_po)
                    t_g2 = P.op("gpsimd", lambda e, xl_t=xl_t: e.tensor_tensor(
                        xl_t[:, :], xl_t[:, :], bet_rep[:, :], ALU.add), waits=[t_g1], inc=s_po)
                    t_st2 = P.op("gpsimd", lambda e, r=r, xl_t=xl_t: e.dma_start(
                        d_xln[128 * r:128 * (r + 1), :], xl_t[:, :]), waits=[t_g2], inc=dsem(xl_t, True))
                    t_ss = P.op("scalar", lambda e, xl_t=xl_t, xb=xb: e.activation(
                        xb[:, :], xl_t[:, :], AF.Square, accum_out=st1[:, 0:1]),
                        waits=[t_g2, t_xn, t2_s2, t_l2prev], inc=s_se)
                    SETFREE(xb, t_ss)
                    t_sq = P.op("scalar", lambda e: e.activation(
                        st1[:, 1:2], st1[:, 0:1], AF.Sqrt), waits=[t_ss], inc=s_se)
                    t_mx = P.op("vector", lambda e: e.tensor_scalar_max(
                        st1[:, 2:3], st1[:, 1:2], 1e-12), waits=[t_sq, t_z2], inc=s_ve)
                    t_rc = P.op("vector", lambda e: e.reciprocal(
                        st1[:, 3:4], st1[:, 2:3]), waits=[t_mx], inc=s_ve)
                    t_l2 = P.op("vector", lambda e: e.tensor_tensor(
                        st1[:, 4:5], st1[:, 3:4], sc_rep[:, 0:1], ALU.mult), waits=[t_rc], inc=s_ve)
                    t_l2prev = t_l2
                    SETFREE(st1, t_l2)
                    t_tr = P.op("tensor", lambda e: e.transpose(
                        PS[5][0:1, 0:128], st1[:, 4:5], idf[:, :]),
                        waits=[t_l2, FREE(PS[5])], inc=s_pe)
                    t_lr = P.op("vector", lambda e: e.tensor_copy(
                        l2row[0:1, :], PS[5][0:1, 0:128]), waits=[t_tr, FREE(l2row)], inc=s_ve)
                    SETFREE(PS[5], t_lr)
                    t_repm = P.op("tensor", lambda e: e.matmul(
                        PS[4][:, 0:128], o1f[0:1, :], l2row[0:1, :], start=True, stop=True),
                        waits=[t_lr, FREE(PS[4])], inc=s_pe)
                    SETFREE(l2row, t_repm)
                    last_xst = None
                    for k in range(KD):
                        pf = PS[k % 2]
                        ust = ustg[k % 2]
                        tt = P.op("tensor", lambda e, k=k, pf=pf, z2_t=z2_t: e.transpose(
                            pf[:, 0:128], z2_t[:, 128 * k:128 * (k + 1)], idf[:, :]),
                            waits=[t_z2, FREE(pf)], inc=s_pe)
                        tu = P.op("vector", lambda e, k=k, r=r, pf=pf: e.tensor_scalar(
                            un_fm[k][:, 128 * r:128 * (r + 1)], pf[:, 0:128],
                            lnw_c[:, k:k + 1], lnb_c[:, k:k + 1], ALU.mult, ALU.add),
                            waits=[tt], inc=s_ve)
                        tx1 = P.op("vector", lambda e, k=k, pf=pf, ust=ust: e.tensor_scalar(
                            ust[:, :], pf[:, 0:128], gam_c[:, k:k + 1], bet_c[:, k:k + 1],
                            ALU.mult, ALU.add), waits=[tu, FREE(ust)], inc=s_ve)
                        SETFREE(pf, tx1)
                        tx2 = P.op("vector", lambda e, k=k, ust=ust, xst=xst: e.tensor_tensor(
                            xst[:, 128 * k:128 * (k + 1)], ust[:, :], PS[4][:, 0:128], ALU.mult),
                            waits=[tx1, t_repm, FREE(xst) if k == 0 else None], inc=s_ve)
                        SETFREE(ust, tx2)
                        last_xst = tx2
                    SETFREE(PS[4], last_xst)
                    SETFREE(z2_t, s_pe.tok(), t_g1)
                    dstap = d_xnorm.ap().rearrange("(k p) n -> p k n", p=128)
                    t_st3 = P.op("sync", lambda e, r=r, xst=xst, dstap=dstap: e.dma_start(
                        dstap[:, :, 128 * r:128 * (r + 1)],
                        xst[:, :].rearrange("p (k n) -> p k n", k=KD)),
                        waits=[last_xst], inc=dsem(xst))
                    SETFREE(xst, t_st3)
                    SETFREE(xl_t, t_ss, t_st2)
                P.emit(nc)


        if phases >= 2:
            with ExitStack() as S:
                e1 = S.enter_context
                wslot = [e1(nc.sbuf_tensor(f"ws{i}", [128, 16 * 512], BF16, side="right")) for i in range(2)]
                ostg = [e1(nc.sbuf_tensor(f"og{i}", [128, 512], BF16, side="right")) for i in range(4)]
                bsq = e1(nc.sbuf_tensor("bsq", [128, NTOK], F32, side="right"))
                ssrow = e1(nc.sbuf_tensor("ssrow", [1, NTOK], F32, side="right"))
                wdtn = e1(nc.sbuf_tensor("wdtn", [64, DM], BF16, side="right"))
                wdt = e1(nc.sbuf_tensor("wdt", [128, 16 * 64], BF16, side="right"))
                dtt = [e1(nc.sbuf_tensor(f"dtt{i}", [128, 64], F32, side="right")) for i in range(4)]
                acsb = e1(nc.sbuf_tensor("acsb", [128, 64], F32, side="right"))
                rowc = e1(nc.sbuf_tensor("rowc", [1, 128], F32, side="right"))
                rowcd = e1(nc.sbuf_tensor("rowcd", [1, 128], F32, side="right"))

                P = Prog("M1")
                guard(P)

                def load_wslot(ws, e0, width):
                    toks = None
                    nk = KD
                    for k in range(nk):
                        toks = P.op("sync", lambda e, ws=ws, e0=e0, k=k, width=width: e.dma_start(
                            ws[:, width * k:width * (k + 1)],
                            d_winb[e0:e0 + width, 128 * k:128 * (k + 1)], transpose=True),
                            waits=[FREE(ws) if k == 0 else None,
                                   (s_wc, s_wc.n) if k == 0 else None], inc=dsem(ws))
                    return toks

                t_w = {0: load_wslot(wslot[0], 0, 512)} if "zx" in m1_parts else {}
                for blk in (range(16) if "zx" in m1_parts else []):
                    e0 = blk * 512
                    is_z = blk < 8
                    ws = wslot[blk % 2]
                    if blk + 1 < 16:
                        t_w[blk + 1] = load_wslot(wslot[(blk + 1) % 2], (blk + 1) * 512, 512)
                    last_mm_of_ws = None
                    for r in range(NT):
                        pp = PS[r % 4]
                        t_mm = None
                        for k in range(KD):
                            t_mm = P.op("tensor", lambda e, pp=pp, k=k, r=r, ws=ws: e.matmul(
                                pp[:, 0:512], un_fm[k][:, 128 * r:128 * (r + 1)],
                                ws[:, 512 * k:512 * (k + 1)], start=(k == 0), stop=(k == KD - 1)),
                                waits=[t_w[blk], FREE(pp)] if k == 0 else [], inc=s_pe if k == KD - 1 else None)
                        last_mm_of_ws = t_mm
                        og = ostg[r % 4]
                        if is_z:
                            t_e = P.op("scalar", lambda e, pp=pp, og=og: e.activation(
                                og[:, :], pp[:, 0:512], AF.Silu), waits=[t_mm, FREE(og)], inc=s_se)
                        else:
                            t_e = P.op("vector", lambda e, pp=pp, og=og: e.tensor_copy(
                                og[:, :], pp[:, 0:512]), waits=[t_mm, FREE(og)], inc=s_ve)
                        SETFREE(pp, t_e)
                        dst = d_sz if is_z else d_x
                        c0 = e0 if is_z else e0 - EIN
                        t_st = P.op("sync", lambda e, r=r, og=og, dst=dst, c0=c0: e.dma_start(
                            dst[128 * r:128 * (r + 1), c0:c0 + 512], og[:, :]),
                            waits=[t_e], inc=dsem(og))
                        SETFREE(og, t_st)
                    SETFREE(ws, last_mm_of_ws)

                wsbc = wslot[0]
                for bc in (range(2) if "bc" in m1_parts else []):
                    e0 = 2 * EIN + bc * DS
                    tgt = b_fm if bc == 0 else c_fm
                    wnorm = bnw_c if bc == 0 else cnw_c
                    t_wl = None
                    for k in range(KD):
                        t_wl = P.op("sync", lambda e, e0=e0, k=k: e.dma_start(
                            wsbc[:, 128 * k:128 * (k + 1)],
                            d_winb[e0:e0 + DS, 128 * k:128 * (k + 1)], transpose=True),
                            waits=[FREE(wsbc) if k == 0 else None], inc=dsem(wsbc))
                    t_raws = []
                    for ns in range(4):
                        pp = PS[ns % 4]
                        t_mm = None
                        for k in range(KD):
                            t_mm = P.op("tensor", lambda e, pp=pp, k=k, ns=ns: e.matmul(
                                pp[:, 0:512], wsbc[:, 128 * k:128 * (k + 1)],
                                un_fm[k][:, 512 * ns:512 * (ns + 1)],
                                start=(k == 0), stop=(k == KD - 1)),
                                waits=[t_wl, FREE(pp)] if k == 0 else [], inc=s_pe if k == KD - 1 else None)
                        t_sqv = P.op("scalar", lambda e, pp=pp, ns=ns: e.activation(
                            bsq[:, 512 * ns:512 * (ns + 1)], pp[:, 0:512], AF.Square),
                            waits=[t_mm, FREE(bsq) if ns == 0 else None], inc=s_se)
                        t_raw = P.op("vector", lambda e, pp=pp, ns=ns, tgt=tgt: e.tensor_copy(
                            tgt[:, 512 * ns:512 * (ns + 1)], pp[:, 0:512]),
                            waits=[t_mm, t_sqv], inc=s_ve)
                        t_raws.append(t_raw)
                        SETFREE(pp, t_sqv, t_raw)
                    SETFREE(wsbc, s_pe.tok())
                    t_evs = []
                    for ns in range(4):
                        pq = PS[4 + ns % 2]
                        t_nm = P.op("tensor", lambda e, pq=pq, ns=ns: e.matmul(
                            pq[0:1, 0:512], ocf[:, :], bsq[:, 512 * ns:512 * (ns + 1)],
                            start=True, stop=True),
                            waits=[(s_se, s_se.n), FREE(pq)], inc=s_pe)
                        t_ev = P.op("vector", lambda e, pq=pq, ns=ns: e.tensor_copy(
                            ssrow[0:1, 512 * ns:512 * (ns + 1)], pq[0:1, 0:512]),
                            waits=[t_nm, FREE(ssrow) if ns == 0 else None], inc=s_ve)
                        SETFREE(pq, t_ev)
                        t_evs.append(t_ev)
                    SETFREE(bsq, s_pe.tok())
                    t_ln = P.op("scalar", lambda e: e.activation(
                        ssrow[0:1, :], ssrow[0:1, :], AF.Ln, scale=0.0078125, bias=1e-5),
                        waits=t_evs, inc=s_se)
                    t_ex = P.op("scalar", lambda e: e.activation(
                        ssrow[0:1, :], ssrow[0:1, :], AF.Exp, scale=-0.5),
                        waits=[t_ln], inc=s_se)
                    for ns in range(4):
                        pq = PS[4 + ns % 2]
                        t_bc2 = P.op("tensor", lambda e, pq=pq, ns=ns: e.matmul(
                            pq[:, 0:512], o1f[0:1, :], ssrow[0:1, 512 * ns:512 * (ns + 1)],
                            start=True, stop=True), waits=[t_ex, FREE(pq)], inc=s_pe)
                        t_fin = P.op("vector", lambda e, pq=pq, ns=ns, tgt=tgt, wn=wnorm: e.scalar_tensor_tensor(
                            tgt[:, 512 * ns:512 * (ns + 1)], tgt[:, 512 * ns:512 * (ns + 1)],
                            wn[:, 0:1], pq[:, 0:512], ALU.mult, ALU.mult),
                            waits=[t_bc2, t_raws[ns]], inc=s_ve)
                        SETFREE(pq, t_fin)
                    SETFREE(ssrow, s_pe.tok())

                do_dt = "dt" in m1_parts
                t_dn = P.op("gpsimd", lambda e: e.dma_start(
                    wdtn[:, :], w_in[2 * EIN + 2 * DS:EPROJ, :]), inc=dsem(wdtn, True)) if do_dt else None
                t_tp = None
                for k in (range(KD) if do_dt else []):
                    pq = PS[4 + k % 2]
                    t_t = P.op("tensor", lambda e, k=k, pq=pq: e.transpose(
                        pq[:, 0:32].bitcast(BF16),
                        wdtn[0:64, 128 * k:128 * (k + 1)], idb[0:64, 0:64]),
                        waits=[t_dn, FREE(pq)], inc=s_pe)
                    t_tp = P.op("vector", lambda e, k=k, pq=pq: e.tensor_copy(
                        wdt[:, 64 * k:64 * (k + 1)], pq[:, 0:32].bitcast(BF16)), waits=[t_t], inc=s_ve)
                    SETFREE(pq, t_tp)
                for r in (range(NT) if do_dt else []):
                    pp = PS[r % 4]
                    t_mm = None
                    for k in range(KD):
                        t_mm = P.op("tensor", lambda e, pp=pp, k=k, r=r: e.matmul(
                            pp[:, 0:64], un_fm[k][:, 128 * r:128 * (r + 1)],
                            wdt[:, 64 * k:64 * (k + 1)], start=(k == 0), stop=(k == KD - 1)),
                            waits=[(s_ve, s_ve.n), FREE(pp)] if k == 0 else [],
                            inc=s_pe if k == KD - 1 else None)
                    t_d0 = P.op("vector", lambda e, pp=pp: e.tensor_tensor(
                        dtt[0][:, :], pp[:, 0:64], dtb_rep[:, :], ALU.add),
                        waits=[t_mm, FREE(dtt[0])], inc=s_ve)
                    SETFREE(pp, t_d0)
                    t_ab = P.op("scalar", lambda e: e.activation(
                        dtt[1][:, :], dtt[0][:, :], AF.Abs),
                        waits=[t_d0, FREE(dtt[1])], inc=s_se)
                    t_e1 = P.op("scalar", lambda e: e.activation(
                        dtt[1][:, :], dtt[1][:, :], AF.Exp, scale=-1.0), waits=[t_ab], inc=s_se)
                    t_l1 = P.op("scalar", lambda e: e.activation(
                        dtt[1][:, :], dtt[1][:, :], AF.Ln, bias=1.0), waits=[t_e1], inc=s_se)
                    t_r0 = P.op("vector", lambda e: e.tensor_scalar_max(
                        dtt[2][:, :], dtt[0][:, :], 0.0), waits=[t_ab, FREE(dtt[2])], inc=s_ve)
                    SETFREE(dtt[0], t_r0)
                    t_dtp = P.op("vector", lambda e: e.tensor_tensor(
                        dtt[3][:, :], dtt[2][:, :], dtt[1][:, :], ALU.add),
                        waits=[t_l1, t_r0, FREE(dtt[3])], inc=s_ve)
                    SETFREE(dtt[1], t_dtp)
                    SETFREE(dtt[2], t_dtp)
                    t_dA = P.op("vector", lambda e, r=r: e.tensor_tensor(
                        dA[:, 64 * r:64 * (r + 1)], dtt[3][:, :], a_rep[:, :], ALU.mult),
                        waits=[t_dtp], inc=s_ve)
                    SETFREE(dtt[3], t_dA)
                    if dt_stop < 2:
                        continue
                    t_acs = P.op("tensor", lambda e, r=r: e.matmul(
                        PS[5][:, 0:64], u2[:, :], dA[:, 64 * r:64 * (r + 1)],
                        start=True, stop=True), waits=[t_dA, FREE(PS[5])], inc=s_pe)
                    t_eacs = P.op("scalar", lambda e, r=r: e.activation(
                        eacs[:, 64 * r:64 * (r + 1)], PS[5][:, 0:64], AF.Exp),
                        waits=[t_acs], inc=s_se)
                    t_acsb = P.op("vector", lambda e: e.tensor_copy(
                        acsb[:, :], PS[5][:, 0:64]), waits=[t_acs, t_eacs, FREE(acsb)], inc=s_ve)
                    SETFREE(PS[5], t_eacs, t_acsb)
                    if dt_stop < 3:
                        continue
                    t_c0 = P.op("tensor", lambda e, r=r: e.matmul(
                        PS[6][0:1, 0:64], half[:, 0:1], dA[:, 64 * r:64 * (r + 1)],
                        start=True, stop=True), waits=[t_dA, FREE(PS[6])], inc=s_pe)
                    t_c1 = P.op("tensor", lambda e, r=r: e.matmul(
                        PS[6][0:1, 64:128], half[:, 1:2], dA[:, 64 * r:64 * (r + 1)],
                        start=True, stop=True), waits=[], inc=s_pe)
                    t_crow = P.op("vector", lambda e: e.tensor_copy(
                        rowc[0:1, :], PS[6][0:1, 0:128]), waits=[t_c1, FREE(rowc)], inc=s_ve)
                    SETFREE(PS[6], t_crow)
                    t_cdr = P.op("scalar", lambda e: e.activation(
                        rowcd[0:1, :], rowc[0:1, :], AF.Exp), waits=[t_crow, FREE(rowcd)], inc=s_se)
                    if dt_stop < 4:
                        continue
                    t_b0 = P.op("tensor", lambda e: e.matmul(
                        PS[7][0:64, 0:64], o1f[0:1, 0:64], rowc[0:1, 0:64],
                        start=True, stop=True), waits=[t_crow, FREE(PS[7])], inc=s_pe)
                    t_b1 = P.op("tensor", lambda e: e.matmul(
                        PS[7][64:128, 0:64], o1f[0:1, 0:64], rowc[0:1, 64:128],
                        start=True, stop=True, tile_position=(0, 64)), waits=[], inc=s_pe)
                    SETFREE(rowc, t_b1)
                    t_sub = P.op("vector", lambda e: e.tensor_tensor(
                        acsb[:, :], PS[7][:, 0:64], acsb[:, :], ALU.subtract),
                        waits=[t_b1, t_acsb], inc=s_ve)
                    SETFREE(PS[7], t_sub)
                    t_dec = P.op("scalar", lambda e, r=r: e.activation(
                        dec[:, 64 * r:64 * (r + 1)], acsb[:, :], AF.Exp),
                        waits=[t_sub], inc=s_se)
                    SETFREE(acsb, t_dec)
                    if dt_stop < 5:
                        continue
                    t_k0 = P.op("tensor", lambda e: e.matmul(
                        PS[4][:, 0:64], o1f[0:1, :], rowcd[0:1, 0:64],
                        start=True, stop=True), waits=[t_cdr, FREE(PS[4])], inc=s_pe)
                    t_k1 = P.op("tensor", lambda e: e.matmul(
                        PS[4][:, 64:128], o1f[0:1, :], rowcd[0:1, 64:128],
                        start=True, stop=True), waits=[], inc=s_pe)
                    t_cdw = P.op("vector", lambda e, r=r: e.tensor_copy(
                        cdrep[:, 128 * r:128 * (r + 1)], PS[4][:, 0:128]),
                        waits=[t_k1], inc=s_ve)
                    SETFREE(PS[4], t_cdw)
                    SETFREE(rowcd, t_k1)
                P.emit(nc)
            UNS.close()


        if phases >= 3:
            with ExitStack() as S:
                e2 = S.enter_context
                xsl = [e2(nc.sbuf_tensor(f"xsl{i}", [128, EIN], BF16, side="right")) for i in range(2)]
                szl = [e2(nc.sbuf_tensor(f"szl{i}", [128, EIN], BF16, side="right")) for i in range(2)]
                sLT = e2(nc.sbuf_tensor("sLT", [128, EIN], BF16, side="right"))
                srun = e2(nc.sbuf_tensor("srun", [128, EIN], BF16, side="right"))
                ygs = [e2(nc.sbuf_tensor(f"ygs{i}", [128, EIN], BF16, side="right")) for i in range(2)]
                ytp = [e2(nc.sbuf_tensor(f"ytp{i}", [128, EIN], BF16, side="right")) for i in range(2)]
                dAm = [e2(nc.sbuf_tensor(f"dAm{i}", [128, 512], F32, side="right")) for i in range(2)]
                eDs = [e2(nc.sbuf_tensor(f"eDs{i}", [128, 512], F32, side="right")) for i in range(2)]
                xds = [e2(nc.sbuf_tensor(f"xds{i}", [128, 512], BF16, side="right")) for i in range(2)]
                stp = [e2(nc.sbuf_tensor(f"stp{i}", [128, 512], BF16, side="right")) for i in range(2)]
                gb1 = [e2(nc.sbuf_tensor(f"gb1{i}", [128, 512], F32, side="right")) for i in range(2)]
                gb2 = [e2(nc.sbuf_tensor(f"gb2{i}", [128, 512], F32, side="right")) for i in range(2)]
                btm = e2(nc.sbuf_tensor("btm", [128, 128], BF16, side="right"))

                P = Prog("M2")
                guard(P)
                t_srz = P.op("gpsimd", lambda e: e.memset(srun[:, :], 0.0), inc=s_po)
                SETFREE(srun, t_srz)
                sLT_free = {}
                srun_dep = {}

                def bcap(t, pstride, pcnt, blocks, off):
                    return bass.AP(t.tensor if isinstance(t, bass.AP) else t, off,
                                   [[pstride, pcnt]] + [list(b) for b in blocks])

                for r in range(NT):
                    xb, zb = xsl[r % 2], szl[r % 2]
                    yg, yt = ygs[r % 2], ytp[r % 2]
                    dump_yg_tok = None
                    t_x = P.op("sync", lambda e, r=r, xb=xb: e.dma_start(
                        xb[:, :], d_x[128 * r:128 * (r + 1), :]), waits=[FREE(xb)], inc=dsem(xb))
                    t_z = P.op("sync", lambda e, r=r, zb=zb: e.dma_start(
                        zb[:, :], d_sz[128 * r:128 * (r + 1), :]), waits=[FREE(zb)], inc=dsem(zb))
                    t_sc0 = P.op("tensor", lambda e, r=r: e.matmul(
                        PS[6][0:64, 0:64], b_fm[:, 128 * r:128 * r + 64],
                        c_fm[:, 128 * r:128 * r + 64], start=True, stop=True),
                        waits=[FREE(PS[6])], inc=s_pe)
                    t_sc1 = P.op("tensor", lambda e, r=r: e.matmul(
                        PS[6][64:128, 0:64], b_fm[:, 128 * r + 64:128 * r + 128],
                        c_fm[:, 128 * r + 64:128 * r + 128], start=True, stop=True,
                        tile_position=(0, 64)), waits=[], inc=s_pe)
                    t_stm = P.op("vector", lambda e, r=r: e.tensor_tensor(
                        stm[:, 64 * r:64 * (r + 1)], PS[6][:, 0:64], tri2[:, :], ALU.mult),
                        waits=[t_sc1], inc=s_ve)
                    SETFREE(PS[6], t_stm)
                    t_bt0 = P.op("tensor", lambda e, r=r: e.transpose(
                        PS[7][0:64, 0:64].bitcast(BF16), b_fm[:, 128 * r:128 * r + 64], idb[:, :]),
                        waits=[FREE(PS[7])], inc=s_pe)
                    t_bt1 = P.op("tensor", lambda e, r=r: e.transpose(
                        PS[7][64:128, 0:64].bitcast(BF16), b_fm[:, 128 * r + 64:128 * r + 128],
                        idb[:, :], tile_position=(0, 64)), waits=[], inc=s_pe)
                    t_btm = P.op("vector", lambda e: e.tensor_copy(
                        btm[:, :], PS[7][:, 0:64].bitcast(BF16)), waits=[t_bt1, FREE(btm)], inc=s_ve)
                    SETFREE(PS[7], t_btm)

                    t_ygl = []
                    for q in range(4):
                        t_sLTs = {}
                        for j in (2 * q, 2 * q + 1):
                            da = dAm[j % 2]
                            ed = eDs[j % 2]
                            pD = PS[6] if j % 2 == 0 else PS[7]
                            t_dm = P.op("vector", lambda e, r=r, j=j, da=da: e.tensor_tensor(
                                da[:, :],
                                bcap(dA, NH * NT, 128, [[1, 8], [0, CH]], 64 * r + 8 * j),
                                bcap(tri2, CH, 128, [[0, 8], [1, CH]], 0),
                                ALU.mult), waits=[FREE(da)], inc=s_ve)
                            t_D = P.op("tensor", lambda e, pD=pD, da=da: e.matmul(
                                pD[:, 0:512], slow[:, :], da[:, :], start=True, stop=True),
                                waits=[t_dm, FREE(pD)], inc=s_pe)
                            SETFREE(da, t_D)
                            t_eD = P.op("scalar", lambda e, pD=pD, ed=ed: e.activation(
                                ed[:, :], pD[:, 0:512], AF.Exp), waits=[t_D, FREE(ed)], inc=s_se)
                            SETFREE(pD, t_eD)
                            t_sLT = P.op("vector", lambda e, r=r, j=j, ed=ed: e.tensor_tensor(
                                sLT[:, 512 * j:512 * (j + 1)], ed[:, :],
                                bcap(stm, CH * NT, 128, [[0, 8], [1, CH]], 64 * r),
                                ALU.mult), waits=[t_eD, t_stm, sLT_free.get(j)], inc=s_ve)
                            SETFREE(ed, t_sLT)
                            t_sLTs[j] = t_sLT
                        pY = {0: PS[0], 1: PS[1]}
                        t_yd = None
                        for cp in range(2):
                            for hh in range(16):
                                h = 16 * q + hh
                                j = h // 8
                                pb = pY[hh // 8]
                                t_yd = P.op("tensor", lambda e, cp=cp, h=h, hh=hh, pb=pb, xb=xb: e.matmul(
                                    pb[64 * cp:64 * (cp + 1), 64 * (hh % 8):64 * (hh % 8) + 64],
                                    sLT[64 * cp:64 * cp + 64, 64 * h:64 * h + 64],
                                    xb[64 * cp:64 * cp + 64, 64 * h:64 * h + 64],
                                    start=True, stop=True,
                                    tile_position=(64, 64) if cp == 1 else None),
                                    waits=[t_sLTs[j], t_x,
                                           FREE(pb) if (cp == 0 and hh % 8 == 0) else None],
                                    inc=s_pe if (cp == 1 and hh in (7, 15)) else None)
                                if cp == 1 and hh == 7:
                                    t_yd0 = (s_pe, s_pe.n)
                        t_yd1 = (s_pe, s_pe.n)
                        sLT_free[2 * q] = t_yd1
                        sLT_free[2 * q + 1] = t_yd1
                        t_xds = {}
                        for j in (2 * q, 2 * q + 1):
                            xd = xds[j % 2]
                            t_xd = P.op("gpsimd", lambda e, r=r, j=j, xd=xd, xb=xb: e.tensor_tensor(
                                xd[:, :], xb[:, 512 * j:512 * (j + 1)],
                                bcap(dec, NH * NT, 128, [[1, 8], [0, CH]], 64 * r + 8 * j),
                                ALU.mult), waits=[t_x, FREE(xd)], inc=s_po)
                            t_xds[j] = t_xd
                        pO = {2 * q: PS[2], 2 * q + 1: PS[3]}
                        pS = {2 * q: PS[4], 2 * q + 1: PS[5]}
                        t_upd = {}
                        t_yoff = {}
                        for cp in range(2):
                            for j in (2 * q, 2 * q + 1):
                                t_yo = P.op("tensor", lambda e, r=r, cp=cp, j=j, pb=pO[j]: e.matmul(
                                    pb[64 * cp:64 * (cp + 1), 0:512],
                                    c_fm[:, 128 * r + 64 * cp:128 * r + 64 * cp + 64],
                                    srun[:, 512 * j:512 * (j + 1)], start=True, stop=True,
                                    tile_position=(0, 64) if cp == 1 else None),
                                    waits=[FREE(pO[j]) if cp == 0 else None,
                                           t_upd.get((cp - 1, j)),
                                           srun_dep.get(j) if cp == 0 else None,
                                           t_srz if r == 0 else None],
                                    inc=s_pe)
                                t_yoff[(cp, j)] = t_yo
                                t_st = P.op("tensor", lambda e, cp=cp, j=j, pb=pS[j]: e.matmul(
                                    pb[:, 0:512], btm[64 * cp:64 * cp + 64, :],
                                    xds[j % 2][64 * cp:64 * cp + 64, :], start=True, stop=True),
                                    waits=[t_btm, t_xds[j], FREE(pS[j])], inc=s_pe)
                                t_u1 = P.op("vector", lambda e, r=r, cp=cp, j=j: e.tensor_tensor(
                                    stp[j % 2][:, :], srun[:, 512 * j:512 * (j + 1)],
                                    bcap(cdrep, 2 * NH * NT, 128, [[1, 8], [0, CH]],
                                         128 * r + 64 * cp + 8 * (j - 2 * q) + 16 * q),
                                    ALU.mult), waits=[t_yo, FREE(stp[j % 2])], inc=s_ve)
                                t_u2 = P.op("vector", lambda e, j=j, pb=pS[j]: e.tensor_tensor(
                                    srun[:, 512 * j:512 * (j + 1)], stp[j % 2][:, :],
                                    pb[:, 0:512], ALU.add), waits=[t_u1, t_st], inc=s_ve)
                                SETFREE(stp[j % 2], t_u2)
                                SETFREE(pS[j], t_u2)
                                t_upd[(cp, j)] = t_u2
                                if cp == 1:
                                    SETFREE(xds[j % 2], (s_pe, s_pe.n))
                                    srun_dep[j] = t_u2
                        for j in (2 * q, 2 * q + 1):
                            g1, g2 = gb1[j % 2], gb2[j % 2]
                            t_g1 = P.op("vector", lambda e, r=r, j=j, g1=g1, pb=pO[j]: e.tensor_tensor(
                                g1[:, :], pb[:, 0:512],
                                bcap(eacs, NH * NT, 128, [[1, 8], [0, CH]], 64 * r + 8 * j),
                                ALU.mult), waits=[t_yoff[(1, j)], FREE(g1)], inc=s_ve)
                            SETFREE(pO[j], t_g1)
                            t_g2 = P.op("gpsimd", lambda e, r=r, j=j, g2=g2, xb=xb: e.tensor_tensor(
                                g2[:, :], xb[:, 512 * j:512 * (j + 1)],
                                bcap(dd_rep, NH, 128, [[1, 8], [0, CH]], 8 * j),
                                ALU.mult), waits=[t_x, FREE(g2)], inc=s_po)
                            pb = pY[(j - 2 * q)]
                            t_g3 = P.op("vector", lambda e, g1=g1, pb=pb: e.scalar_tensor_tensor(
                                g1[:, :], pb[:, 0:512], epsb[:, 2:3], g1[:, :],
                                ALU.add, ALU.add), waits=[t_yd1, t_g1], inc=s_ve)
                            SETFREE(pb, t_g3)
                            t_g4 = P.op("gpsimd", lambda e, g1=g1, g2=g2: e.tensor_tensor(
                                g1[:, :], g1[:, :], g2[:, :], ALU.add),
                                waits=[t_g3, t_g2], inc=s_po)
                            SETFREE(g2, t_g4)
                            t_g5 = P.op("gpsimd", lambda e, j=j, g1=g1, yg=yg, zb=zb: e.tensor_tensor(
                                yg[:, 512 * j:512 * (j + 1)], g1[:, :],
                                zb[:, 512 * j:512 * (j + 1)], ALU.mult),
                                waits=[t_g4, t_z, FREE(yg) if j == 0 else None], inc=s_po)
                            SETFREE(g1, t_g5)
                            t_ygl.append(t_g5)
                    SETFREE(xb, s_po.tok(), s_pe.tok())
                    SETFREE(zb, s_po.tok())
                    SETFREE(btm, s_pe.tok(), t_btm)
                    if d_m2 is not None and r == 0:
                        td1 = P.op("gpsimd", lambda e: e.dma_start(d_m2[:, 0:EIN], sLT[:, :]),
                                   waits=[(s_ve, s_ve.n)], inc=dsem("dmp1", True))
                        td2 = P.op("gpsimd", lambda e: e.dma_start(d_m2[:, EIN:2 * EIN], srun[:, :]),
                                   waits=[(s_ve, s_ve.n)], inc=dsem("dmp2", True))
                        td3 = P.op("gpsimd", lambda e, yg=yg: e.dma_start(d_m2[:, 2 * EIN:3 * EIN], yg[:, :]),
                                   waits=[(s_po, s_po.n)], inc=dsem("dmp3", True))
                        for jj in range(8):
                            sLT_free[jj] = [sLT_free.get(jj), td1]
                            srun_dep[jj] = [srun_dep.get(jj), td2]
                        dump_yg_tok = td3
                    else:
                        dump_yg_tok = None
                    last_yt = None
                    prev_ev = {}
                    for k in range(KE):
                        pT = PS[6] if k % 2 == 0 else PS[7]
                        t_t = P.op("tensor", lambda e, k=k, pT=pT, yg=yg: e.transpose(
                            pT[:, 0:64].bitcast(BF16), yg[:, 128 * k:128 * (k + 1)], idb[:, :]),
                            waits=[t_ygl[k // 4] if k // 4 < len(t_ygl) else None,
                                   t_ygl[-1] if k == 0 else None,
                                   FREE(pT), prev_ev.get(k % 2)], inc=s_pe)
                        eng = "vector" if k % 2 == 0 else "scalar"
                        if eng == "vector":
                            t_ev = P.op("vector", lambda e, k=k, pT=pT, yt=yt: e.tensor_copy(
                                yt[:, 128 * k:128 * (k + 1)], pT[:, 0:64].bitcast(BF16)),
                                waits=[t_t, FREE(yt) if k == 0 else None], inc=s_ve)
                        else:
                            t_ev = P.op("scalar", lambda e, k=k, pT=pT, yt=yt: e.copy(
                                yt[:, 128 * k:128 * (k + 1)], pT[:, 0:64].bitcast(BF16)),
                                waits=[t_t, FREE(yt) if k == 1 else None], inc=s_se)
                        SETFREE(pT, t_ev)
                        prev_ev[k % 2] = t_ev
                        last_yt = t_ev
                    SETFREE(yg, s_pe.tok(), dump_yg_tok)
                    dstap = d_yg.ap().rearrange("(k p) n -> p k n", p=128)
                    t_sty = P.op("sync", lambda e, r=r, yt=yt, dstap=dstap: e.dma_start(
                        dstap[:, :, 128 * r:128 * (r + 1)],
                        yt[:, :].rearrange("p (k n) -> p k n", k=KE)),
                        waits=[last_yt, (s_ve, s_ve.n), (s_se, s_se.n)], inc=dsem(yt))
                    SETFREE(yt, t_sty)
                P.emit(nc)


        if phases >= 4:
            with ExitStack() as S:
                e3 = S.enter_context
                woh = e3(nc.sbuf_tensor("woh", [128, KE * 1024], BF16, side="right"))
                ygc = [e3(nc.sbuf_tensor(f"ygc{i}", [128, EIN], BF16, side="right")) for i in range(2)]
                mstg = [e3(nc.sbuf_tensor(f"mstg{i}", [128, 1024], BF16, side="right")) for i in range(2)]

                P = Prog("M3")
                guard(P)
                ygap = d_yg.ap().rearrange("(k p) n -> p k n", p=128)
                for dh in range(2):
                    t_wl = None
                    for ec in range(KE):
                        t_wl = P.op("sync", lambda e, dh=dh, ec=ec: e.dma_start(
                            woh[:, 1024 * ec:1024 * (ec + 1)],
                            d_woutb[1024 * dh:1024 * (dh + 1), 128 * ec:128 * (ec + 1)],
                            transpose=True),
                            waits=[FREE(woh) if ec == 0 else None,
                                   (s_wo, s_wo.n) if ec == 0 else None], inc=dsem(woh))
                    for r in range(NT):
                        yc = ygc[r % 2]
                        t_y = P.op("sync", lambda e, r=r, yc=yc: e.dma_start(
                            yc[:, :].rearrange("p (k n) -> p k n", k=KE),
                            ygap[:, :, 128 * r:128 * (r + 1)]),
                            waits=[FREE(yc)], inc=dsem(yc))
                        pa, pb = PS[(2 * r) % 4], PS[(2 * r + 1) % 4]
                        t_ma = t_mb = None
                        for ec in range(KE):
                            t_ma = P.op("tensor", lambda e, ec=ec, pa=pa, yc=yc: e.matmul(
                                pa[:, 0:512], yc[:, 128 * ec:128 * (ec + 1)],
                                woh[:, 1024 * ec:1024 * ec + 512],
                                start=(ec == 0), stop=(ec == KE - 1)),
                                waits=[t_y, t_wl, FREE(pa)] if ec == 0 else [],
                                inc=s_pe if ec == KE - 1 else None)
                            t_mb = P.op("tensor", lambda e, ec=ec, pb=pb, yc=yc: e.matmul(
                                pb[:, 0:512], yc[:, 128 * ec:128 * (ec + 1)],
                                woh[:, 1024 * ec + 512:1024 * (ec + 1)],
                                start=(ec == 0), stop=(ec == KE - 1)),
                                waits=[FREE(pb)] if ec == 0 else [],
                                inc=s_pe if ec == KE - 1 else None)
                        ms = mstg[r % 2]
                        t_e1 = P.op("vector", lambda e, pa=pa, ms=ms: e.tensor_copy(
                            ms[:, 0:512], pa[:, 0:512]), waits=[t_ma, FREE(ms)], inc=s_ve)
                        t_e2 = P.op("vector", lambda e, pb=pb, ms=ms: e.tensor_copy(
                            ms[:, 512:1024], pb[:, 0:512]), waits=[t_mb], inc=s_ve)
                        SETFREE(pa, t_e1)
                        SETFREE(pb, t_e2)
                        t_st = P.op("sync", lambda e, r=r, dh=dh, ms=ms: e.dma_start(
                            d_mam[128 * r:128 * (r + 1), 1024 * dh:1024 * (dh + 1)], ms[:, :]),
                            waits=[t_e1, t_e2], inc=dsem(ms))
                        SETFREE(ms, t_st)
                        SETFREE(yc, (s_pe, s_pe.n))
                    SETFREE(woh, (s_pe, s_pe.n))
                P.emit(nc)
            M2S.close()


        if phases >= 5:
            with ExitStack() as S:
                e4 = S.enter_context
                zdc = ent(nc.sbuf_tensor("zdc", [128, NE], F32))
                ccol = e4(nc.sbuf_tensor("ccol", [128, NE], F32, side="right"))
                xnf = e4(nc.sbuf_tensor("xnf", [128, KD * DM], BF16, side="right"))
                muf = [e4(nc.sbuf_tensor(f"muf{i}", [128, DM], F32, side="right")) for i in range(2)]
                msq = [e4(nc.sbuf_tensor(f"msq{i}", [128, DM], F32, side="right")) for i in range(2)]
                mut = [e4(nc.sbuf_tensor(f"mut{i}", [128, DM], BF16, side="right")) for i in range(2)]
                exl = [e4(nc.sbuf_tensor(f"exl{i}", [128, DM], BF16, side="right")) for i in range(2)]
                csr = e4(nc.sbuf_tensor("csr", [1, NE * SL], F32, side="right"))
                zd4 = e4(nc.sbuf_tensor("zd4", [128, 4], F32, side="right"))

                P = Prog("B5a")
                guard(P)
                t_nm = {}
                for k in range(KD):
                    mf, mq = muf[k % 2], msq[k % 2]
                    t_l = P.op("sync", lambda e, k=k, mf=mf: e.dma_start(
                        mf[:, :], mu[128 * k:128 * (k + 1), :]), waits=[FREE(mf)], inc=dsem(mf))
                    t_s = P.op("scalar", lambda e, mf=mf, mq=mq: e.activation(
                        mq[:, :], mf[:, :], AF.Square), waits=[t_l, FREE(mq)], inc=s_se)
                    SETFREE(mf, t_s)
                    for ns in range(4):
                        t_nm[ns] = P.op("tensor", lambda e, ns=ns, k=k, mq=mq: e.matmul(
                            PS[4 + ns][0:1, 0:512],
                            ocf[:, :], mq[:, 512 * ns:512 * (ns + 1)],
                            start=(k == 0), stop=(k == KD - 1)),
                            waits=[t_s, FREE(PS[4 + ns]) if k == 0 else None],
                            inc=s_pe if k == KD - 1 else None)
                    SETFREE(mq, (s_pe, s_pe.n))
                t_ev = []
                for ns in range(4):
                    t_ev.append(P.op("vector", lambda e, ns=ns: e.tensor_copy(
                        csr[0:1, 512 * ns:512 * (ns + 1)],
                        PS[4 + ns][0:1, 0:512]),
                        waits=[t_nm[ns], FREE(csr)], inc=s_ve))
                for ns in range(4):
                    SETFREE(PS[4 + ns], t_ev[ns])
                t_sq = P.op("scalar", lambda e: e.activation(
                    csr[0:1, :], csr[0:1, :], AF.Sqrt), waits=t_ev, inc=s_se)
                t_mx = P.op("vector", lambda e: e.tensor_scalar_max(
                    csr[0:1, :], csr[0:1, :], 1e-12), waits=[t_sq], inc=s_ve)
                t_rc = P.op("vector", lambda e: e.reciprocal(
                    csr[0:1, :], csr[0:1, :]), waits=[t_mx], inc=s_ve)
                SETFREE(csr, t_rc)
                for t in range(NE):
                    pq = PS[6 + t % 2]
                    t_t = P.op("tensor", lambda e, t=t, pq=pq: e.transpose(
                        pq[:, 0:1], csr[0:1, 128 * t:128 * (t + 1)], o1f[0:1, 0:1]),
                        waits=[t_rc, FREE(pq)], inc=s_pe)
                    t_c = P.op("vector", lambda e, t=t, pq=pq: e.tensor_copy(
                        ccol[:, t:t + 1], pq[:, 0:1]), waits=[t_t], inc=s_ve)
                    SETFREE(pq, t_c)
                t_ccol = (s_ve, s_ve.n)
                t_xnf = None
                for k in range(KD):
                    t_xnf = P.op("sync", lambda e, k=k: e.dma_start(
                        xnf[:, DM * k:DM * (k + 1)], d_xnorm[128 * k:128 * (k + 1), :]),
                        waits=[FREE(xnf) if k == 0 else None], inc=dsem(xnf))
                muap = mu.ap().rearrange("(k p) s -> p k s", p=128)
                for t in range(NE):
                    mt = mut[t % 2]
                    ex = exl[t % 2]
                    t_mu = P.op("gpsimd", lambda e, t=t, mt=mt: e.dma_start(
                        mt[:, :].rearrange("p (k s) -> p k s", k=KD),
                        muap[:, :, 128 * t:128 * (t + 1)]),
                        waits=[FREE(mt)], inc=dsem(mt, True))
                    t_es = []
                    for ns in range(4):
                        pp = PS[ns % 4]
                        t_mm = None
                        for k in range(KD):
                            t_mm = P.op("tensor", lambda e, ns=ns, k=k, pp=pp, mt=mt: e.matmul(
                                pp[:, 0:512], mt[:, 128 * k:128 * (k + 1)],
                                xnf[:, DM * k + 512 * ns:DM * k + 512 * (ns + 1)],
                                start=(k == 0), stop=(k == KD - 1)),
                                waits=[t_mu, t_xnf, FREE(pp)] if k == 0 else [],
                                inc=s_pe if k == KD - 1 else None)
                        t_e = P.op("scalar", lambda e, ns=ns, t=t, pp=pp, ex=ex: e.activation(
                            ex[:, 512 * ns:512 * (ns + 1)], pp[:, 0:512], AF.Exp,
                            scale=ccol[:, t:t + 1], accum_out=zd4[:, ns:ns + 1]),
                            waits=[t_mm, t_ccol, FREE(ex) if ns == 0 else None,
                                   FREE(zd4) if ns == 0 else None], inc=s_se)
                        SETFREE(pp, t_e)
                        t_es.append(t_e)
                    t_zd = P.op("vector", lambda e, t=t: e.reduce_sum(
                        zdc[:, t:t + 1], zd4[:, 0:4], axis=AX), waits=t_es, inc=s_ve)
                    SETFREE(zd4, t_zd)
                    t_st = P.op("sync", lambda e, t=t, ex=ex: e.dma_start(
                        d_expl[128 * t:128 * (t + 1), :], ex[:, :]), waits=t_es, inc=dsem(ex))
                    SETFREE(ex, t_st)
                    SETFREE(mt, (s_pe, s_pe.n))
                P.emit(nc)

        if phases >= 6:
            with ExitStack() as S:
                e5 = S.enter_context
                dtm = e5(nc.sbuf_tensor("dtm", [128, NT * DM], BF16, side="right"))
                exb = [e5(nc.sbuf_tensor(f"exb{i}", [128, DM], BF16, side="right")) for i in range(2)]
                zrep = e5(nc.sbuf_tensor("zrep", [128, 128], F32, side="right"))
                zrow = e5(nc.sbuf_tensor("zrow", [1, 128], F32, side="right"))
                zcolr = e5(nc.sbuf_tensor("zcolr", [128, 1], F32, side="right"))
                xlt = [e5(nc.sbuf_tensor(f"xlt{i}", [128, NT * 128], BF16, side="right")) for i in range(2)]
                sst = [e5(nc.sbuf_tensor(f"sst{i}", [128, 512], BF16, side="right")) for i in range(2)]

                P = Prog("B5b6")
                guard(P)
                for t in range(NE):
                    ex = exb[t % 2]
                    t_le = P.op("sync", lambda e, t=t, ex=ex: e.dma_start(
                        ex[:, :], d_expl[128 * t:128 * (t + 1), :]), waits=[FREE(ex)], inc=dsem(ex))
                    t_rc = P.op("vector", lambda e, t=t: e.reciprocal(
                        zcolr[:, 0:1], zdc[:, t:t + 1]), waits=[FREE(zcolr)], inc=s_ve)
                    t_tp = P.op("tensor", lambda e: e.transpose(
                        PS[6][0:1, 0:128], zcolr[:, 0:1], idf[:, :]),
                        waits=[t_rc, FREE(PS[6])], inc=s_pe)
                    t_zr = P.op("vector", lambda e: e.tensor_copy(
                        zrow[0:1, :], PS[6][0:1, 0:128]), waits=[t_tp, FREE(zrow)], inc=s_ve)
                    SETFREE(PS[6], t_zr)
                    SETFREE(zcolr, t_zr)
                    t_zb = P.op("tensor", lambda e: e.matmul(
                        PS[7][:, 0:128], o1f[0:1, :], zrow[0:1, :], start=True, stop=True),
                        waits=[t_zr, FREE(PS[7])], inc=s_pe)
                    t_ze = P.op("vector", lambda e: e.tensor_copy(
                        zrep[:, :], PS[7][:, 0:128]), waits=[t_zb, FREE(zrep)], inc=s_ve)
                    SETFREE(PS[7], t_ze)
                    SETFREE(zrow, t_zb)
                    last = None
                    for r in range(NT):
                        pq = PS[r % 2]
                        t_t = P.op("tensor", lambda e, r=r, pq=pq, ex=ex: e.transpose(
                            pq[:, 0:64].bitcast(BF16), ex[:, 128 * r:128 * (r + 1)], idb[:, :]),
                            waits=[t_le, FREE(pq)], inc=s_pe)
                        t_d = P.op("vector", lambda e, r=r, t=t, pq=pq: e.scalar_tensor_tensor(
                            dtm[:, DM * r + 128 * t:DM * r + 128 * (t + 1)],
                            pq[:, 0:64].bitcast(BF16), epsb[:, 2:3], zrep[:, :],
                            ALU.add, ALU.mult), waits=[t_t, t_ze], inc=s_ve)
                        SETFREE(pq, t_d)
                        last = t_d
                    SETFREE(ex, (s_pe, s_pe.n))
                    SETFREE(zrep, last)
                t_dtm = (s_ve, s_ve.n)
                xlap = d_xln.ap().rearrange("(r p) k -> p r k", p=128)
                for k in range(KD):
                    xt = xlt[k % 2]
                    t_lx = P.op("sync", lambda e, k=k, xt=xt: e.dma_start(
                        xt[:, :].rearrange("p (r c) -> p r c", r=NT),
                        xlap[:, :, 128 * k:128 * (k + 1)]),
                        waits=[FREE(xt)], inc=dsem(xt))
                    for ns in range(4):
                        pp = PS[2 + ns % 2]
                        t_mm = None
                        for r in range(NT):
                            t_mm = P.op("tensor", lambda e, ns=ns, r=r, pp=pp, xt=xt: e.matmul(
                                pp[:, 0:512], xt[:, 128 * r:128 * (r + 1)],
                                dtm[:, DM * r + 512 * ns:DM * r + 512 * (ns + 1)],
                                start=(r == 0), stop=(r == NT - 1)),
                                waits=[t_lx, t_dtm, FREE(pp)] if r == 0 else [],
                                inc=s_pe if r == NT - 1 else None)
                        ss = sst[ns % 2]
                        t_e = P.op("vector", lambda e, pp=pp, ss=ss: e.tensor_copy(
                            ss[:, :], pp[:, 0:512]), waits=[t_mm, FREE(ss)], inc=s_ve)
                        SETFREE(pp, t_e)
                        t_s = P.op("sync", lambda e, k=k, ns=ns, ss=ss: e.dma_start(
                            d_slotin[128 * k:128 * (k + 1), 512 * ns:512 * (ns + 1)], ss[:, :]),
                            waits=[t_e], inc=dsem(ss))
                        SETFREE(ss, t_s)
                    SETFREE(xt, (s_pe, s_pe.n))
                P.emit(nc)

        if phases >= 7:
            with ExitStack() as S:
                e6 = S.enter_context
                sos = e6(nc.sbuf_tensor("sos", [128, NE * DM], BF16, side="right"))
                w1t = [e6(nc.sbuf_tensor(f"w1t{i}", [128, KD * HH], BF16, side="right")) for i in range(2)]
                w2t = [e6(nc.sbuf_tensor("w2t0", [128, 4 * DM], BF16, side="right"))] * 2
                sit = [e6(nc.sbuf_tensor(f"sit{i}", [128, KD * 128], BF16, side="right")) for i in range(2)]
                b1r = [e6(nc.sbuf_tensor(f"b1r{i}", [1, HH], BF16, side="right")) for i in range(2)]
                b2r = [e6(nc.sbuf_tensor(f"b2r{i}", [1, DM], BF16, side="right")) for i in range(2)]
                hsm = e6(nc.sbuf_tensor("hsm", [128, HH], BF16, side="right"))
                hT = e6(nc.sbuf_tensor("hT", [128, HH], BF16, side="right"))

                P = Prog("B7")
                guard(P)
                w1ap = w1.ap().rearrange("e (k p) h -> e p k h", p=128)
                w2ap = w2.ap().rearrange("e (j p) d -> e p j d", p=128)
                siap = d_slotin.ap().rearrange("(k p) s -> p k s", p=128)
                for ei in range(NE):
                    wa, wb = w1t[ei % 2], w2t[ei % 2]
                    si = sit[ei % 2]
                    ba, bb = b1r[ei % 2], b2r[ei % 2]
                    t_w1 = P.op("gpsimd", lambda e, ei=ei, wa=wa: e.dma_start(
                        wa[:, :].rearrange("p (k h) -> p k h", k=KD), w1ap[ei]),
                        waits=[FREE(wa)], inc=dsem(wa, True))
                    t_w2 = P.op("gpsimd", lambda e, ei=ei, wb=wb: e.dma_start(
                        wb[:, :].rearrange("p (j d) -> p j d", j=4), w2ap[ei]),
                        waits=[FREE(wb)], inc=dsem(wb, True))
                    t_b1 = P.op("gpsimd", lambda e, ei=ei, ba=ba: e.dma_start(
                        ba[0:1, :], b1[ei:ei + 1, :]), waits=[FREE(ba)], inc=dsem(ba, True))
                    t_b2 = P.op("gpsimd", lambda e, ei=ei, bb=bb: e.dma_start(
                        bb[0:1, :], b2[ei:ei + 1, :]), waits=[FREE(bb)], inc=dsem(bb, True))
                    t_si = P.op("sync", lambda e, ei=ei, si=si: e.dma_start(
                        si[:, :].rearrange("p (k s) -> p k s", k=KD),
                        siap[:, :, 128 * ei:128 * (ei + 1)]),
                        waits=[FREE(si)], inc=dsem(si))
                    t_mm = None
                    for k in range(KD):
                        t_mm = P.op("tensor", lambda e, k=k, si=si, wa=wa: e.matmul(
                            PS[0][:, 0:512], si[:, 128 * k:128 * (k + 1)],
                            wa[:, HH * k:HH * (k + 1)], start=(k == 0), stop=False),
                            waits=[t_si, t_w1, FREE(PS[0])] if k == 0 else [], inc=None)
                    t_mm = P.op("tensor", lambda e, ba=ba: e.matmul(
                        PS[0][:, 0:512], o1b[0:1, :], ba[0:1, :], start=False, stop=True),
                        waits=[t_b1], inc=s_pe)
                    t_h = P.op("scalar", lambda e: e.activation(
                        hsm[:, :], PS[0][:, 0:512], AF.Gelu), waits=[t_mm, FREE(hsm)], inc=s_se)
                    SETFREE(PS[0], t_h)
                    t_hts = []
                    for j in range(4):
                        pq = PS[1 + j % 2]
                        t_t = P.op("tensor", lambda e, j=j, pq=pq: e.transpose(
                            pq[:, 0:64].bitcast(BF16), hsm[:, 128 * j:128 * (j + 1)], idb[:, :]),
                            waits=[t_h, FREE(pq)], inc=s_pe)
                        t_e = P.op("vector", lambda e, j=j, pq=pq: e.tensor_copy(
                            hT[:, 128 * j:128 * (j + 1)], pq[:, 0:64].bitcast(BF16)),
                            waits=[t_t, FREE(hT) if j == 0 else None], inc=s_ve)
                        SETFREE(pq, t_e)
                        t_hts.append(t_e)
                    SETFREE(hsm, s_pe.tok())
                    for dns in range(4):
                        pp = PS[3 + dns % 2]
                        t_mm = None
                        for j in range(4):
                            t_mm = P.op("tensor", lambda e, j=j, dns=dns, pp=pp, wb=wb: e.matmul(
                                pp[:, 0:512], hT[:, 128 * j:128 * (j + 1)],
                                wb[:, DM * j + 512 * dns:DM * j + 512 * (dns + 1)],
                                start=(j == 0), stop=False),
                                waits=[t_hts[j], t_w2, FREE(pp)] if j == 0 else [], inc=None)
                        t_mm = P.op("tensor", lambda e, dns=dns, pp=pp, bb=bb: e.matmul(
                            pp[:, 0:512], o1b[0:1, :], bb[0:1, 512 * dns:512 * (dns + 1)],
                            start=False, stop=True), waits=[t_b2], inc=s_pe)
                        t_e = P.op("vector", lambda e, ei=ei, dns=dns, pp=pp: e.tensor_copy(
                            sos[:, DM * ei + 512 * dns:DM * ei + 512 * (dns + 1)], pp[:, 0:512]),
                            waits=[t_mm], inc=s_ve)
                        SETFREE(pp, t_e)
                    SETFREE(hT, s_pe.tok())
                    SETFREE(si, s_pe.tok())
                    SETFREE(wa, s_pe.tok())
                    SETFREE(wb, s_pe.tok())
                    SETFREE(ba, s_pe.tok())
                    SETFREE(bb, s_pe.tok())
                t_sos = (s_ve, s_ve.n)

                exb2 = [e6(nc.sbuf_tensor(f"exb2{i}", [128, DM], BF16, side="right")) for i in range(2)]
                xfin = [e6(nc.sbuf_tensor(f"xfin{i}", [128, DM], F32, side="right")) for i in range(2)]
                xnb = [e6(nc.sbuf_tensor(f"xnb{i}", [128, DM], BF16, side="right")) for i in range(2)]
                mamb = [e6(nc.sbuf_tensor(f"mamb{i}", [128, DM], BF16, side="right")) for i in range(2)]
                obuf = [e6(nc.sbuf_tensor(f"obuf{i}", [128, DM], F32, side="right")) for i in range(2)]
                zcr = e6(nc.sbuf_tensor("zcr", [128, 1], F32, side="right"))

                exap = d_expl.ap().rearrange("(t p) n -> p t n", p=128)
                for r in range(NT):
                    ex = exb2[r % 2]
                    xf, xnl, mml, ob = xfin[r % 2], xnb[r % 2], mamb[r % 2], obuf[r % 2]
                    t_le = P.op("sync", lambda e, r=r, ex=ex: e.dma_start(
                        ex[:, :].rearrange("p (t n) -> p t n", t=NE),
                        exap[:, :, 128 * r:128 * (r + 1)]), waits=[FREE(ex)], inc=dsem(ex))
                    t_lx = P.op("sync", lambda e, r=r, xf=xf: e.dma_start(
                        xf[:, :], x_in[128 * r:128 * (r + 1), :]), waits=[FREE(xf)], inc=dsem(xf))
                    t_ln = P.op("sync", lambda e, r=r, xnl=xnl: e.dma_start(
                        xnl[:, :], d_xn[128 * r:128 * (r + 1), :]), waits=[FREE(xnl)], inc=dsem(xnl))
                    t_lm = P.op("sync", lambda e, r=r, mml=mml: e.dma_start(
                        mml[:, :], d_mam[128 * r:128 * (r + 1), :]), waits=[FREE(mml)], inc=dsem(mml))
                    t_md = {}
                    t_zc = None
                    for t in range(NE):
                        for dns in range(4):
                            t_md[dns] = P.op("tensor", lambda e, t=t, dns=dns, ex=ex: e.matmul(
                                PS[dns][:, 0:512], ex[:, 128 * t:128 * (t + 1)],
                                sos[:, DM * t + 512 * dns:DM * t + 512 * (dns + 1)],
                                start=(t == 0), stop=(t == NE - 1)),
                                waits=[t_le, t_sos, FREE(PS[dns])] if t == 0 else [],
                                inc=s_pe if t == NE - 1 else None)
                        t_zc = P.op("tensor", lambda e, t=t, ex=ex: e.matmul(
                            PS[4][:, 0:1], ex[:, 128 * t:128 * (t + 1)], ocb[:, 0:1],
                            start=(t == 0), stop=(t == NE - 1)),
                            waits=[FREE(PS[4])] if t == 0 else [],
                            inc=s_pe if t == NE - 1 else None)
                    t_rc = P.op("vector", lambda e: e.reciprocal(
                        zcr[:, 0:1], PS[4][:, 0:1]), waits=[t_zc, FREE(zcr)], inc=s_ve)
                    SETFREE(PS[4], t_rc)
                    t_a1 = P.op("gpsimd", lambda e, xf=xf, xnl=xnl: e.tensor_tensor(
                        xf[:, :], xf[:, :], xnl[:, :], ALU.add), waits=[t_lx, t_ln], inc=s_po)
                    t_a2 = P.op("gpsimd", lambda e, xf=xf, mml=mml: e.tensor_tensor(
                        xf[:, :], xf[:, :], mml[:, :], ALU.add), waits=[t_a1, t_lm], inc=s_po)
                    SETFREE(xnl, t_a2)
                    SETFREE(mml, t_a2)
                    t_os = []
                    for dns in range(4):
                        t_o = P.op("vector", lambda e, dns=dns, xf=xf, ob=ob: e.scalar_tensor_tensor(
                            ob[:, 512 * dns:512 * (dns + 1)], PS[dns][:, 0:512],
                            zcr[:, 0:1], xf[:, 512 * dns:512 * (dns + 1)],
                            ALU.mult, ALU.add),
                            waits=[t_md[dns], t_rc, t_a2, FREE(ob) if dns == 0 else None], inc=s_ve)
                        SETFREE(PS[dns], t_o)
                        t_os.append(t_o)
                    SETFREE(zcr, t_os[-1])
                    SETFREE(xf, t_os[-1])
                    SETFREE(ex, (s_pe, s_pe.n))
                    t_st = P.op("sync", lambda e, r=r, ob=ob: e.dma_start(
                        out[128 * r:128 * (r + 1), :], ob[:, :]), waits=t_os, inc=dsem(ob))
                    SETFREE(ob, t_st)
                P.emit(nc)

        if d_m1 is not None and 2 <= phases < 4:
            P = Prog("dbg_m1")
            sdbg2 = Sem(nc, "sdbg2", 16)
            P.op("sync", lambda e: e.dma_start(d_m1[:, 0:1024], dA[:, :]), inc=sdbg2)
            P.op("sync", lambda e: e.dma_start(d_m1[:, 1024:2048], eacs[:, :]), inc=sdbg2)
            P.op("sync", lambda e: e.dma_start(d_m1[:, 2048:3072], dec[:, :]), inc=sdbg2)
            P.op("sync", lambda e: e.dma_start(d_m1[:, 3072:5120], cdrep[:, :]), inc=sdbg2)
            sdbg3 = Sem(nc, "sdbg3", 16)
            P.op("gpsimd", lambda e: e.dma_start(d_bc[0:128, :], b_fm[:, :]), inc=sdbg3)
            P.op("gpsimd", lambda e: e.dma_start(d_bc[128:256, :], c_fm[:, :]), inc=sdbg3)
            P.op("sync", lambda e: e.nop(), waits=[sdbg2.tok(), sdbg3.tok()])
            P.emit(nc)

        if d_un is not None and 1 <= phases < 3:
            P = Prog("dbg_un")
            sdbg = Sem(nc, "sdbg", 16)
            for k in range(KD):
                P.op("sync", lambda e, k=k: e.dma_start(
                    d_un[128 * k:128 * (k + 1), :], un_fm[k][:, :]), inc=sdbg)
            P.op("sync", lambda e: e.nop(), waits=[sdbg.tok()])
            P.emit(nc)

        P = Prog("drain")
        for sm in list(_dma_sems.values()) + [s_ld, s_wc]:
            if sm.n > 0:
                P.op("sync", lambda e, sm=sm: e.nop(), waits=[sm.tok()])
        P.emit(nc)

        _ = (d_x, d_sz, d_yg, d_mam, d_slotin, d_expl, d_bc, mu, w1, b1, w2, b2,
             b_fm, c_fm, dA, eacs, dec, stm, u2, slow, tri2, half, ocb, o1b, idb,
             bnw_c, cnw_c, a_rep, dtb_rep, dd_rep, t_wc_in, t_wc_out, out, d_winb, d_woutb)

    return nc


def host_consts():
    idn = np.eye(128)
    u2 = np.zeros((128, 128), np.float32)
    u2[0:64, 0:64] = np.triu(np.ones((64, 64)))
    u2[64:128, 64:128] = u2[0:64, 0:64]
    sl_ = np.zeros((128, 128), np.float32)
    sl_[0:64, 0:64] = np.tril(np.ones((64, 64)), -1)
    sl_[64:128, 64:128] = sl_[0:64, 0:64]
    tri2 = np.zeros((128, 64), np.float32)
    tri2[0:64, :] = np.triu(np.ones((64, 64)))
    tri2[64:128, :] = tri2[0:64, :]
    half = np.zeros((128, 2), np.float32)
    half[0:64, 0] = 1.0
    half[64:128, 1] = 1.0
    return {
        "c_idf": idn.astype(np.float32),
        "c_idb": idn.astype(BF),
        "c_u2": u2,
        "c_slow": sl_,
        "c_tri2": tri2,
        "c_half": half,
        "c_o1f": np.ones((1, 128), np.float32),
        "c_o1b": np.ones((1, 128), BF),
        "c_ocb": np.ones((128, 1), BF),
        "c_ocf": np.ones((128, 1), np.float32),
        "c_eps": np.tile(np.pad(np.array([CONST_VALS], np.float32), ((0,0),(0,16-len(CONST_VALS)))), (128, 1)),
    }


def make_in_maps(inputs):
    cst = host_consts()
    f32c = lambda a: np.ascontiguousarray(np.asarray(a, np.float32))
    shared = dict(
        w_in=f32c(inputs["L0_attn_in_proj_w"]),
        w_out=f32c(inputs["L0_attn_out_proj_w"]),
        p_D=f32c(inputs["L0_attn_D"]).reshape(1, -1),
        p_dtb=f32c(inputs["L0_attn_dt_bias"]).reshape(1, -1),
        p_alog=f32c(inputs["L0_attn_A_log"]).reshape(1, -1),
        p_bnw=f32c(inputs["L0_attn_bnorm_w"]).reshape(1, -1),
        p_cnw=f32c(inputs["L0_attn_cnorm_w"]).reshape(1, -1),
        p_lnw=f32c(inputs["L0_attn_ln_w"]).reshape(1, -1),
        p_lnb=f32c(inputs["L0_attn_ln_b"]).reshape(1, -1),
        p_gam=f32c(inputs["L0_ffn_gamma"]).reshape(1, -1),
        p_bet=f32c(inputs["L0_ffn_beta"]).reshape(1, -1),
        p_scale=f32c(inputs["L0_ffn_scale"]).reshape(1, 1),
        mu=f32c(inputs["L0_ffn_mu"]).reshape(DM, NE * SL),
        w1=f32c(inputs["L0_ffn_W1"]),
        b1=f32c(inputs["L0_ffn_b1"]),
        w2=f32c(inputs["L0_ffn_W2"]),
        b2=f32c(inputs["L0_ffn_b2"]),
        **cst,
    )
    x = np.asarray(inputs["L0_attn_u"], np.float32)
    return [dict(shared, x_in=np.ascontiguousarray(x[c])) for c in range(NCORES)]


def kernel(**inputs):
    nc = build()
    in_maps = make_in_maps(inputs)
    res = run_bass_kernel_spmd(nc, in_maps, core_ids=list(range(NCORES)))
    return np.stack([np.asarray(r["out"], np.float32) for r in res.results], axis=0)


# version 04
# speedup vs baseline: 1.1697x; 1.1697x over previous
import numpy as np
import ml_dtypes
import concourse.bass as bass
import concourse.mybir as mybir
from concourse.bass_utils import run_bass_kernel_spmd
from contextlib import ExitStack

F32 = mybir.dt.float32
BF16 = mybir.dt.bfloat16
AF = mybir.ActivationFunctionType
ALU = mybir.AluOpType
AX = mybir.AxisListType.X
BF = ml_dtypes.bfloat16

NCORES = 8
NTOK = 2048
DM = 2048
EIN = 4096
NH = 64
HD = 64
DS = 128
CH = 64
NT = NTOK // 128
KD = DM // 128
KE = EIN // 128
EPROJ = 8512
NE = 16
SL = 128
HH = 512

ENGINES = ("tensor", "vector", "scalar", "gpsimd", "sync")
CONST_VALS = (1e-5, 1e-12, 0.0, 1.0, -1.0, 0.5, -0.5, 2.0, 0.0078125)


class Sem:
    __slots__ = ("h", "n", "step", "_cm")

    def __init__(self, nc, name, step=1):
        self._cm = nc.semaphore(name)
        self.h = self._cm.__enter__()
        self.n = 0
        self.step = step

    def inc(self, k=None):
        self.n += self.step if k is None else k
        return self.n

    def tok(self):
        return (self, self.n)


class Prog:

    def __init__(self, name=""):
        self.name = name
        self.items = []

    def op(self, eng, fn, waits=(), inc=None, inck=None):
        ws = []

        def add(w):
            if w is None:
                return
            if isinstance(w, list):
                for x in w:
                    add(x)
                return
            sem, tk = w
            if tk > 0:
                ws.append((sem, tk))

        for w in waits:
            add(w)
        incs = []
        tok = None
        if inc is not None:
            inc.inc(inck)
            tok = (inc, inc.n)
            incs = [(inc, inc.step if inck is None else inck)]
        self.items.append((eng, fn, tuple(ws), incs))
        return tok

    def emit(self, nc):
        by_eng = {}
        for it in self.items:
            by_eng.setdefault(it[0], []).append(it)
        with nc.Block() as b:
            for eng_name in ENGINES:
                items = by_eng.get(eng_name)
                if not items:
                    continue

                def body(e, items=items):
                    waited = {}
                    for engn, fn, waits, incs in items:
                        for (sem, tk) in waits:
                            if waited.get(id(sem), 0) < tk:
                                e.wait_ge(sem.h, tk)
                                waited[id(sem)] = tk
                        instr = fn(e)
                        for (sem, k) in incs:
                            instr.then_inc(sem.h, k)

                getattr(b, eng_name)(body)


def build(dbg=(), phases=99, m1_parts=("zx", "bc", "dt"), dt_stop=9):
    nc = bass.Bass("TRN2", target_bir_lowering=False)
    dkind = lambda n: "ExternalOutput" if n in dbg else "Internal"

    x_in = nc.dram_tensor("x_in", [NTOK, DM], F32, kind="ExternalInput")
    w_in = nc.dram_tensor("w_in", [EPROJ, DM], F32, kind="ExternalInput")
    w_out = nc.dram_tensor("w_out", [DM, EIN], F32, kind="ExternalInput")
    p_D = nc.dram_tensor("p_D", [1, NH], F32, kind="ExternalInput")
    p_dtb = nc.dram_tensor("p_dtb", [1, NH], F32, kind="ExternalInput")
    p_alog = nc.dram_tensor("p_alog", [1, NH], F32, kind="ExternalInput")
    p_bnw = nc.dram_tensor("p_bnw", [1, DS], F32, kind="ExternalInput")
    p_cnw = nc.dram_tensor("p_cnw", [1, DS], F32, kind="ExternalInput")
    p_lnw = nc.dram_tensor("p_lnw", [1, DM], F32, kind="ExternalInput")
    p_lnb = nc.dram_tensor("p_lnb", [1, DM], F32, kind="ExternalInput")
    p_gam = nc.dram_tensor("p_gam", [1, DM], F32, kind="ExternalInput")
    p_bet = nc.dram_tensor("p_bet", [1, DM], F32, kind="ExternalInput")
    p_scale = nc.dram_tensor("p_scale", [1, 1], F32, kind="ExternalInput")
    mu = nc.dram_tensor("mu", [DM, NE * SL], F32, kind="ExternalInput")
    w1 = nc.dram_tensor("w1", [NE, DM, HH], F32, kind="ExternalInput")
    b1 = nc.dram_tensor("b1", [NE, HH], F32, kind="ExternalInput")
    w2 = nc.dram_tensor("w2", [NE, HH, DM], F32, kind="ExternalInput")
    b2 = nc.dram_tensor("b2", [NE, DM], F32, kind="ExternalInput")

    c_idf = nc.dram_tensor("c_idf", [128, 128], F32, kind="ExternalInput")
    c_idb = nc.dram_tensor("c_idb", [128, 128], BF16, kind="ExternalInput")
    c_u2 = nc.dram_tensor("c_u2", [128, 128], F32, kind="ExternalInput")
    c_slow = nc.dram_tensor("c_slow", [128, 128], F32, kind="ExternalInput")
    c_tri2 = nc.dram_tensor("c_tri2", [128, CH], F32, kind="ExternalInput")
    c_half = nc.dram_tensor("c_half", [128, 2], F32, kind="ExternalInput")
    c_o1f = nc.dram_tensor("c_o1f", [1, 128], F32, kind="ExternalInput")
    c_o1b = nc.dram_tensor("c_o1b", [1, 128], BF16, kind="ExternalInput")
    c_ocb = nc.dram_tensor("c_ocb", [128, 1], BF16, kind="ExternalInput")
    c_ocf = nc.dram_tensor("c_ocf", [128, 1], F32, kind="ExternalInput")
    c_eps = nc.dram_tensor("c_eps", [128, 16], F32, kind="ExternalInput")

    out = nc.dram_tensor("out", [NTOK, DM], F32, kind="ExternalOutput")

    d_xn = nc.dram_tensor("d_xn", [NTOK, DM], BF16, kind=dkind("d_xn"))
    d_xln = nc.dram_tensor("d_xln", [NTOK, DM], BF16, kind=dkind("d_xln"))
    d_xnorm = nc.dram_tensor("d_xnorm", [DM, NTOK], BF16, kind=dkind("d_xnorm"))
    d_winb = nc.dram_tensor("d_winb", [EPROJ, DM], BF16, kind=dkind("d_winb"))
    d_woutb = nc.dram_tensor("d_woutb", [DM, EIN], BF16, kind=dkind("d_woutb"))
    d_x = nc.dram_tensor("d_x", [NTOK, EIN], BF16, kind=dkind("d_x"))
    d_sz = nc.dram_tensor("d_sz", [NTOK, EIN], BF16, kind=dkind("d_sz"))
    d_yg = nc.dram_tensor("d_yg", [EIN, NTOK], BF16, kind=dkind("d_yg"))
    d_mam = nc.dram_tensor("d_mam", [NTOK, DM], BF16, kind=dkind("d_mam"))
    d_slotin = nc.dram_tensor("d_slotin", [DM, NE * SL], BF16, kind=dkind("d_slotin"))
    d_expl = nc.dram_tensor("d_expl", [NE * SL, NTOK], BF16, kind=dkind("d_expl"))
    d_bc = nc.dram_tensor("d_bc", [2 * DS, NTOK], BF16, kind="ExternalOutput" if ("d_bc" in dbg or "d_m1" in dbg) else "Internal")
    d_un = nc.dram_tensor("d_un", [DM, NTOK], BF16, kind=dkind("d_un")) if "d_un" in dbg else None
    d_m1 = nc.dram_tensor("d_m1", [128, 5 * 1024 + 2048], F32, kind="ExternalOutput") if "d_m1" in dbg else None
    d_m2 = nc.dram_tensor("d_m2", [128, 3 * EIN + CH * NT], F32, kind="ExternalOutput") if "d_m2" in dbg else None

    with ExitStack() as TOP:
        ent = TOP.enter_context

        s_ld = Sem(nc, "s_ld", 16)
        s_wc = Sem(nc, "s_wc", 16)
        s_wo = Sem(nc, "s_wo", 16)
        s_pe = Sem(nc, "s_pe")
        s_ve = Sem(nc, "s_ve")
        s_se = Sem(nc, "s_se")
        s_po = Sem(nc, "s_po")
        _dma_sems = {}

        def dsem(buf, sw=False):
            k = (buf if isinstance(buf, str) else id(buf), sw)
            if k not in _dma_sems:
                _dma_sems[k] = Sem(nc, f"sd{len(_dma_sems)}", 16)
            return _dma_sems[k]

        idf = ent(nc.sbuf_tensor("idf", [128, 128], F32))
        idb = ent(nc.sbuf_tensor("idb", [128, 128], BF16))
        u2 = ent(nc.sbuf_tensor("u2", [128, 128], F32))
        slow = ent(nc.sbuf_tensor("slow", [128, 128], F32))
        tri2 = ent(nc.sbuf_tensor("tri2", [128, CH], F32))
        half = ent(nc.sbuf_tensor("half", [128, 2], F32))
        o1f = ent(nc.sbuf_tensor("o1f", [1, 128], F32))
        o1b = ent(nc.sbuf_tensor("o1b", [1, 128], BF16))
        ocb = ent(nc.sbuf_tensor("ocb", [128, 1], BF16))
        ocf = ent(nc.sbuf_tensor("ocf", [128, 1], F32))
        epsb = ent(nc.sbuf_tensor("epsb", [128, 16], F32))
        lnw_c = ent(nc.sbuf_tensor("lnw_c", [128, KD], F32))
        lnb_c = ent(nc.sbuf_tensor("lnb_c", [128, KD], F32))
        gam_c = ent(nc.sbuf_tensor("gam_c", [128, KD], F32))
        bet_c = ent(nc.sbuf_tensor("bet_c", [128, KD], F32))
        bnw_c = ent(nc.sbuf_tensor("bnw_c", [128, 1], F32))
        cnw_c = ent(nc.sbuf_tensor("cnw_c", [128, 1], F32))
        sc_rep = ent(nc.sbuf_tensor("sc_rep", [128, 1], F32))
        a_rep = ent(nc.sbuf_tensor("a_rep", [128, NH], F32))
        dtb_rep = ent(nc.sbuf_tensor("dtb_rep", [128, NH], F32))
        dd_rep = ent(nc.sbuf_tensor("dd_rep", [128, NH], F32))
        M2S = ExitStack()
        b_fm = M2S.enter_context(nc.sbuf_tensor("b_fm", [128, NTOK], BF16))
        c_fm = M2S.enter_context(nc.sbuf_tensor("c_fm", [128, NTOK], BF16))
        dA = M2S.enter_context(nc.sbuf_tensor("dA", [128, NH * NT], F32))
        eacs = M2S.enter_context(nc.sbuf_tensor("eacs", [128, NH * NT], F32))
        dec = M2S.enter_context(nc.sbuf_tensor("dec", [128, NH * NT], F32))
        stm = M2S.enter_context(nc.sbuf_tensor("stm", [128, CH * NT], BF16))
        cdrep = M2S.enter_context(nc.sbuf_tensor("cdrep", [128, 2 * NH * NT], F32))

        PS = [ent(nc.psum_tensor(f"ps{i}", [128, 512], F32)) for i in range(8)]

        UNS = ExitStack()
        un_fm = [UNS.enter_context(nc.sbuf_tensor(f"un{k}", [128, NTOK], BF16)) for k in range(KD)]

        free = {}

        def FREE(buf):
            v = free.get(id(buf), [])
            return v if isinstance(v, list) else [v]

        def SETFREE(buf, *toks):
            free[id(buf)] = [t for t in toks if t is not None]

        def guard(P):
            toks = [sm.tok() for sm in _dma_sems.values() if sm.n > 0]
            for eng in ENGINES:
                P.op(eng, lambda e: e.nop(), waits=toks)

        def rep_fill(P, row_dram, width, dst, buf, ps_base, dep0, negate=False, do_exp=False):
            t_ld = P.op("sync", lambda e, b_=buf, rd=row_dram, w=width: e.dma_start(
                b_[0:1, 0:w], rd[0:1, 0:w]), waits=[FREE(buf), dep0], inc=dsem(buf))
            dep = t_ld
            if do_exp:
                dep = P.op("scalar", lambda e, b_=buf, w=width: e.activation(
                    b_[0:1, 0:w], b_[0:1, 0:w], AF.Exp), waits=[t_ld], inc=s_se)
            last = None
            for j in range((width + 511) // 512):
                w = min(512, width - j * 512)
                pk = PS[ps_base + (j % 2)]
                tt = P.op("tensor", lambda e, j=j, w=w, pk=pk, b_=buf: e.matmul(
                    pk[:, 0:w], o1f[0:1, :], b_[0:1, 512 * j:512 * j + w],
                    start=True, stop=True), waits=[dep, FREE(pk)], inc=s_pe)
                if negate:
                    tv = P.op("vector", lambda e, j=j, w=w, pk=pk: e.tensor_scalar_mul(
                        dst[:, 512 * j:512 * j + w], pk[:, 0:w], -1.0), waits=[tt], inc=s_ve)
                else:
                    tv = P.op("vector", lambda e, j=j, w=w, pk=pk: e.tensor_copy(
                        dst[:, 512 * j:512 * j + w], pk[:, 0:w]), waits=[tt], inc=s_ve)
                SETFREE(pk, tv)
                last = tv
            SETFREE(buf, last)
            return last

        with ExitStack() as S:
            rowbuf = S.enter_context(nc.sbuf_tensor("rowbuf", [1, DM], F32, side="right"))
            rowbuf2 = S.enter_context(nc.sbuf_tensor("rowbuf2", [1, DM], F32, side="right"))
            P = Prog("B0")
            for dst, src in [(idf, c_idf), (idb, c_idb), (u2, c_u2), (slow, c_slow),
                             (tri2, c_tri2), (half, c_half), (o1f, c_o1f),
                             (o1b, c_o1b), (ocb, c_ocb), (ocf, c_ocf), (epsb, c_eps)]:
                P.op("sync", lambda e, d=dst, s=src: e.dma_start(d[:, :], s[:, :]), inc=s_ld)
            t_consts = s_ld.tok()
            for ci, cv in enumerate(CONST_VALS):
                nc.const_aps.register(F32, cv, epsb[:, ci:ci + 1])

            for i in range(8):
                r0, r1 = i * 1064, (i + 1) * 1064
                P.op("gpsimd", lambda e, r0=r0, r1=r1: e.dma_start(
                    d_winb[r0:r1, :], w_in[r0:r1, :]), inc=s_wc)
            t_wc_in = s_wc.tok()
            for i in range(4):
                r0, r1 = i * 512, (i + 1) * 512
                P.op("gpsimd", lambda e, r0=r0, r1=r1: e.dma_start(
                    d_woutb[r0:r1, :], w_out[r0:r1, :]), inc=s_wo)
            t_wc_out = s_wo.tok()

            def col_fill(row_dram, ncols, dst_col, buf, ps_base):
                t_ld = P.op("sync", lambda e, b_=buf, rd=row_dram, n=ncols: e.dma_start(
                    b_[0:1, 0:n * 128], rd[0:1, 0:n * 128]),
                    waits=[FREE(buf), t_consts], inc=dsem(buf))
                last = None
                for k in range(ncols):
                    pk = PS[ps_base + (k % 2)]
                    tt = P.op("tensor", lambda e, k=k, pk=pk, b_=buf: e.transpose(
                        pk[:, 0:1], b_[0:1, 128 * k:128 * (k + 1)], o1f[0:1, 0:1]),
                        waits=[t_ld, FREE(pk)], inc=s_pe)
                    tv = P.op("vector", lambda e, k=k, pk=pk: e.tensor_copy(
                        dst_col[:, k:k + 1], pk[:, 0:1]), waits=[tt], inc=s_ve)
                    SETFREE(pk, tv)
                    last = tv
                SETFREE(buf, last)

            col_fill(p_lnw, KD, lnw_c, rowbuf, 0)
            col_fill(p_lnb, KD, lnb_c, rowbuf2, 2)
            col_fill(p_gam, KD, gam_c, rowbuf, 0)
            col_fill(p_bet, KD, bet_c, rowbuf2, 2)
            col_fill(p_bnw, 1, bnw_c, rowbuf, 0)
            col_fill(p_cnw, 1, cnw_c, rowbuf2, 2)

            rep_fill(P, p_scale, 1, sc_rep, rowbuf, 4, t_consts)
            rep_fill(P, p_dtb, NH, dtb_rep, rowbuf2, 4, t_consts)
            rep_fill(P, p_D, NH, dd_rep, rowbuf, 4, t_consts)
            rep_fill(P, p_alog, NH, a_rep, rowbuf2, 4, t_consts, negate=True, do_exp=True)
            P.emit(nc)

        if phases >= 1:
            with ExitStack() as S:
                e0 = S.enter_context
                gam_rep = e0(nc.sbuf_tensor("gam_rep", [128, DM], F32, side="right"))
                bet_rep = e0(nc.sbuf_tensor("bet_rep", [128, DM], F32, side="right"))
                rowb = e0(nc.sbuf_tensor("rowb", [1, DM], F32, side="right"))
                xbuf = [e0(nc.sbuf_tensor(f"xb{i}", [128, DM], F32, side="right")) for i in range(2)]
                xn32 = [e0(nc.sbuf_tensor(f"xn32{i}", [128, DM], F32, side="right")) for i in range(2)]
                z232 = [e0(nc.sbuf_tensor(f"z232{i}", [128, DM], F32, side="right")) for i in range(2)]
                xl32 = [e0(nc.sbuf_tensor(f"xl32{i}", [128, DM], F32, side="right")) for i in range(2)]
                xstg = [e0(nc.sbuf_tensor(f"xstg{i}", [128, DM], BF16, side="right")) for i in range(2)]
                ustg = [e0(nc.sbuf_tensor(f"ustg{i}", [128, 128], F32, side="right")) for i in range(2)]
                l2row = e0(nc.sbuf_tensor("l2row", [1, 128], F32, side="right"))
                st1 = e0(nc.sbuf_tensor("st1", [128, 8], F32, side="right"))
                st2 = e0(nc.sbuf_tensor("st2", [128, 8], F32, side="right"))

                P = Prog("P0")
                rep_fill(P, p_gam, DM, gam_rep, rowb, 4, None)
                rep_fill(P, p_bet, DM, bet_rep, rowb, 4, None)

                t_l2prev = None
                for r in range(NT):
                    bi = r % 2
                    xb, xn_t, z2_t, xl_t = xbuf[bi], xn32[bi], z232[bi], xl32[bi]
                    xst = xstg[bi]

                    t_ld = P.op("sync", lambda e, r=r, xb=xb: e.dma_start(
                        xb[:, :], x_in[128 * r:128 * (r + 1), :]), waits=[FREE(xb)], inc=dsem(xb))
                    t_s1 = P.op("vector", lambda e, xb=xb: e.reduce_sum(
                        st1[:, 0:1], xb[:, :], axis=AX), waits=[t_ld, FREE(st1)], inc=s_ve)
                    t_s2 = P.op("scalar", lambda e, xb=xb, xn_t=xn_t: e.activation(
                        xn_t[:, :], xb[:, :], AF.Square, accum_out=st1[:, 1:2]),
                        waits=[t_ld, FREE(xn_t), t_s1], inc=s_se)
                    t_m = P.op("vector", lambda e: e.tensor_scalar_mul(
                        st1[:, 2:3], st1[:, 0:1], 1.0 / DM), waits=[t_s1, t_s2], inc=s_ve)
                    t_v = P.op("vector", lambda e: e.tensor_scalar_mul(
                        st1[:, 3:4], st1[:, 1:2], 1.0 / DM), waits=[t_m], inc=s_ve)
                    t_msq = P.op("vector", lambda e: e.tensor_tensor(
                        st1[:, 4:5], st1[:, 2:3], st1[:, 2:3], ALU.mult), waits=[t_v], inc=s_ve)
                    t_v2 = P.op("vector", lambda e: e.tensor_tensor(
                        st1[:, 5:6], st1[:, 3:4], st1[:, 4:5], ALU.subtract), waits=[t_msq], inc=s_ve)
                    t_lnv = P.op("scalar", lambda e: e.activation(
                        st1[:, 6:7], st1[:, 5:6], AF.Ln, bias=1e-5), waits=[t_v2], inc=s_se)
                    t_rs = P.op("scalar", lambda e: e.activation(
                        st1[:, 7:8], st1[:, 6:7], AF.Exp, scale=-0.5), waits=[t_lnv], inc=s_se)
                    t_xn = P.op("vector", lambda e, xb=xb, xn_t=xn_t: e.tensor_scalar(
                        xn_t[:, :], xb[:, :], st1[:, 2:3], st1[:, 7:8], ALU.subtract, ALU.mult),
                        waits=[t_rs], inc=s_ve)
                    t_st1 = P.op("gpsimd", lambda e, r=r, xn_t=xn_t: e.dma_start(
                        d_xn[128 * r:128 * (r + 1), :], xn_t[:, :]), waits=[t_xn], inc=dsem(xn_t, True))
                    t2_s1 = P.op("vector", lambda e, xn_t=xn_t: e.reduce_sum(
                        st2[:, 0:1], xn_t[:, :], axis=AX), waits=[t_xn, FREE(st2)], inc=s_ve)
                    t2_s2 = P.op("scalar", lambda e, xn_t=xn_t, z2_t=z2_t: e.activation(
                        z2_t[:, :], xn_t[:, :], AF.Square, accum_out=st2[:, 1:2]),
                        waits=[t_rs, t2_s1, FREE(z2_t)], inc=s_se)
                    t2_m = P.op("vector", lambda e: e.tensor_scalar_mul(
                        st2[:, 2:3], st2[:, 0:1], 1.0 / DM), waits=[t2_s1, t2_s2], inc=s_ve)
                    t2_v = P.op("vector", lambda e: e.tensor_scalar_mul(
                        st2[:, 3:4], st2[:, 1:2], 1.0 / DM), waits=[t2_m], inc=s_ve)
                    t2_msq = P.op("vector", lambda e: e.tensor_tensor(
                        st2[:, 4:5], st2[:, 2:3], st2[:, 2:3], ALU.mult), waits=[t2_v], inc=s_ve)
                    t2_v2 = P.op("vector", lambda e: e.tensor_tensor(
                        st2[:, 5:6], st2[:, 3:4], st2[:, 4:5], ALU.subtract), waits=[t2_msq], inc=s_ve)
                    t2_lnv = P.op("scalar", lambda e: e.activation(
                        st2[:, 6:7], st2[:, 5:6], AF.Ln, bias=1e-5), waits=[t2_v2], inc=s_se)
                    t2_rs = P.op("scalar", lambda e: e.activation(
                        st2[:, 7:8], st2[:, 6:7], AF.Exp, scale=-0.5), waits=[t2_lnv], inc=s_se)
                    t_z2 = P.op("vector", lambda e, xn_t=xn_t, z2_t=z2_t: e.tensor_scalar(
                        z2_t[:, :], xn_t[:, :], st2[:, 2:3], st2[:, 7:8], ALU.subtract, ALU.mult),
                        waits=[t2_rs], inc=s_ve)
                    SETFREE(xn_t, t_z2, t_st1)
                    SETFREE(st2, t_z2)
                    t_g1 = P.op("gpsimd", lambda e, z2_t=z2_t, xl_t=xl_t: e.tensor_tensor(
                        xl_t[:, :], z2_t[:, :], gam_rep[:, :], ALU.mult),
                        waits=[t_z2, FREE(xl_t)], inc=s_po)
                    t_g2 = P.op("gpsimd", lambda e, xl_t=xl_t: e.tensor_tensor(
                        xl_t[:, :], xl_t[:, :], bet_rep[:, :], ALU.add), waits=[t_g1], inc=s_po)
                    t_st2 = P.op("gpsimd", lambda e, r=r, xl_t=xl_t: e.dma_start(
                        d_xln[128 * r:128 * (r + 1), :], xl_t[:, :]), waits=[t_g2], inc=dsem(xl_t, True))
                    t_ss = P.op("scalar", lambda e, xl_t=xl_t, xb=xb: e.activation(
                        xb[:, :], xl_t[:, :], AF.Square, accum_out=st1[:, 0:1]),
                        waits=[t_g2, t_xn, t2_s2, t_l2prev], inc=s_se)
                    SETFREE(xb, t_ss)
                    t_sq = P.op("scalar", lambda e: e.activation(
                        st1[:, 1:2], st1[:, 0:1], AF.Sqrt), waits=[t_ss], inc=s_se)
                    t_mx = P.op("vector", lambda e: e.tensor_scalar_max(
                        st1[:, 2:3], st1[:, 1:2], 1e-12), waits=[t_sq, t_z2], inc=s_ve)
                    t_rc = P.op("vector", lambda e: e.reciprocal(
                        st1[:, 3:4], st1[:, 2:3]), waits=[t_mx], inc=s_ve)
                    t_l2 = P.op("vector", lambda e: e.tensor_tensor(
                        st1[:, 4:5], st1[:, 3:4], sc_rep[:, 0:1], ALU.mult), waits=[t_rc], inc=s_ve)
                    t_l2prev = t_l2
                    SETFREE(st1, t_l2)
                    t_tr = P.op("tensor", lambda e: e.transpose(
                        PS[5][0:1, 0:128], st1[:, 4:5], idf[:, :]),
                        waits=[t_l2, FREE(PS[5])], inc=s_pe)
                    t_lr = P.op("vector", lambda e: e.tensor_copy(
                        l2row[0:1, :], PS[5][0:1, 0:128]), waits=[t_tr, FREE(l2row)], inc=s_ve)
                    SETFREE(PS[5], t_lr)
                    t_repm = P.op("tensor", lambda e: e.matmul(
                        PS[4][:, 0:128], o1f[0:1, :], l2row[0:1, :], start=True, stop=True),
                        waits=[t_lr, FREE(PS[4])], inc=s_pe)
                    SETFREE(l2row, t_repm)
                    last_xst = None
                    for k in range(KD):
                        pf = PS[k % 2]
                        ust = ustg[k % 2]
                        tt = P.op("tensor", lambda e, k=k, pf=pf, z2_t=z2_t: e.transpose(
                            pf[:, 0:128], z2_t[:, 128 * k:128 * (k + 1)], idf[:, :]),
                            waits=[t_z2, FREE(pf)], inc=s_pe)
                        tu = P.op("vector", lambda e, k=k, r=r, pf=pf: e.tensor_scalar(
                            un_fm[k][:, 128 * r:128 * (r + 1)], pf[:, 0:128],
                            lnw_c[:, k:k + 1], lnb_c[:, k:k + 1], ALU.mult, ALU.add),
                            waits=[tt], inc=s_ve)
                        tx1 = P.op("vector", lambda e, k=k, pf=pf, ust=ust: e.tensor_scalar(
                            ust[:, :], pf[:, 0:128], gam_c[:, k:k + 1], bet_c[:, k:k + 1],
                            ALU.mult, ALU.add), waits=[tu, FREE(ust)], inc=s_ve)
                        SETFREE(pf, tx1)
                        tx2 = P.op("vector", lambda e, k=k, ust=ust, xst=xst: e.tensor_tensor(
                            xst[:, 128 * k:128 * (k + 1)], ust[:, :], PS[4][:, 0:128], ALU.mult),
                            waits=[tx1, t_repm, FREE(xst) if k == 0 else None], inc=s_ve)
                        SETFREE(ust, tx2)
                        last_xst = tx2
                    SETFREE(PS[4], last_xst)
                    SETFREE(z2_t, s_pe.tok(), t_g1)
                    dstap = d_xnorm.ap().rearrange("(k p) n -> p k n", p=128)
                    t_st3 = P.op("sync", lambda e, r=r, xst=xst, dstap=dstap: e.dma_start(
                        dstap[:, :, 128 * r:128 * (r + 1)],
                        xst[:, :].rearrange("p (k n) -> p k n", k=KD)),
                        waits=[last_xst], inc=dsem(xst))
                    SETFREE(xst, t_st3)
                    SETFREE(xl_t, t_ss, t_st2)
                P.emit(nc)


        if phases >= 2:
            with ExitStack() as S:
                e1 = S.enter_context
                wslot = [e1(nc.sbuf_tensor(f"ws{i}", [128, 16 * 512], BF16, side="right")) for i in range(2)]
                ostg = [e1(nc.sbuf_tensor(f"og{i}", [128, 512], BF16, side="right")) for i in range(4)]
                bsq = e1(nc.sbuf_tensor("bsq", [128, NTOK], F32, side="right"))
                ssrow = e1(nc.sbuf_tensor("ssrow", [1, NTOK], F32, side="right"))
                wdtn = e1(nc.sbuf_tensor("wdtn", [64, DM], BF16, side="right"))
                wdt = e1(nc.sbuf_tensor("wdt", [128, 16 * 64], BF16, side="right"))
                dtt = [e1(nc.sbuf_tensor(f"dtt{i}", [128, 64], F32, side="right")) for i in range(4)]
                acsb = e1(nc.sbuf_tensor("acsb", [128, 64], F32, side="right"))
                rowc = e1(nc.sbuf_tensor("rowc", [1, 128], F32, side="right"))
                rowcd = e1(nc.sbuf_tensor("rowcd", [1, 128], F32, side="right"))

                P = Prog("M1")
                guard(P)

                def load_wslot(ws, e0, width):
                    toks = None
                    nk = KD
                    for k in range(nk):
                        toks = P.op("sync", lambda e, ws=ws, e0=e0, k=k, width=width: e.dma_start(
                            ws[:, width * k:width * (k + 1)],
                            d_winb[e0:e0 + width, 128 * k:128 * (k + 1)], transpose=True),
                            waits=[FREE(ws) if k == 0 else None,
                                   (s_wc, s_wc.n) if k == 0 else None], inc=dsem(ws))
                    return toks

                t_w = {0: load_wslot(wslot[0], 0, 512)} if "zx" in m1_parts else {}
                for blk in (range(16) if "zx" in m1_parts else []):
                    e0 = blk * 512
                    is_z = blk < 8
                    ws = wslot[blk % 2]
                    if blk + 1 < 16:
                        t_w[blk + 1] = load_wslot(wslot[(blk + 1) % 2], (blk + 1) * 512, 512)
                    last_mm_of_ws = None
                    for r in range(NT):
                        pp = PS[r % 4]
                        t_mm = None
                        for k in range(KD):
                            t_mm = P.op("tensor", lambda e, pp=pp, k=k, r=r, ws=ws: e.matmul(
                                pp[:, 0:512], un_fm[k][:, 128 * r:128 * (r + 1)],
                                ws[:, 512 * k:512 * (k + 1)], start=(k == 0), stop=(k == KD - 1)),
                                waits=[t_w[blk], FREE(pp)] if k == 0 else [], inc=s_pe if k == KD - 1 else None)
                        last_mm_of_ws = t_mm
                        og = ostg[r % 4]
                        if is_z:
                            t_e = P.op("scalar", lambda e, pp=pp, og=og: e.activation(
                                og[:, :], pp[:, 0:512], AF.Silu), waits=[t_mm, FREE(og)], inc=s_se)
                        else:
                            t_e = P.op("vector", lambda e, pp=pp, og=og: e.tensor_copy(
                                og[:, :], pp[:, 0:512]), waits=[t_mm, FREE(og)], inc=s_ve)
                        SETFREE(pp, t_e)
                        dst = d_sz if is_z else d_x
                        c0 = e0 if is_z else e0 - EIN
                        t_st = P.op("sync", lambda e, r=r, og=og, dst=dst, c0=c0: e.dma_start(
                            dst[128 * r:128 * (r + 1), c0:c0 + 512], og[:, :]),
                            waits=[t_e], inc=dsem(og))
                        SETFREE(og, t_st)
                    SETFREE(ws, last_mm_of_ws)

                wsbc = wslot[0]
                for bc in (range(2) if "bc" in m1_parts else []):
                    e0 = 2 * EIN + bc * DS
                    tgt = b_fm if bc == 0 else c_fm
                    wnorm = bnw_c if bc == 0 else cnw_c
                    t_wl = None
                    for k in range(KD):
                        t_wl = P.op("sync", lambda e, e0=e0, k=k: e.dma_start(
                            wsbc[:, 128 * k:128 * (k + 1)],
                            d_winb[e0:e0 + DS, 128 * k:128 * (k + 1)], transpose=True),
                            waits=[FREE(wsbc) if k == 0 else None], inc=dsem(wsbc))
                    t_raws = []
                    for ns in range(4):
                        pp = PS[ns % 4]
                        t_mm = None
                        for k in range(KD):
                            t_mm = P.op("tensor", lambda e, pp=pp, k=k, ns=ns: e.matmul(
                                pp[:, 0:512], wsbc[:, 128 * k:128 * (k + 1)],
                                un_fm[k][:, 512 * ns:512 * (ns + 1)],
                                start=(k == 0), stop=(k == KD - 1)),
                                waits=[t_wl, FREE(pp)] if k == 0 else [], inc=s_pe if k == KD - 1 else None)
                        t_sqv = P.op("scalar", lambda e, pp=pp, ns=ns: e.activation(
                            bsq[:, 512 * ns:512 * (ns + 1)], pp[:, 0:512], AF.Square),
                            waits=[t_mm, FREE(bsq) if ns == 0 else None], inc=s_se)
                        t_raw = P.op("vector", lambda e, pp=pp, ns=ns, tgt=tgt: e.tensor_copy(
                            tgt[:, 512 * ns:512 * (ns + 1)], pp[:, 0:512]),
                            waits=[t_mm, t_sqv], inc=s_ve)
                        t_raws.append(t_raw)
                        SETFREE(pp, t_sqv, t_raw)
                    SETFREE(wsbc, s_pe.tok())
                    t_evs = []
                    for ns in range(4):
                        pq = PS[4 + ns % 2]
                        t_nm = P.op("tensor", lambda e, pq=pq, ns=ns: e.matmul(
                            pq[0:1, 0:512], ocf[:, :], bsq[:, 512 * ns:512 * (ns + 1)],
                            start=True, stop=True),
                            waits=[(s_se, s_se.n), FREE(pq)], inc=s_pe)
                        t_ev = P.op("vector", lambda e, pq=pq, ns=ns: e.tensor_copy(
                            ssrow[0:1, 512 * ns:512 * (ns + 1)], pq[0:1, 0:512]),
                            waits=[t_nm, FREE(ssrow) if ns == 0 else None], inc=s_ve)
                        SETFREE(pq, t_ev)
                        t_evs.append(t_ev)
                    SETFREE(bsq, s_pe.tok())
                    t_ln = P.op("scalar", lambda e: e.activation(
                        ssrow[0:1, :], ssrow[0:1, :], AF.Ln, scale=0.0078125, bias=1e-5),
                        waits=t_evs, inc=s_se)
                    t_ex = P.op("scalar", lambda e: e.activation(
                        ssrow[0:1, :], ssrow[0:1, :], AF.Exp, scale=-0.5),
                        waits=[t_ln], inc=s_se)
                    for ns in range(4):
                        pq = PS[4 + ns % 2]
                        t_bc2 = P.op("tensor", lambda e, pq=pq, ns=ns: e.matmul(
                            pq[:, 0:512], o1f[0:1, :], ssrow[0:1, 512 * ns:512 * (ns + 1)],
                            start=True, stop=True), waits=[t_ex, FREE(pq)], inc=s_pe)
                        t_fin = P.op("vector", lambda e, pq=pq, ns=ns, tgt=tgt, wn=wnorm: e.scalar_tensor_tensor(
                            tgt[:, 512 * ns:512 * (ns + 1)], tgt[:, 512 * ns:512 * (ns + 1)],
                            wn[:, 0:1], pq[:, 0:512], ALU.mult, ALU.mult),
                            waits=[t_bc2, t_raws[ns]], inc=s_ve)
                        SETFREE(pq, t_fin)
                    SETFREE(ssrow, s_pe.tok())

                do_dt = "dt" in m1_parts
                t_dn = P.op("gpsimd", lambda e: e.dma_start(
                    wdtn[:, :], w_in[2 * EIN + 2 * DS:EPROJ, :]), inc=dsem(wdtn, True)) if do_dt else None
                t_tp = None
                for k in (range(KD) if do_dt else []):
                    pq = PS[4 + k % 2]
                    t_t = P.op("tensor", lambda e, k=k, pq=pq: e.transpose(
                        pq[:, 0:32].bitcast(BF16),
                        wdtn[0:64, 128 * k:128 * (k + 1)], idb[0:64, 0:64]),
                        waits=[t_dn, FREE(pq)], inc=s_pe)
                    t_tp = P.op("vector", lambda e, k=k, pq=pq: e.tensor_copy(
                        wdt[:, 64 * k:64 * (k + 1)], pq[:, 0:32].bitcast(BF16)), waits=[t_t], inc=s_ve)
                    SETFREE(pq, t_tp)
                for r in (range(NT) if do_dt else []):
                    pp = PS[r % 4]
                    t_mm = None
                    for k in range(KD):
                        t_mm = P.op("tensor", lambda e, pp=pp, k=k, r=r: e.matmul(
                            pp[:, 0:64], un_fm[k][:, 128 * r:128 * (r + 1)],
                            wdt[:, 64 * k:64 * (k + 1)], start=(k == 0), stop=(k == KD - 1)),
                            waits=[(s_ve, s_ve.n), FREE(pp)] if k == 0 else [],
                            inc=s_pe if k == KD - 1 else None)
                    t_d0 = P.op("vector", lambda e, pp=pp: e.tensor_tensor(
                        dtt[0][:, :], pp[:, 0:64], dtb_rep[:, :], ALU.add),
                        waits=[t_mm, FREE(dtt[0])], inc=s_ve)
                    SETFREE(pp, t_d0)
                    t_ab = P.op("scalar", lambda e: e.activation(
                        dtt[1][:, :], dtt[0][:, :], AF.Abs),
                        waits=[t_d0, FREE(dtt[1])], inc=s_se)
                    t_e1 = P.op("scalar", lambda e: e.activation(
                        dtt[1][:, :], dtt[1][:, :], AF.Exp, scale=-1.0), waits=[t_ab], inc=s_se)
                    t_l1 = P.op("scalar", lambda e: e.activation(
                        dtt[1][:, :], dtt[1][:, :], AF.Ln, bias=1.0), waits=[t_e1], inc=s_se)
                    t_r0 = P.op("vector", lambda e: e.tensor_scalar_max(
                        dtt[2][:, :], dtt[0][:, :], 0.0), waits=[t_ab, FREE(dtt[2])], inc=s_ve)
                    SETFREE(dtt[0], t_r0)
                    t_dtp = P.op("vector", lambda e: e.tensor_tensor(
                        dtt[3][:, :], dtt[2][:, :], dtt[1][:, :], ALU.add),
                        waits=[t_l1, t_r0, FREE(dtt[3])], inc=s_ve)
                    SETFREE(dtt[1], t_dtp)
                    SETFREE(dtt[2], t_dtp)
                    t_dA = P.op("vector", lambda e, r=r: e.tensor_tensor(
                        dA[:, 64 * r:64 * (r + 1)], dtt[3][:, :], a_rep[:, :], ALU.mult),
                        waits=[t_dtp], inc=s_ve)
                    SETFREE(dtt[3], t_dA)
                    if dt_stop < 2:
                        continue
                    t_acs = P.op("tensor", lambda e, r=r: e.matmul(
                        PS[5][:, 0:64], u2[:, :], dA[:, 64 * r:64 * (r + 1)],
                        start=True, stop=True), waits=[t_dA, FREE(PS[5])], inc=s_pe)
                    t_eacs = P.op("scalar", lambda e, r=r: e.activation(
                        eacs[:, 64 * r:64 * (r + 1)], PS[5][:, 0:64], AF.Exp),
                        waits=[t_acs], inc=s_se)
                    t_acsb = P.op("vector", lambda e: e.tensor_copy(
                        acsb[:, :], PS[5][:, 0:64]), waits=[t_acs, t_eacs, FREE(acsb)], inc=s_ve)
                    SETFREE(PS[5], t_eacs, t_acsb)
                    if dt_stop < 3:
                        continue
                    t_c0 = P.op("tensor", lambda e, r=r: e.matmul(
                        PS[6][0:1, 0:64], half[:, 0:1], dA[:, 64 * r:64 * (r + 1)],
                        start=True, stop=True), waits=[t_dA, FREE(PS[6])], inc=s_pe)
                    t_c1 = P.op("tensor", lambda e, r=r: e.matmul(
                        PS[6][0:1, 64:128], half[:, 1:2], dA[:, 64 * r:64 * (r + 1)],
                        start=True, stop=True), waits=[], inc=s_pe)
                    t_crow = P.op("vector", lambda e: e.tensor_copy(
                        rowc[0:1, :], PS[6][0:1, 0:128]), waits=[t_c1, FREE(rowc)], inc=s_ve)
                    SETFREE(PS[6], t_crow)
                    t_cdr = P.op("scalar", lambda e: e.activation(
                        rowcd[0:1, :], rowc[0:1, :], AF.Exp), waits=[t_crow, FREE(rowcd)], inc=s_se)
                    if dt_stop < 4:
                        continue
                    t_b0 = P.op("tensor", lambda e: e.matmul(
                        PS[7][0:64, 0:64], o1f[0:1, 0:64], rowc[0:1, 0:64],
                        start=True, stop=True), waits=[t_crow, FREE(PS[7])], inc=s_pe)
                    t_b1 = P.op("tensor", lambda e: e.matmul(
                        PS[7][64:128, 0:64], o1f[0:1, 0:64], rowc[0:1, 64:128],
                        start=True, stop=True, tile_position=(0, 64)), waits=[], inc=s_pe)
                    SETFREE(rowc, t_b1)
                    t_sub = P.op("vector", lambda e: e.tensor_tensor(
                        acsb[:, :], PS[7][:, 0:64], acsb[:, :], ALU.subtract),
                        waits=[t_b1, t_acsb], inc=s_ve)
                    SETFREE(PS[7], t_sub)
                    t_dec = P.op("scalar", lambda e, r=r: e.activation(
                        dec[:, 64 * r:64 * (r + 1)], acsb[:, :], AF.Exp),
                        waits=[t_sub], inc=s_se)
                    SETFREE(acsb, t_dec)
                    if dt_stop < 5:
                        continue
                    t_k0 = P.op("tensor", lambda e: e.matmul(
                        PS[4][:, 0:64], o1f[0:1, :], rowcd[0:1, 0:64],
                        start=True, stop=True), waits=[t_cdr, FREE(PS[4])], inc=s_pe)
                    t_k1 = P.op("tensor", lambda e: e.matmul(
                        PS[4][:, 64:128], o1f[0:1, :], rowcd[0:1, 64:128],
                        start=True, stop=True), waits=[], inc=s_pe)
                    t_cdw = P.op("vector", lambda e, r=r: e.tensor_copy(
                        cdrep[:, 128 * r:128 * (r + 1)], PS[4][:, 0:128]),
                        waits=[t_k1], inc=s_ve)
                    SETFREE(PS[4], t_cdw)
                    SETFREE(rowcd, t_k1)
                P.emit(nc)
            UNS.close()


        if phases >= 3:
            with ExitStack() as S:
                e2 = S.enter_context
                xsl = [e2(nc.sbuf_tensor(f"xsl{i}", [128, EIN], BF16, side="right")) for i in range(2)]
                szl = [e2(nc.sbuf_tensor(f"szl{i}", [128, EIN], BF16, side="right")) for i in range(2)]
                sLT = e2(nc.sbuf_tensor("sLT", [128, EIN], BF16, side="right"))
                srun = e2(nc.sbuf_tensor("srun", [128, EIN], BF16, side="right"))
                ygs = [e2(nc.sbuf_tensor(f"ygs{i}", [128, EIN], BF16, side="right")) for i in range(2)]
                ytp = [e2(nc.sbuf_tensor(f"ytp{i}", [128, EIN], BF16, side="right")) for i in range(2)]
                dAm = [e2(nc.sbuf_tensor(f"dAm{i}", [128, 512], F32, side="right")) for i in range(2)]
                eDs = [e2(nc.sbuf_tensor(f"eDs{i}", [128, 512], F32, side="right")) for i in range(2)]
                xds = [e2(nc.sbuf_tensor(f"xds{i}", [128, 512], BF16, side="right")) for i in range(2)]
                stp = [e2(nc.sbuf_tensor(f"stp{i}", [128, 512], BF16, side="right")) for i in range(2)]
                gb1 = [e2(nc.sbuf_tensor(f"gb1{i}", [128, 512], F32, side="right")) for i in range(2)]
                gb2 = [e2(nc.sbuf_tensor(f"gb2{i}", [128, 512], F32, side="right")) for i in range(2)]
                btm = e2(nc.sbuf_tensor("btm", [128, 128], BF16, side="right"))

                P = Prog("M2")
                guard(P)
                t_srz = P.op("gpsimd", lambda e: e.memset(srun[:, :], 0.0), inc=s_po)
                SETFREE(srun, t_srz)
                sLT_free = {}
                srun_dep = {}

                def bcap(t, pstride, pcnt, blocks, off):
                    return bass.AP(t.tensor if isinstance(t, bass.AP) else t, off,
                                   [[pstride, pcnt]] + [list(b) for b in blocks])

                for r in range(NT):
                    xb, zb = xsl[r % 2], szl[r % 2]
                    yg, yt = ygs[r % 2], ytp[r % 2]
                    dump_yg_tok = None
                    t_x = P.op("sync", lambda e, r=r, xb=xb: e.dma_start(
                        xb[:, :], d_x[128 * r:128 * (r + 1), :]), waits=[FREE(xb)], inc=dsem(xb))
                    t_z = P.op("sync", lambda e, r=r, zb=zb: e.dma_start(
                        zb[:, :], d_sz[128 * r:128 * (r + 1), :]), waits=[FREE(zb)], inc=dsem(zb))
                    t_sc0 = P.op("tensor", lambda e, r=r: e.matmul(
                        PS[6][0:64, 0:64], b_fm[:, 128 * r:128 * r + 64],
                        c_fm[:, 128 * r:128 * r + 64], start=True, stop=True),
                        waits=[FREE(PS[6])], inc=s_pe)
                    t_sc1 = P.op("tensor", lambda e, r=r: e.matmul(
                        PS[6][64:128, 0:64], b_fm[:, 128 * r + 64:128 * r + 128],
                        c_fm[:, 128 * r + 64:128 * r + 128], start=True, stop=True,
                        tile_position=(0, 64)), waits=[], inc=s_pe)
                    t_stm = P.op("vector", lambda e, r=r: e.tensor_tensor(
                        stm[:, 64 * r:64 * (r + 1)], PS[6][:, 0:64], tri2[:, :], ALU.mult),
                        waits=[t_sc1], inc=s_ve)
                    SETFREE(PS[6], t_stm)
                    t_bt0 = P.op("tensor", lambda e, r=r: e.transpose(
                        PS[7][0:64, 0:64].bitcast(BF16), b_fm[:, 128 * r:128 * r + 64], idb[:, :]),
                        waits=[FREE(PS[7])], inc=s_pe)
                    t_bt1 = P.op("tensor", lambda e, r=r: e.transpose(
                        PS[7][64:128, 0:64].bitcast(BF16), b_fm[:, 128 * r + 64:128 * r + 128],
                        idb[:, :], tile_position=(0, 64)), waits=[], inc=s_pe)
                    t_btm = P.op("vector", lambda e: e.tensor_copy(
                        btm[:, :], PS[7][:, 0:64].bitcast(BF16)), waits=[t_bt1, FREE(btm)], inc=s_ve)
                    SETFREE(PS[7], t_btm)

                    t_ygl = []
                    for q in range(4):
                        t_sLTs = {}
                        for j in (2 * q, 2 * q + 1):
                            da = dAm[j % 2]
                            ed = eDs[j % 2]
                            pD = PS[6] if j % 2 == 0 else PS[7]
                            t_dm = P.op("vector", lambda e, r=r, j=j, da=da: e.tensor_tensor(
                                da[:, :],
                                bcap(dA, NH * NT, 128, [[1, 8], [0, CH]], 64 * r + 8 * j),
                                bcap(tri2, CH, 128, [[0, 8], [1, CH]], 0),
                                ALU.mult), waits=[FREE(da)], inc=s_ve)
                            t_D = P.op("tensor", lambda e, pD=pD, da=da: e.matmul(
                                pD[:, 0:512], slow[:, :], da[:, :], start=True, stop=True),
                                waits=[t_dm, FREE(pD)], inc=s_pe)
                            SETFREE(da, t_D)
                            t_eD = P.op("scalar", lambda e, pD=pD, ed=ed: e.activation(
                                ed[:, :], pD[:, 0:512], AF.Exp), waits=[t_D, FREE(ed)], inc=s_se)
                            SETFREE(pD, t_eD)
                            t_sLT = P.op("vector", lambda e, r=r, j=j, ed=ed: e.tensor_tensor(
                                sLT[:, 512 * j:512 * (j + 1)], ed[:, :],
                                bcap(stm, CH * NT, 128, [[0, 8], [1, CH]], 64 * r),
                                ALU.mult), waits=[t_eD, t_stm, sLT_free.get(j)], inc=s_ve)
                            SETFREE(ed, t_sLT)
                            t_sLTs[j] = t_sLT
                        pY = {0: PS[0], 1: PS[1]}
                        t_yd = None
                        for cp in range(2):
                            for hh in range(16):
                                h = 16 * q + hh
                                j = h // 8
                                pb = pY[hh // 8]
                                t_yd = P.op("tensor", lambda e, cp=cp, h=h, hh=hh, pb=pb, xb=xb: e.matmul(
                                    pb[64 * cp:64 * (cp + 1), 64 * (hh % 8):64 * (hh % 8) + 64],
                                    sLT[64 * cp:64 * cp + 64, 64 * h:64 * h + 64],
                                    xb[64 * cp:64 * cp + 64, 64 * h:64 * h + 64],
                                    start=True, stop=True,
                                    tile_position=(64, 64) if cp == 1 else None),
                                    waits=[t_sLTs[j], t_x,
                                           FREE(pb) if (cp == 0 and hh % 8 == 0) else None],
                                    inc=s_pe if (cp == 1 and hh in (7, 15)) else None)
                                if cp == 1 and hh == 7:
                                    t_yd0 = (s_pe, s_pe.n)
                        t_yd1 = (s_pe, s_pe.n)
                        sLT_free[2 * q] = t_yd1
                        sLT_free[2 * q + 1] = t_yd1
                        t_xds = {}
                        for j in (2 * q, 2 * q + 1):
                            xd = xds[j % 2]
                            t_xd = P.op("gpsimd", lambda e, r=r, j=j, xd=xd, xb=xb: e.tensor_tensor(
                                xd[:, :], xb[:, 512 * j:512 * (j + 1)],
                                bcap(dec, NH * NT, 128, [[1, 8], [0, CH]], 64 * r + 8 * j),
                                ALU.mult), waits=[t_x, FREE(xd)], inc=s_po)
                            t_xds[j] = t_xd
                        pO = {2 * q: PS[2], 2 * q + 1: PS[3]}
                        pS = {2 * q: PS[4], 2 * q + 1: PS[5]}
                        t_upd = {}
                        t_yoff = {}
                        for cp in range(2):
                            for j in (2 * q, 2 * q + 1):
                                t_yo = P.op("tensor", lambda e, r=r, cp=cp, j=j, pb=pO[j]: e.matmul(
                                    pb[64 * cp:64 * (cp + 1), 0:512],
                                    c_fm[:, 128 * r + 64 * cp:128 * r + 64 * cp + 64],
                                    srun[:, 512 * j:512 * (j + 1)], start=True, stop=True,
                                    tile_position=(0, 64) if cp == 1 else None),
                                    waits=[FREE(pO[j]) if cp == 0 else None,
                                           t_upd.get((cp - 1, j)),
                                           srun_dep.get(j) if cp == 0 else None,
                                           t_srz if r == 0 else None],
                                    inc=s_pe)
                                t_yoff[(cp, j)] = t_yo
                                t_st = P.op("tensor", lambda e, cp=cp, j=j, pb=pS[j]: e.matmul(
                                    pb[:, 0:512], btm[64 * cp:64 * cp + 64, :],
                                    xds[j % 2][64 * cp:64 * cp + 64, :], start=True, stop=True),
                                    waits=[t_btm, t_xds[j], FREE(pS[j])], inc=s_pe)
                                t_u1 = P.op("vector", lambda e, r=r, cp=cp, j=j: e.tensor_tensor(
                                    stp[j % 2][:, :], srun[:, 512 * j:512 * (j + 1)],
                                    bcap(cdrep, 2 * NH * NT, 128, [[1, 8], [0, CH]],
                                         128 * r + 64 * cp + 8 * (j - 2 * q) + 16 * q),
                                    ALU.mult), waits=[t_yo, FREE(stp[j % 2])], inc=s_ve)
                                t_u2 = P.op("vector", lambda e, j=j, pb=pS[j]: e.tensor_tensor(
                                    srun[:, 512 * j:512 * (j + 1)], stp[j % 2][:, :],
                                    pb[:, 0:512], ALU.add), waits=[t_u1, t_st], inc=s_ve)
                                SETFREE(stp[j % 2], t_u2)
                                SETFREE(pS[j], t_u2)
                                t_upd[(cp, j)] = t_u2
                                if cp == 1:
                                    SETFREE(xds[j % 2], (s_pe, s_pe.n))
                                    srun_dep[j] = t_u2
                        for j in (2 * q, 2 * q + 1):
                            g1, g2 = gb1[j % 2], gb2[j % 2]
                            t_g1 = P.op("vector", lambda e, r=r, j=j, g1=g1, pb=pO[j]: e.tensor_tensor(
                                g1[:, :], pb[:, 0:512],
                                bcap(eacs, NH * NT, 128, [[1, 8], [0, CH]], 64 * r + 8 * j),
                                ALU.mult), waits=[t_yoff[(1, j)], FREE(g1)], inc=s_ve)
                            SETFREE(pO[j], t_g1)
                            t_g2 = P.op("gpsimd", lambda e, r=r, j=j, g2=g2, xb=xb: e.tensor_tensor(
                                g2[:, :], xb[:, 512 * j:512 * (j + 1)],
                                bcap(dd_rep, NH, 128, [[1, 8], [0, CH]], 8 * j),
                                ALU.mult), waits=[t_x, FREE(g2)], inc=s_po)
                            pb = pY[(j - 2 * q)]
                            t_g3 = P.op("vector", lambda e, g1=g1, pb=pb: e.scalar_tensor_tensor(
                                g1[:, :], pb[:, 0:512], epsb[:, 2:3], g1[:, :],
                                ALU.add, ALU.add), waits=[t_yd1, t_g1], inc=s_ve)
                            SETFREE(pb, t_g3)
                            t_g4 = P.op("gpsimd", lambda e, g1=g1, g2=g2: e.tensor_tensor(
                                g1[:, :], g1[:, :], g2[:, :], ALU.add),
                                waits=[t_g3, t_g2], inc=s_po)
                            SETFREE(g2, t_g4)
                            t_g5 = P.op("gpsimd", lambda e, j=j, g1=g1, yg=yg, zb=zb: e.tensor_tensor(
                                yg[:, 512 * j:512 * (j + 1)], g1[:, :],
                                zb[:, 512 * j:512 * (j + 1)], ALU.mult),
                                waits=[t_g4, t_z, FREE(yg) if j == 0 else None], inc=s_po)
                            SETFREE(g1, t_g5)
                            t_ygl.append(t_g5)
                    SETFREE(xb, s_po.tok(), s_pe.tok())
                    SETFREE(zb, s_po.tok())
                    SETFREE(btm, s_pe.tok(), t_btm)
                    if d_m2 is not None and r == 0:
                        td1 = P.op("gpsimd", lambda e: e.dma_start(d_m2[:, 0:EIN], sLT[:, :]),
                                   waits=[(s_ve, s_ve.n)], inc=dsem("dmp1", True))
                        td2 = P.op("gpsimd", lambda e: e.dma_start(d_m2[:, EIN:2 * EIN], srun[:, :]),
                                   waits=[(s_ve, s_ve.n)], inc=dsem("dmp2", True))
                        td3 = P.op("gpsimd", lambda e, yg=yg: e.dma_start(d_m2[:, 2 * EIN:3 * EIN], yg[:, :]),
                                   waits=[(s_po, s_po.n)], inc=dsem("dmp3", True))
                        for jj in range(8):
                            sLT_free[jj] = [sLT_free.get(jj), td1]
                            srun_dep[jj] = [srun_dep.get(jj), td2]
                        dump_yg_tok = td3
                    else:
                        dump_yg_tok = None
                    last_yt = None
                    prev_ev = {}
                    for k in range(KE):
                        pT = PS[6] if k % 2 == 0 else PS[7]
                        t_t = P.op("tensor", lambda e, k=k, pT=pT, yg=yg: e.transpose(
                            pT[:, 0:64].bitcast(BF16), yg[:, 128 * k:128 * (k + 1)], idb[:, :]),
                            waits=[t_ygl[k // 4] if k // 4 < len(t_ygl) else None,
                                   t_ygl[-1] if k == 0 else None,
                                   FREE(pT), prev_ev.get(k % 2)], inc=s_pe)
                        eng = "vector" if k % 2 == 0 else "scalar"
                        if eng == "vector":
                            t_ev = P.op("vector", lambda e, k=k, pT=pT, yt=yt: e.tensor_copy(
                                yt[:, 128 * k:128 * (k + 1)], pT[:, 0:64].bitcast(BF16)),
                                waits=[t_t, FREE(yt) if k == 0 else None], inc=s_ve)
                        else:
                            t_ev = P.op("scalar", lambda e, k=k, pT=pT, yt=yt: e.copy(
                                yt[:, 128 * k:128 * (k + 1)], pT[:, 0:64].bitcast(BF16)),
                                waits=[t_t, FREE(yt) if k == 1 else None], inc=s_se)
                        SETFREE(pT, t_ev)
                        prev_ev[k % 2] = t_ev
                        last_yt = t_ev
                    SETFREE(yg, s_pe.tok(), dump_yg_tok)
                    dstap = d_yg.ap().rearrange("(k p) n -> p k n", p=128)
                    t_sty = P.op("sync", lambda e, r=r, yt=yt, dstap=dstap: e.dma_start(
                        dstap[:, :, 128 * r:128 * (r + 1)],
                        yt[:, :].rearrange("p (k n) -> p k n", k=KE)),
                        waits=[last_yt, (s_ve, s_ve.n), (s_se, s_se.n)], inc=dsem(yt))
                    SETFREE(yt, t_sty)
                P.emit(nc)


        if phases >= 4:
            with ExitStack() as S:
                e3 = S.enter_context
                woh = e3(nc.sbuf_tensor("woh", [128, KE * 1024], BF16, side="right"))
                ygc = [e3(nc.sbuf_tensor(f"ygc{i}", [128, EIN], BF16, side="right")) for i in range(2)]
                mstg = [e3(nc.sbuf_tensor(f"mstg{i}", [128, 1024], BF16, side="right")) for i in range(2)]

                P = Prog("M3")
                guard(P)
                ygap = d_yg.ap().rearrange("(k p) n -> p k n", p=128)
                for dh in range(2):
                    t_wl = None
                    for ec in range(KE):
                        t_wl = P.op("sync", lambda e, dh=dh, ec=ec: e.dma_start(
                            woh[:, 1024 * ec:1024 * (ec + 1)],
                            d_woutb[1024 * dh:1024 * (dh + 1), 128 * ec:128 * (ec + 1)],
                            transpose=True),
                            waits=[FREE(woh) if ec == 0 else None,
                                   (s_wo, s_wo.n) if ec == 0 else None], inc=dsem(woh))
                    for r in range(NT):
                        yc = ygc[r % 2]
                        t_y = P.op("sync", lambda e, r=r, yc=yc: e.dma_start(
                            yc[:, :].rearrange("p (k n) -> p k n", k=KE),
                            ygap[:, :, 128 * r:128 * (r + 1)]),
                            waits=[FREE(yc)], inc=dsem(yc))
                        pa, pb = PS[(2 * r) % 4], PS[(2 * r + 1) % 4]
                        t_ma = t_mb = None
                        for ec in range(KE):
                            t_ma = P.op("tensor", lambda e, ec=ec, pa=pa, yc=yc: e.matmul(
                                pa[:, 0:512], yc[:, 128 * ec:128 * (ec + 1)],
                                woh[:, 1024 * ec:1024 * ec + 512],
                                start=(ec == 0), stop=(ec == KE - 1)),
                                waits=[t_y, t_wl, FREE(pa)] if ec == 0 else [],
                                inc=s_pe if ec == KE - 1 else None)
                            t_mb = P.op("tensor", lambda e, ec=ec, pb=pb, yc=yc: e.matmul(
                                pb[:, 0:512], yc[:, 128 * ec:128 * (ec + 1)],
                                woh[:, 1024 * ec + 512:1024 * (ec + 1)],
                                start=(ec == 0), stop=(ec == KE - 1)),
                                waits=[FREE(pb)] if ec == 0 else [],
                                inc=s_pe if ec == KE - 1 else None)
                        ms = mstg[r % 2]
                        t_e1 = P.op("vector", lambda e, pa=pa, ms=ms: e.tensor_copy(
                            ms[:, 0:512], pa[:, 0:512]), waits=[t_ma, FREE(ms)], inc=s_ve)
                        t_e2 = P.op("vector", lambda e, pb=pb, ms=ms: e.tensor_copy(
                            ms[:, 512:1024], pb[:, 0:512]), waits=[t_mb], inc=s_ve)
                        SETFREE(pa, t_e1)
                        SETFREE(pb, t_e2)
                        t_st = P.op("sync", lambda e, r=r, dh=dh, ms=ms: e.dma_start(
                            d_mam[128 * r:128 * (r + 1), 1024 * dh:1024 * (dh + 1)], ms[:, :]),
                            waits=[t_e1, t_e2], inc=dsem(ms))
                        SETFREE(ms, t_st)
                        SETFREE(yc, (s_pe, s_pe.n))
                    SETFREE(woh, (s_pe, s_pe.n))
                P.emit(nc)
            M2S.close()


        if phases >= 5:
            with ExitStack() as S:
                e4 = S.enter_context
                zdc = ent(nc.sbuf_tensor("zdc", [128, NE], F32))
                ccol = e4(nc.sbuf_tensor("ccol", [128, NE], F32, side="right"))
                xnf = e4(nc.sbuf_tensor("xnf", [128, KD * DM], BF16, side="right"))
                muf = [e4(nc.sbuf_tensor(f"muf{i}", [128, DM], F32, side="right")) for i in range(2)]
                msq = [e4(nc.sbuf_tensor(f"msq{i}", [128, DM], F32, side="right")) for i in range(2)]
                mut = [e4(nc.sbuf_tensor(f"mut{i}", [128, DM], BF16, side="right")) for i in range(2)]
                exl = [e4(nc.sbuf_tensor(f"exl{i}", [128, DM], BF16, side="right")) for i in range(2)]
                csr = e4(nc.sbuf_tensor("csr", [1, NE * SL], F32, side="right"))
                zd4 = e4(nc.sbuf_tensor("zd4", [128, 4], F32, side="right"))

                P = Prog("B5a")
                guard(P)
                t_nm = {}
                for k in range(KD):
                    mf, mq = muf[k % 2], msq[k % 2]
                    t_l = P.op("sync", lambda e, k=k, mf=mf: e.dma_start(
                        mf[:, :], mu[128 * k:128 * (k + 1), :]), waits=[FREE(mf)], inc=dsem(mf))
                    t_s = P.op("scalar", lambda e, mf=mf, mq=mq: e.activation(
                        mq[:, :], mf[:, :], AF.Square), waits=[t_l, FREE(mq)], inc=s_se)
                    SETFREE(mf, t_s)
                    for ns in range(4):
                        t_nm[ns] = P.op("tensor", lambda e, ns=ns, k=k, mq=mq: e.matmul(
                            PS[4 + ns][0:1, 0:512],
                            ocf[:, :], mq[:, 512 * ns:512 * (ns + 1)],
                            start=(k == 0), stop=(k == KD - 1)),
                            waits=[t_s, FREE(PS[4 + ns]) if k == 0 else None],
                            inc=s_pe)
                    SETFREE(mq, (s_pe, s_pe.n))
                t_ev = []
                for ns in range(4):
                    t_ev.append(P.op("vector", lambda e, ns=ns: e.tensor_copy(
                        csr[0:1, 512 * ns:512 * (ns + 1)],
                        PS[4 + ns][0:1, 0:512]),
                        waits=[t_nm[ns], FREE(csr)], inc=s_ve))
                for ns in range(4):
                    SETFREE(PS[4 + ns], t_ev[ns])
                t_sq = P.op("scalar", lambda e: e.activation(
                    csr[0:1, :], csr[0:1, :], AF.Sqrt), waits=t_ev, inc=s_se)
                t_mx = P.op("vector", lambda e: e.tensor_scalar_max(
                    csr[0:1, :], csr[0:1, :], 1e-12), waits=[t_sq], inc=s_ve)
                t_rc = P.op("vector", lambda e: e.reciprocal(
                    csr[0:1, :], csr[0:1, :]), waits=[t_mx], inc=s_ve)
                SETFREE(csr, t_rc)
                for t in range(NE):
                    pq = PS[6 + t % 2]
                    t_t = P.op("tensor", lambda e, t=t, pq=pq: e.transpose(
                        pq[:, 0:1], csr[0:1, 128 * t:128 * (t + 1)], o1f[0:1, 0:1]),
                        waits=[t_rc, FREE(pq)], inc=s_pe)
                    t_c = P.op("vector", lambda e, t=t, pq=pq: e.tensor_copy(
                        ccol[:, t:t + 1], pq[:, 0:1]), waits=[t_t], inc=s_ve)
                    SETFREE(pq, t_c)
                t_ccol = (s_ve, s_ve.n)
                t_xnf = None
                for k in range(KD):
                    t_xnf = P.op("sync", lambda e, k=k: e.dma_start(
                        xnf[:, DM * k:DM * (k + 1)], d_xnorm[128 * k:128 * (k + 1), :]),
                        waits=[FREE(xnf) if k == 0 else None], inc=dsem(xnf))
                muap = mu.ap().rearrange("(k p) s -> p k s", p=128)
                for t in range(NE):
                    mt = mut[t % 2]
                    ex = exl[t % 2]
                    t_mu = P.op("gpsimd", lambda e, t=t, mt=mt: e.dma_start(
                        mt[:, :].rearrange("p (k s) -> p k s", k=KD),
                        muap[:, :, 128 * t:128 * (t + 1)]),
                        waits=[FREE(mt)], inc=dsem(mt, True))
                    t_es = []
                    for ns in range(4):
                        pp = PS[ns % 4]
                        t_mm = None
                        for k in range(KD):
                            t_mm = P.op("tensor", lambda e, ns=ns, k=k, pp=pp, mt=mt: e.matmul(
                                pp[:, 0:512], mt[:, 128 * k:128 * (k + 1)],
                                xnf[:, DM * k + 512 * ns:DM * k + 512 * (ns + 1)],
                                start=(k == 0), stop=(k == KD - 1)),
                                waits=[t_mu, t_xnf, FREE(pp)] if k == 0 else [],
                                inc=s_pe if k == KD - 1 else None)
                        t_e = P.op("scalar", lambda e, ns=ns, t=t, pp=pp, ex=ex: e.activation(
                            ex[:, 512 * ns:512 * (ns + 1)], pp[:, 0:512], AF.Exp,
                            scale=ccol[:, t:t + 1], accum_out=zd4[:, ns:ns + 1]),
                            waits=[t_mm, t_ccol, FREE(ex) if ns == 0 else None,
                                   FREE(zd4) if ns == 0 else None], inc=s_se)
                        SETFREE(pp, t_e)
                        t_es.append(t_e)
                    t_zd = P.op("vector", lambda e, t=t: e.reduce_sum(
                        zdc[:, t:t + 1], zd4[:, 0:4], axis=AX), waits=t_es, inc=s_ve)
                    SETFREE(zd4, t_zd)
                    t_st = P.op("sync", lambda e, t=t, ex=ex: e.dma_start(
                        d_expl[128 * t:128 * (t + 1), :], ex[:, :]), waits=t_es, inc=dsem(ex))
                    SETFREE(ex, t_st)
                    SETFREE(mt, (s_pe, s_pe.n))
                P.emit(nc)

        if phases >= 6:
            with ExitStack() as S:
                e5 = S.enter_context
                dtm = e5(nc.sbuf_tensor("dtm", [128, NT * DM], BF16, side="right"))
                exb = [e5(nc.sbuf_tensor(f"exb{i}", [128, DM], BF16, side="right")) for i in range(2)]
                zrep = e5(nc.sbuf_tensor("zrep", [128, 128], F32, side="right"))
                zrow = e5(nc.sbuf_tensor("zrow", [1, 128], F32, side="right"))
                zcolr = e5(nc.sbuf_tensor("zcolr", [128, 1], F32, side="right"))
                xlt = [e5(nc.sbuf_tensor(f"xlt{i}", [128, NT * 128], BF16, side="right")) for i in range(2)]
                sst = [e5(nc.sbuf_tensor(f"sst{i}", [128, 512], BF16, side="right")) for i in range(2)]

                P = Prog("B5b6")
                guard(P)
                for t in range(NE):
                    ex = exb[t % 2]
                    t_le = P.op("sync", lambda e, t=t, ex=ex: e.dma_start(
                        ex[:, :], d_expl[128 * t:128 * (t + 1), :]), waits=[FREE(ex)], inc=dsem(ex))
                    t_rc = P.op("vector", lambda e, t=t: e.reciprocal(
                        zcolr[:, 0:1], zdc[:, t:t + 1]), waits=[FREE(zcolr)], inc=s_ve)
                    t_tp = P.op("tensor", lambda e: e.transpose(
                        PS[6][0:1, 0:128], zcolr[:, 0:1], idf[:, :]),
                        waits=[t_rc, FREE(PS[6])], inc=s_pe)
                    t_zr = P.op("vector", lambda e: e.tensor_copy(
                        zrow[0:1, :], PS[6][0:1, 0:128]), waits=[t_tp, FREE(zrow)], inc=s_ve)
                    SETFREE(PS[6], t_zr)
                    SETFREE(zcolr, t_zr)
                    t_zb = P.op("tensor", lambda e: e.matmul(
                        PS[7][:, 0:128], o1f[0:1, :], zrow[0:1, :], start=True, stop=True),
                        waits=[t_zr, FREE(PS[7])], inc=s_pe)
                    t_ze = P.op("vector", lambda e: e.tensor_copy(
                        zrep[:, :], PS[7][:, 0:128]), waits=[t_zb, FREE(zrep)], inc=s_ve)
                    SETFREE(PS[7], t_ze)
                    SETFREE(zrow, t_zb)
                    last = None
                    for r in range(NT):
                        pq = PS[r % 2]
                        t_t = P.op("tensor", lambda e, r=r, pq=pq, ex=ex: e.transpose(
                            pq[:, 0:64].bitcast(BF16), ex[:, 128 * r:128 * (r + 1)], idb[:, :]),
                            waits=[t_le, FREE(pq)], inc=s_pe)
                        t_d = P.op("vector", lambda e, r=r, t=t, pq=pq: e.scalar_tensor_tensor(
                            dtm[:, DM * r + 128 * t:DM * r + 128 * (t + 1)],
                            pq[:, 0:64].bitcast(BF16), epsb[:, 2:3], zrep[:, :],
                            ALU.add, ALU.mult), waits=[t_t, t_ze], inc=s_ve)
                        SETFREE(pq, t_d)
                        last = t_d
                    SETFREE(ex, (s_pe, s_pe.n))
                    SETFREE(zrep, last)
                t_dtm = (s_ve, s_ve.n)
                xlap = d_xln.ap().rearrange("(r p) k -> p r k", p=128)
                for k in range(KD):
                    xt = xlt[k % 2]
                    t_lx = P.op("sync", lambda e, k=k, xt=xt: e.dma_start(
                        xt[:, :].rearrange("p (r c) -> p r c", r=NT),
                        xlap[:, :, 128 * k:128 * (k + 1)]),
                        waits=[FREE(xt)], inc=dsem(xt))
                    for ns in range(4):
                        pp = PS[2 + ns % 2]
                        t_mm = None
                        for r in range(NT):
                            t_mm = P.op("tensor", lambda e, ns=ns, r=r, pp=pp, xt=xt: e.matmul(
                                pp[:, 0:512], xt[:, 128 * r:128 * (r + 1)],
                                dtm[:, DM * r + 512 * ns:DM * r + 512 * (ns + 1)],
                                start=(r == 0), stop=(r == NT - 1)),
                                waits=[t_lx, t_dtm, FREE(pp)] if r == 0 else [],
                                inc=s_pe if r == NT - 1 else None)
                        ss = sst[ns % 2]
                        t_e = P.op("vector", lambda e, pp=pp, ss=ss: e.tensor_copy(
                            ss[:, :], pp[:, 0:512]), waits=[t_mm, FREE(ss)], inc=s_ve)
                        SETFREE(pp, t_e)
                        t_s = P.op("sync", lambda e, k=k, ns=ns, ss=ss: e.dma_start(
                            d_slotin[128 * k:128 * (k + 1), 512 * ns:512 * (ns + 1)], ss[:, :]),
                            waits=[t_e], inc=dsem(ss))
                        SETFREE(ss, t_s)
                    SETFREE(xt, (s_pe, s_pe.n))
                P.emit(nc)

        if phases >= 7:
            with ExitStack() as S:
                e6 = S.enter_context
                sos = e6(nc.sbuf_tensor("sos", [128, NE * DM], BF16, side="right"))
                w1t = [e6(nc.sbuf_tensor(f"w1t{i}", [128, KD * HH], BF16, side="right")) for i in range(2)]
                w2t = [e6(nc.sbuf_tensor("w2t0", [128, 4 * DM], BF16, side="right"))] * 2
                sit = [e6(nc.sbuf_tensor(f"sit{i}", [128, KD * 128], BF16, side="right")) for i in range(2)]
                b1r = [e6(nc.sbuf_tensor(f"b1r{i}", [1, HH], BF16, side="right")) for i in range(2)]
                b2r = [e6(nc.sbuf_tensor(f"b2r{i}", [1, DM], BF16, side="right")) for i in range(2)]
                hsm = e6(nc.sbuf_tensor("hsm", [128, HH], BF16, side="right"))
                hT = e6(nc.sbuf_tensor("hT", [128, HH], BF16, side="right"))

                P = Prog("B7")
                guard(P)
                w1ap = w1.ap().rearrange("e (k p) h -> e p k h", p=128)
                w2ap = w2.ap().rearrange("e (j p) d -> e p j d", p=128)
                siap = d_slotin.ap().rearrange("(k p) s -> p k s", p=128)
                for ei in range(NE):
                    wa, wb = w1t[ei % 2], w2t[ei % 2]
                    si = sit[ei % 2]
                    ba, bb = b1r[ei % 2], b2r[ei % 2]
                    t_w1 = P.op("gpsimd", lambda e, ei=ei, wa=wa: e.dma_start(
                        wa[:, :].rearrange("p (k h) -> p k h", k=KD), w1ap[ei]),
                        waits=[FREE(wa)], inc=dsem(wa, True))
                    t_w2 = P.op("gpsimd", lambda e, ei=ei, wb=wb: e.dma_start(
                        wb[:, :].rearrange("p (j d) -> p j d", j=4), w2ap[ei]),
                        waits=[FREE(wb)], inc=dsem(wb, True))
                    t_b1 = P.op("gpsimd", lambda e, ei=ei, ba=ba: e.dma_start(
                        ba[0:1, :], b1[ei:ei + 1, :]), waits=[FREE(ba)], inc=dsem(ba, True))
                    t_b2 = P.op("gpsimd", lambda e, ei=ei, bb=bb: e.dma_start(
                        bb[0:1, :], b2[ei:ei + 1, :]), waits=[FREE(bb)], inc=dsem(bb, True))
                    t_si = P.op("sync", lambda e, ei=ei, si=si: e.dma_start(
                        si[:, :].rearrange("p (k s) -> p k s", k=KD),
                        siap[:, :, 128 * ei:128 * (ei + 1)]),
                        waits=[FREE(si)], inc=dsem(si))
                    t_mm = None
                    for k in range(KD):
                        t_mm = P.op("tensor", lambda e, k=k, si=si, wa=wa: e.matmul(
                            PS[0][:, 0:512], si[:, 128 * k:128 * (k + 1)],
                            wa[:, HH * k:HH * (k + 1)], start=(k == 0), stop=False),
                            waits=[t_si, t_w1, FREE(PS[0])] if k == 0 else [], inc=None)
                    t_mm = P.op("tensor", lambda e, ba=ba: e.matmul(
                        PS[0][:, 0:512], o1b[0:1, :], ba[0:1, :], start=False, stop=True),
                        waits=[t_b1], inc=s_pe)
                    t_h = P.op("scalar", lambda e: e.activation(
                        hsm[:, :], PS[0][:, 0:512], AF.Gelu), waits=[t_mm, FREE(hsm)], inc=s_se)
                    SETFREE(PS[0], t_h)
                    t_hts = []
                    for j in range(4):
                        pq = PS[1 + j % 2]
                        t_t = P.op("tensor", lambda e, j=j, pq=pq: e.transpose(
                            pq[:, 0:64].bitcast(BF16), hsm[:, 128 * j:128 * (j + 1)], idb[:, :]),
                            waits=[t_h, FREE(pq)], inc=s_pe)
                        t_e = P.op("vector", lambda e, j=j, pq=pq: e.tensor_copy(
                            hT[:, 128 * j:128 * (j + 1)], pq[:, 0:64].bitcast(BF16)),
                            waits=[t_t, FREE(hT) if j == 0 else None], inc=s_ve)
                        SETFREE(pq, t_e)
                        t_hts.append(t_e)
                    SETFREE(hsm, s_pe.tok())
                    for dns in range(4):
                        pp = PS[3 + dns % 2]
                        t_mm = None
                        for j in range(4):
                            t_mm = P.op("tensor", lambda e, j=j, dns=dns, pp=pp, wb=wb: e.matmul(
                                pp[:, 0:512], hT[:, 128 * j:128 * (j + 1)],
                                wb[:, DM * j + 512 * dns:DM * j + 512 * (dns + 1)],
                                start=(j == 0), stop=False),
                                waits=[t_hts[j], t_w2, FREE(pp)] if j == 0 else [], inc=None)
                        t_mm = P.op("tensor", lambda e, dns=dns, pp=pp, bb=bb: e.matmul(
                            pp[:, 0:512], o1b[0:1, :], bb[0:1, 512 * dns:512 * (dns + 1)],
                            start=False, stop=True), waits=[t_b2], inc=s_pe)
                        t_e = P.op("vector", lambda e, ei=ei, dns=dns, pp=pp: e.tensor_copy(
                            sos[:, DM * ei + 512 * dns:DM * ei + 512 * (dns + 1)], pp[:, 0:512]),
                            waits=[t_mm], inc=s_ve)
                        SETFREE(pp, t_e)
                    SETFREE(hT, s_pe.tok())
                    SETFREE(si, s_pe.tok())
                    SETFREE(wa, s_pe.tok())
                    SETFREE(wb, s_pe.tok())
                    SETFREE(ba, s_pe.tok())
                    SETFREE(bb, s_pe.tok())
                t_sos = (s_ve, s_ve.n)

                exb2 = [e6(nc.sbuf_tensor(f"exb2{i}", [128, DM], BF16, side="right")) for i in range(2)]
                xfin = [e6(nc.sbuf_tensor(f"xfin{i}", [128, DM], F32, side="right")) for i in range(2)]
                xnb = [e6(nc.sbuf_tensor(f"xnb{i}", [128, DM], BF16, side="right")) for i in range(2)]
                mamb = [e6(nc.sbuf_tensor(f"mamb{i}", [128, DM], BF16, side="right")) for i in range(2)]
                obuf = [e6(nc.sbuf_tensor(f"obuf{i}", [128, DM], F32, side="right")) for i in range(2)]
                zcr = e6(nc.sbuf_tensor("zcr", [128, 1], F32, side="right"))

                exap = d_expl.ap().rearrange("(t p) n -> p t n", p=128)
                for r in range(NT):
                    ex = exb2[r % 2]
                    xf, xnl, mml, ob = xfin[r % 2], xnb[r % 2], mamb[r % 2], obuf[r % 2]
                    t_le = P.op("sync", lambda e, r=r, ex=ex: e.dma_start(
                        ex[:, :].rearrange("p (t n) -> p t n", t=NE),
                        exap[:, :, 128 * r:128 * (r + 1)]), waits=[FREE(ex)], inc=dsem(ex))
                    t_lx = P.op("sync", lambda e, r=r, xf=xf: e.dma_start(
                        xf[:, :], x_in[128 * r:128 * (r + 1), :]), waits=[FREE(xf)], inc=dsem(xf))
                    t_ln = P.op("sync", lambda e, r=r, xnl=xnl: e.dma_start(
                        xnl[:, :], d_xn[128 * r:128 * (r + 1), :]), waits=[FREE(xnl)], inc=dsem(xnl))
                    t_lm = P.op("sync", lambda e, r=r, mml=mml: e.dma_start(
                        mml[:, :], d_mam[128 * r:128 * (r + 1), :]), waits=[FREE(mml)], inc=dsem(mml))
                    t_md = {}
                    t_zc = None
                    for t in range(NE):
                        for dns in range(4):
                            t_md[dns] = P.op("tensor", lambda e, t=t, dns=dns, ex=ex: e.matmul(
                                PS[dns][:, 0:512], ex[:, 128 * t:128 * (t + 1)],
                                sos[:, DM * t + 512 * dns:DM * t + 512 * (dns + 1)],
                                start=(t == 0), stop=(t == NE - 1)),
                                waits=[t_le, t_sos, FREE(PS[dns])] if t == 0 else [],
                                inc=s_pe if t == NE - 1 else None)
                        t_zc = P.op("tensor", lambda e, t=t, ex=ex: e.matmul(
                            PS[4][:, 0:1], ex[:, 128 * t:128 * (t + 1)], ocb[:, 0:1],
                            start=(t == 0), stop=(t == NE - 1)),
                            waits=[FREE(PS[4])] if t == 0 else [],
                            inc=s_pe if t == NE - 1 else None)
                    t_rc = P.op("vector", lambda e: e.reciprocal(
                        zcr[:, 0:1], PS[4][:, 0:1]), waits=[t_zc, FREE(zcr)], inc=s_ve)
                    SETFREE(PS[4], t_rc)
                    t_a1 = P.op("gpsimd", lambda e, xf=xf, xnl=xnl: e.tensor_tensor(
                        xf[:, :], xf[:, :], xnl[:, :], ALU.add), waits=[t_lx, t_ln], inc=s_po)
                    t_a2 = P.op("gpsimd", lambda e, xf=xf, mml=mml: e.tensor_tensor(
                        xf[:, :], xf[:, :], mml[:, :], ALU.add), waits=[t_a1, t_lm], inc=s_po)
                    SETFREE(xnl, t_a2)
                    SETFREE(mml, t_a2)
                    t_os = []
                    for dns in range(4):
                        t_o = P.op("vector", lambda e, dns=dns, xf=xf, ob=ob: e.scalar_tensor_tensor(
                            ob[:, 512 * dns:512 * (dns + 1)], PS[dns][:, 0:512],
                            zcr[:, 0:1], xf[:, 512 * dns:512 * (dns + 1)],
                            ALU.mult, ALU.add),
                            waits=[t_md[dns], t_rc, t_a2, FREE(ob) if dns == 0 else None], inc=s_ve)
                        SETFREE(PS[dns], t_o)
                        t_os.append(t_o)
                    SETFREE(zcr, t_os[-1])
                    SETFREE(xf, t_os[-1])
                    SETFREE(ex, (s_pe, s_pe.n))
                    t_st = P.op("sync", lambda e, r=r, ob=ob: e.dma_start(
                        out[128 * r:128 * (r + 1), :], ob[:, :]), waits=t_os, inc=dsem(ob))
                    SETFREE(ob, t_st)
                P.emit(nc)

        if d_m1 is not None and 2 <= phases < 4:
            P = Prog("dbg_m1")
            sdbg2 = Sem(nc, "sdbg2", 16)
            P.op("sync", lambda e: e.dma_start(d_m1[:, 0:1024], dA[:, :]), inc=sdbg2)
            P.op("sync", lambda e: e.dma_start(d_m1[:, 1024:2048], eacs[:, :]), inc=sdbg2)
            P.op("sync", lambda e: e.dma_start(d_m1[:, 2048:3072], dec[:, :]), inc=sdbg2)
            P.op("sync", lambda e: e.dma_start(d_m1[:, 3072:5120], cdrep[:, :]), inc=sdbg2)
            sdbg3 = Sem(nc, "sdbg3", 16)
            P.op("gpsimd", lambda e: e.dma_start(d_bc[0:128, :], b_fm[:, :]), inc=sdbg3)
            P.op("gpsimd", lambda e: e.dma_start(d_bc[128:256, :], c_fm[:, :]), inc=sdbg3)
            P.op("sync", lambda e: e.nop(), waits=[sdbg2.tok(), sdbg3.tok()])
            P.emit(nc)

        if d_un is not None and 1 <= phases < 3:
            P = Prog("dbg_un")
            sdbg = Sem(nc, "sdbg", 16)
            for k in range(KD):
                P.op("sync", lambda e, k=k: e.dma_start(
                    d_un[128 * k:128 * (k + 1), :], un_fm[k][:, :]), inc=sdbg)
            P.op("sync", lambda e: e.nop(), waits=[sdbg.tok()])
            P.emit(nc)

        P = Prog("drain")
        for sm in list(_dma_sems.values()) + [s_ld, s_wc]:
            if sm.n > 0:
                P.op("sync", lambda e, sm=sm: e.nop(), waits=[sm.tok()])
        P.emit(nc)

        _ = (d_x, d_sz, d_yg, d_mam, d_slotin, d_expl, d_bc, mu, w1, b1, w2, b2,
             b_fm, c_fm, dA, eacs, dec, stm, u2, slow, tri2, half, ocb, o1b, idb,
             bnw_c, cnw_c, a_rep, dtb_rep, dd_rep, t_wc_in, t_wc_out, out, d_winb, d_woutb)

    return nc


def host_consts():
    idn = np.eye(128)
    u2 = np.zeros((128, 128), np.float32)
    u2[0:64, 0:64] = np.triu(np.ones((64, 64)))
    u2[64:128, 64:128] = u2[0:64, 0:64]
    sl_ = np.zeros((128, 128), np.float32)
    sl_[0:64, 0:64] = np.tril(np.ones((64, 64)), -1)
    sl_[64:128, 64:128] = sl_[0:64, 0:64]
    tri2 = np.zeros((128, 64), np.float32)
    tri2[0:64, :] = np.triu(np.ones((64, 64)))
    tri2[64:128, :] = tri2[0:64, :]
    half = np.zeros((128, 2), np.float32)
    half[0:64, 0] = 1.0
    half[64:128, 1] = 1.0
    return {
        "c_idf": idn.astype(np.float32),
        "c_idb": idn.astype(BF),
        "c_u2": u2,
        "c_slow": sl_,
        "c_tri2": tri2,
        "c_half": half,
        "c_o1f": np.ones((1, 128), np.float32),
        "c_o1b": np.ones((1, 128), BF),
        "c_ocb": np.ones((128, 1), BF),
        "c_ocf": np.ones((128, 1), np.float32),
        "c_eps": np.tile(np.pad(np.array([CONST_VALS], np.float32), ((0,0),(0,16-len(CONST_VALS)))), (128, 1)),
    }


def make_in_maps(inputs):
    cst = host_consts()
    f32c = lambda a: np.ascontiguousarray(np.asarray(a, np.float32))
    shared = dict(
        w_in=f32c(inputs["L0_attn_in_proj_w"]),
        w_out=f32c(inputs["L0_attn_out_proj_w"]),
        p_D=f32c(inputs["L0_attn_D"]).reshape(1, -1),
        p_dtb=f32c(inputs["L0_attn_dt_bias"]).reshape(1, -1),
        p_alog=f32c(inputs["L0_attn_A_log"]).reshape(1, -1),
        p_bnw=f32c(inputs["L0_attn_bnorm_w"]).reshape(1, -1),
        p_cnw=f32c(inputs["L0_attn_cnorm_w"]).reshape(1, -1),
        p_lnw=f32c(inputs["L0_attn_ln_w"]).reshape(1, -1),
        p_lnb=f32c(inputs["L0_attn_ln_b"]).reshape(1, -1),
        p_gam=f32c(inputs["L0_ffn_gamma"]).reshape(1, -1),
        p_bet=f32c(inputs["L0_ffn_beta"]).reshape(1, -1),
        p_scale=f32c(inputs["L0_ffn_scale"]).reshape(1, 1),
        mu=f32c(inputs["L0_ffn_mu"]).reshape(DM, NE * SL),
        w1=f32c(inputs["L0_ffn_W1"]),
        b1=f32c(inputs["L0_ffn_b1"]),
        w2=f32c(inputs["L0_ffn_W2"]),
        b2=f32c(inputs["L0_ffn_b2"]),
        **cst,
    )
    x = np.asarray(inputs["L0_attn_u"], np.float32)
    return [dict(shared, x_in=np.ascontiguousarray(x[c])) for c in range(NCORES)]


def kernel(**inputs):
    nc = build()
    in_maps = make_in_maps(inputs)
    res = run_bass_kernel_spmd(nc, in_maps, core_ids=list(range(NCORES)))
    return np.stack([np.asarray(r["out"], np.float32) for r in res.results], axis=0)


# version 11 (final)
# speedup vs baseline: 1.2417x; 1.0615x over previous
import numpy as np
import ml_dtypes
import concourse.bass as bass
import concourse.mybir as mybir
from concourse.bass_utils import run_bass_kernel_spmd
from contextlib import ExitStack

F32 = mybir.dt.float32
BF16 = mybir.dt.bfloat16
AF = mybir.ActivationFunctionType
ALU = mybir.AluOpType
AX = mybir.AxisListType.X
BF = ml_dtypes.bfloat16

NCORES = 8
NTOK = 2048
DM = 2048
EIN = 4096
NH = 64
HD = 64
DS = 128
CH = 64
NT = NTOK // 128
KD = DM // 128
KE = EIN // 128
EPROJ = 8512
NE = 16
SL = 128
HH = 512

ENGINES = ("tensor", "vector", "scalar", "gpsimd", "sync")
CONST_VALS = (1e-5, 1e-12, 0.0, 1.0, -1.0, 0.5, -0.5, 2.0, 0.0078125)


class Sem:
    __slots__ = ("h", "n", "step", "_cm")

    def __init__(self, nc, name, step=1):
        self._cm = nc.semaphore(name)
        self.h = self._cm.__enter__()
        self.n = 0
        self.step = step

    def inc(self, k=None):
        self.n += self.step if k is None else k
        return self.n

    def tok(self):
        return (self, self.n)


class Prog:

    def __init__(self, name=""):
        self.name = name
        self.items = []

    def op(self, eng, fn, waits=(), inc=None, inck=None):
        ws = []

        def add(w):
            if w is None:
                return
            if isinstance(w, list):
                for x in w:
                    add(x)
                return
            sem, tk = w
            if tk > 0:
                ws.append((sem, tk))

        for w in waits:
            add(w)
        incs = []
        tok = None
        if inc is not None:
            inc.inc(inck)
            tok = (inc, inc.n)
            incs = [(inc, inc.step if inck is None else inck)]
        self.items.append((eng, fn, tuple(ws), incs))
        return tok

    def emit(self, nc):
        by_eng = {}
        for it in self.items:
            by_eng.setdefault(it[0], []).append(it)
        with nc.Block() as b:
            for eng_name in ENGINES:
                items = by_eng.get(eng_name)
                if not items:
                    continue

                def body(e, items=items):
                    waited = {}
                    for engn, fn, waits, incs in items:
                        for (sem, tk) in waits:
                            if waited.get(id(sem), 0) < tk:
                                e.wait_ge(sem.h, tk)
                                waited[id(sem)] = tk
                        instr = fn(e)
                        for (sem, k) in incs:
                            instr.then_inc(sem.h, k)

                getattr(b, eng_name)(body)


def build(dbg=(), phases=99, m1_parts=("zx", "bc", "dt"), dt_stop=9):
    nc = bass.Bass("TRN2", target_bir_lowering=False)
    dkind = lambda n: "ExternalOutput" if n in dbg else "Internal"

    x_in = nc.dram_tensor("x_in", [NTOK, DM], F32, kind="ExternalInput")
    w_in = nc.dram_tensor("w_in", [EPROJ, DM], F32, kind="ExternalInput")
    w_out = nc.dram_tensor("w_out", [DM, EIN], F32, kind="ExternalInput")
    p_D = nc.dram_tensor("p_D", [1, NH], F32, kind="ExternalInput")
    p_dtb = nc.dram_tensor("p_dtb", [1, NH], F32, kind="ExternalInput")
    p_alog = nc.dram_tensor("p_alog", [1, NH], F32, kind="ExternalInput")
    p_bnw = nc.dram_tensor("p_bnw", [1, DS], F32, kind="ExternalInput")
    p_cnw = nc.dram_tensor("p_cnw", [1, DS], F32, kind="ExternalInput")
    p_lnw = nc.dram_tensor("p_lnw", [1, DM], F32, kind="ExternalInput")
    p_lnb = nc.dram_tensor("p_lnb", [1, DM], F32, kind="ExternalInput")
    p_gam = nc.dram_tensor("p_gam", [1, DM], F32, kind="ExternalInput")
    p_bet = nc.dram_tensor("p_bet", [1, DM], F32, kind="ExternalInput")
    p_scale = nc.dram_tensor("p_scale", [1, 1], F32, kind="ExternalInput")
    mu = nc.dram_tensor("mu", [DM, NE * SL], F32, kind="ExternalInput")
    w1 = nc.dram_tensor("w1", [NE, DM, HH], F32, kind="ExternalInput")
    b1 = nc.dram_tensor("b1", [NE, HH], F32, kind="ExternalInput")
    w2 = nc.dram_tensor("w2", [NE, HH, DM], F32, kind="ExternalInput")
    b2 = nc.dram_tensor("b2", [NE, DM], F32, kind="ExternalInput")

    c_idf = nc.dram_tensor("c_idf", [128, 128], F32, kind="ExternalInput")
    c_idb = nc.dram_tensor("c_idb", [128, 128], BF16, kind="ExternalInput")
    c_u2 = nc.dram_tensor("c_u2", [128, 128], F32, kind="ExternalInput")
    c_slow = nc.dram_tensor("c_slow", [128, 128], F32, kind="ExternalInput")
    c_slowb = nc.dram_tensor("c_slowb", [128, 128], BF16, kind="ExternalInput")
    c_tri2 = nc.dram_tensor("c_tri2", [128, CH], F32, kind="ExternalInput")
    c_half = nc.dram_tensor("c_half", [128, 2], F32, kind="ExternalInput")
    c_o1f = nc.dram_tensor("c_o1f", [1, 128], F32, kind="ExternalInput")
    c_o1b = nc.dram_tensor("c_o1b", [1, 128], BF16, kind="ExternalInput")
    c_ocb = nc.dram_tensor("c_ocb", [128, 1], BF16, kind="ExternalInput")
    c_ocf = nc.dram_tensor("c_ocf", [128, 1], F32, kind="ExternalInput")
    c_eps = nc.dram_tensor("c_eps", [128, 16], F32, kind="ExternalInput")

    out = nc.dram_tensor("out", [NTOK, DM], F32, kind="ExternalOutput")

    d_xn = nc.dram_tensor("d_xn", [NTOK, DM], BF16, kind=dkind("d_xn"))
    d_xln = nc.dram_tensor("d_xln", [NTOK, DM], BF16, kind=dkind("d_xln"))
    d_xnorm = nc.dram_tensor("d_xnorm", [DM, NTOK], BF16, kind=dkind("d_xnorm"))
    d_winb = nc.dram_tensor("d_winb", [EPROJ, DM], BF16, kind=dkind("d_winb"))
    d_woutb = nc.dram_tensor("d_woutb", [DM, EIN], BF16, kind=dkind("d_woutb"))
    d_x = nc.dram_tensor("d_x", [NTOK, EIN], BF16, kind=dkind("d_x"))
    d_sz = nc.dram_tensor("d_sz", [NTOK, EIN], BF16, kind=dkind("d_sz"))
    d_yg = nc.dram_tensor("d_yg", [EIN, NTOK], BF16, kind=dkind("d_yg"))
    d_mam = nc.dram_tensor("d_mam", [NTOK, DM], BF16, kind=dkind("d_mam"))
    d_slotin = nc.dram_tensor("d_slotin", [DM, NE * SL], BF16, kind=dkind("d_slotin"))
    d_w1b = nc.dram_tensor("d_w1b", [NE, DM, HH], BF16, kind="Internal")
    d_w2b = nc.dram_tensor("d_w2b", [NE, HH, DM], BF16, kind="Internal")
    d_expl = nc.dram_tensor("d_expl", [NE * SL, NTOK], BF16, kind=dkind("d_expl"))
    d_bc = nc.dram_tensor("d_bc", [2 * DS, NTOK], BF16, kind="ExternalOutput" if ("d_bc" in dbg or "d_m1" in dbg) else "Internal")
    d_un = nc.dram_tensor("d_un", [DM, NTOK], BF16, kind=dkind("d_un")) if "d_un" in dbg else None
    d_m1 = nc.dram_tensor("d_m1", [128, 5 * 1024 + 2048], F32, kind="ExternalOutput") if "d_m1" in dbg else None
    d_m2 = nc.dram_tensor("d_m2", [128, 3 * EIN + CH * NT], F32, kind="ExternalOutput") if "d_m2" in dbg else None

    with ExitStack() as TOP:
        ent = TOP.enter_context

        s_ld = Sem(nc, "s_ld", 16)
        s_wc = Sem(nc, "s_wc", 16)
        s_wo = Sem(nc, "s_wo", 16)
        s_pe = Sem(nc, "s_pe")
        s_ve = Sem(nc, "s_ve")
        s_se = Sem(nc, "s_se")
        s_po = Sem(nc, "s_po")
        _dma_sems = {}

        def dsem(buf, sw=False):
            k = (buf if isinstance(buf, str) else id(buf), sw)
            if k not in _dma_sems:
                _dma_sems[k] = Sem(nc, f"sd{len(_dma_sems)}", 16)
            return _dma_sems[k]

        idf = ent(nc.sbuf_tensor("idf", [128, 128], F32))
        idb = ent(nc.sbuf_tensor("idb", [128, 128], BF16))
        u2 = ent(nc.sbuf_tensor("u2", [128, 128], F32))
        slow = ent(nc.sbuf_tensor("slow", [128, 128], F32))
        slowb = ent(nc.sbuf_tensor("slowb", [128, 128], BF16))
        tri2 = ent(nc.sbuf_tensor("tri2", [128, CH], F32))
        half = ent(nc.sbuf_tensor("half", [128, 2], F32))
        o1f = ent(nc.sbuf_tensor("o1f", [1, 128], F32))
        o1b = ent(nc.sbuf_tensor("o1b", [1, 128], BF16))
        ocb = ent(nc.sbuf_tensor("ocb", [128, 1], BF16))
        ocf = ent(nc.sbuf_tensor("ocf", [128, 1], F32))
        epsb = ent(nc.sbuf_tensor("epsb", [128, 16], F32))
        lnw_c = ent(nc.sbuf_tensor("lnw_c", [128, KD], F32))
        lnb_c = ent(nc.sbuf_tensor("lnb_c", [128, KD], F32))
        gam_c = ent(nc.sbuf_tensor("gam_c", [128, KD], F32))
        bet_c = ent(nc.sbuf_tensor("bet_c", [128, KD], F32))
        bnw_c = ent(nc.sbuf_tensor("bnw_c", [128, 1], F32))
        cnw_c = ent(nc.sbuf_tensor("cnw_c", [128, 1], F32))
        sc_rep = ent(nc.sbuf_tensor("sc_rep", [128, 1], F32))
        a_rep = ent(nc.sbuf_tensor("a_rep", [128, NH], F32))
        dtb_rep = ent(nc.sbuf_tensor("dtb_rep", [128, NH], F32))
        dd_rep = ent(nc.sbuf_tensor("dd_rep", [128, NH], F32))
        M2S = ExitStack()
        b_fm = M2S.enter_context(nc.sbuf_tensor("b_fm", [128, NTOK], BF16))
        c_fm = M2S.enter_context(nc.sbuf_tensor("c_fm", [128, NTOK], BF16))
        dA = M2S.enter_context(nc.sbuf_tensor("dA", [128, NH * NT], F32))
        eacs = M2S.enter_context(nc.sbuf_tensor("eacs", [128, NH * NT], F32))
        dec = M2S.enter_context(nc.sbuf_tensor("dec", [128, NH * NT], F32))
        stm = M2S.enter_context(nc.sbuf_tensor("stm", [128, CH * NT], BF16))
        cdrep = M2S.enter_context(nc.sbuf_tensor("cdrep", [128, 2 * NH * NT], F32))

        PS = [ent(nc.psum_tensor(f"ps{i}", [128, 512], F32)) for i in range(8)]

        UNS = ExitStack()
        un_fm = [UNS.enter_context(nc.sbuf_tensor(f"un{k}", [128, NTOK], BF16)) for k in range(KD)]

        free = {}

        def FREE(buf):
            v = free.get(id(buf), [])
            return v if isinstance(v, list) else [v]

        def SETFREE(buf, *toks):
            free[id(buf)] = [t for t in toks if t is not None]

        def guard(P):
            toks = [sm.tok() for sm in _dma_sems.values() if sm.n > 0]
            for eng in ENGINES:
                P.op(eng, lambda e: e.nop(), waits=toks)

        def rep_fill(P, row_dram, width, dst, buf, ps_base, dep0, negate=False, do_exp=False):
            t_ld = P.op("sync", lambda e, b_=buf, rd=row_dram, w=width: e.dma_start(
                b_[0:1, 0:w], rd[0:1, 0:w]), waits=[FREE(buf), dep0], inc=dsem(buf))
            dep = t_ld
            if do_exp:
                dep = P.op("scalar", lambda e, b_=buf, w=width: e.activation(
                    b_[0:1, 0:w], b_[0:1, 0:w], AF.Exp), waits=[t_ld], inc=s_se)
            last = None
            for j in range((width + 511) // 512):
                w = min(512, width - j * 512)
                pk = PS[ps_base + (j % 2)]
                tt = P.op("tensor", lambda e, j=j, w=w, pk=pk, b_=buf: e.matmul(
                    pk[:, 0:w], o1f[0:1, :], b_[0:1, 512 * j:512 * j + w],
                    start=True, stop=True), waits=[dep, FREE(pk)], inc=s_pe)
                if negate:
                    tv = P.op("vector", lambda e, j=j, w=w, pk=pk: e.tensor_scalar_mul(
                        dst[:, 512 * j:512 * j + w], pk[:, 0:w], -1.0), waits=[tt], inc=s_ve)
                else:
                    tv = P.op("vector", lambda e, j=j, w=w, pk=pk: e.tensor_copy(
                        dst[:, 512 * j:512 * j + w], pk[:, 0:w]), waits=[tt], inc=s_ve)
                SETFREE(pk, tv)
                last = tv
            SETFREE(buf, last)
            return last

        with ExitStack() as S:
            rowbuf = S.enter_context(nc.sbuf_tensor("rowbuf", [1, DM], F32, side="right"))
            rowbuf2 = S.enter_context(nc.sbuf_tensor("rowbuf2", [1, DM], F32, side="right"))
            P = Prog("B0")
            for dst, src in [(idf, c_idf), (idb, c_idb), (u2, c_u2), (slow, c_slow), (slowb, c_slowb),
                             (tri2, c_tri2), (half, c_half), (o1f, c_o1f),
                             (o1b, c_o1b), (ocb, c_ocb), (ocf, c_ocf), (epsb, c_eps)]:
                P.op("sync", lambda e, d=dst, s=src: e.dma_start(d[:, :], s[:, :]), inc=s_ld)
            t_consts = s_ld.tok()
            for ci, cv in enumerate(CONST_VALS):
                nc.const_aps.register(F32, cv, epsb[:, ci:ci + 1])

            for i in range(8):
                r0, r1 = i * 1064, (i + 1) * 1064
                P.op("gpsimd", lambda e, r0=r0, r1=r1: e.dma_start(
                    d_winb[r0:r1, :], w_in[r0:r1, :]), inc=s_wc)
            t_wc_in = s_wc.tok()
            s_w12 = Sem(nc, "s_w12", 16)
            for i in range(4):
                r0, r1 = i * 512, (i + 1) * 512
                P.op("gpsimd", lambda e, r0=r0, r1=r1: e.dma_start(
                    d_woutb[r0:r1, :], w_out[r0:r1, :]), inc=s_wo)
            t_wc_out = s_wo.tok()

            def col_fill(row_dram, ncols, dst_col, buf, ps_base):
                t_ld = P.op("sync", lambda e, b_=buf, rd=row_dram, n=ncols: e.dma_start(
                    b_[0:1, 0:n * 128], rd[0:1, 0:n * 128]),
                    waits=[FREE(buf), t_consts], inc=dsem(buf))
                last = None
                for k in range(ncols):
                    pk = PS[ps_base + (k % 2)]
                    tt = P.op("tensor", lambda e, k=k, pk=pk, b_=buf: e.transpose(
                        pk[:, 0:1], b_[0:1, 128 * k:128 * (k + 1)], o1f[0:1, 0:1]),
                        waits=[t_ld, FREE(pk)], inc=s_pe)
                    tv = P.op("vector", lambda e, k=k, pk=pk: e.tensor_copy(
                        dst_col[:, k:k + 1], pk[:, 0:1]), waits=[tt], inc=s_ve)
                    SETFREE(pk, tv)
                    last = tv
                SETFREE(buf, last)

            col_fill(p_lnw, KD, lnw_c, rowbuf, 0)
            col_fill(p_lnb, KD, lnb_c, rowbuf2, 2)
            col_fill(p_gam, KD, gam_c, rowbuf, 0)
            col_fill(p_bet, KD, bet_c, rowbuf2, 2)
            col_fill(p_bnw, 1, bnw_c, rowbuf, 0)
            col_fill(p_cnw, 1, cnw_c, rowbuf2, 2)

            rep_fill(P, p_scale, 1, sc_rep, rowbuf, 4, t_consts)
            rep_fill(P, p_dtb, NH, dtb_rep, rowbuf2, 4, t_consts)
            rep_fill(P, p_D, NH, dd_rep, rowbuf, 4, t_consts)
            rep_fill(P, p_alog, NH, a_rep, rowbuf2, 4, t_consts, negate=True, do_exp=True)
            P.emit(nc)

        if phases >= 1:
            with ExitStack() as S:
                e0 = S.enter_context
                gam_rep = e0(nc.sbuf_tensor("gam_rep", [128, DM], F32, side="right"))
                bet_rep = e0(nc.sbuf_tensor("bet_rep", [128, DM], F32, side="right"))
                rowb = e0(nc.sbuf_tensor("rowb", [1, DM], F32, side="right"))
                xbuf = [e0(nc.sbuf_tensor(f"xb{i}", [128, DM], F32, side="right")) for i in range(2)]
                xn32 = [e0(nc.sbuf_tensor(f"xn32{i}", [128, DM], F32, side="right")) for i in range(2)]
                z232 = [e0(nc.sbuf_tensor(f"z232{i}", [128, DM], F32, side="right")) for i in range(2)]
                xl32 = [e0(nc.sbuf_tensor(f"xl32{i}", [128, DM], F32, side="right")) for i in range(2)]
                xstg = [e0(nc.sbuf_tensor(f"xstg{i}", [128, DM], BF16, side="right")) for i in range(2)]
                ustg = [e0(nc.sbuf_tensor(f"ustg{i}", [128, 128], F32, side="right")) for i in range(2)]
                l2rows = [e0(nc.sbuf_tensor(f"l2row{i}", [1, 128], F32, side="right")) for i in range(2)]
                st1s = [e0(nc.sbuf_tensor(f"st1_{i}", [128, 8], F32, side="right")) for i in range(2)]
                st2s = [e0(nc.sbuf_tensor(f"st2_{i}", [128, 8], F32, side="right")) for i in range(2)]
                stCs = [e0(nc.sbuf_tensor(f"stC_{i}", [128, 8], F32, side="right")) for i in range(2)]

                P = Prog("P0")
                rep_fill(P, p_gam, DM, gam_rep, rowb, 4, None)
                rep_fill(P, p_bet, DM, bet_rep, rowb, 4, None)

                t_l2prev = None
                for r in range(NT):
                    bi = r % 2
                    xb, xn_t, z2_t, xl_t = xbuf[bi], xn32[bi], z232[bi], xl32[bi]
                    xst = xstg[bi]
                    st1, st2, l2row = st1s[bi], st2s[bi], l2rows[bi]
                    stC = stCs[bi]

                    t_ld = P.op("sync", lambda e, r=r, xb=xb: e.dma_start(
                        xb[:, :], x_in[128 * r:128 * (r + 1), :]), waits=[FREE(xb)], inc=dsem(xb))
                    t_s1 = P.op("vector", lambda e, xb=xb: e.reduce_sum(
                        st1[:, 0:1], xb[:, :], axis=AX), waits=[t_ld, FREE(st1)], inc=s_ve)
                    t_s2 = P.op("scalar", lambda e, xb=xb, xn_t=xn_t: e.activation(
                        xn_t[:, :], xb[:, :], AF.Square, accum_out=st1[:, 1:2]),
                        waits=[t_ld, FREE(xn_t), t_s1], inc=s_se)
                    t_m = P.op("vector", lambda e: e.tensor_scalar_mul(
                        st1[:, 2:3], st1[:, 0:1], 1.0 / DM), waits=[t_s1, t_s2], inc=s_ve)
                    t_v = P.op("vector", lambda e: e.tensor_scalar_mul(
                        st1[:, 3:4], st1[:, 1:2], 1.0 / DM), waits=[t_m], inc=s_ve)
                    t_msq = P.op("vector", lambda e: e.tensor_tensor(
                        st1[:, 4:5], st1[:, 2:3], st1[:, 2:3], ALU.mult), waits=[t_v], inc=s_ve)
                    t_v2 = P.op("vector", lambda e: e.tensor_tensor(
                        st1[:, 5:6], st1[:, 3:4], st1[:, 4:5], ALU.subtract), waits=[t_msq], inc=s_ve)
                    t_lnv = P.op("scalar", lambda e: e.activation(
                        st1[:, 6:7], st1[:, 5:6], AF.Ln, bias=1e-5), waits=[t_v2], inc=s_se)
                    t_rs = P.op("scalar", lambda e: e.activation(
                        st1[:, 7:8], st1[:, 6:7], AF.Exp, scale=-0.5), waits=[t_lnv], inc=s_se)
                    t_xn = P.op("vector", lambda e, xb=xb, xn_t=xn_t: e.tensor_scalar(
                        xn_t[:, :], xb[:, :], st1[:, 2:3], st1[:, 7:8], ALU.subtract, ALU.mult),
                        waits=[t_rs], inc=s_ve)
                    t_st1 = P.op("gpsimd", lambda e, r=r, xn_t=xn_t: e.dma_start(
                        d_xn[128 * r:128 * (r + 1), :], xn_t[:, :]), waits=[t_xn], inc=dsem(xn_t, True))
                    t2_w = P.op("vector", lambda e: e.tensor_scalar_add(
                        st2[:, 0:1], st1[:, 5:6], 1e-5), waits=[t_rs, FREE(st2)], inc=s_ve)
                    t2_rw = P.op("vector", lambda e: e.reciprocal(
                        st2[:, 1:2], st2[:, 0:1]), waits=[t2_w], inc=s_ve)
                    t2_v2 = P.op("vector", lambda e: e.tensor_tensor(
                        st2[:, 2:3], st1[:, 5:6], st2[:, 1:2], ALU.mult), waits=[t2_rw], inc=s_ve)
                    t2_lnv = P.op("scalar", lambda e: e.activation(
                        st2[:, 3:4], st2[:, 2:3], AF.Ln, bias=1e-5), waits=[t2_v2], inc=s_se)
                    t2_rs = P.op("scalar", lambda e: e.activation(
                        st2[:, 4:5], st2[:, 3:4], AF.Exp, scale=-0.5), waits=[t2_lnv], inc=s_se)
                    t_z2 = P.op("vector", lambda e, xn_t=xn_t, z2_t=z2_t: e.tensor_scalar_mul(
                        z2_t[:, :], xn_t[:, :], st2[:, 4:5]),
                        waits=[t2_rs, FREE(z2_t)], inc=s_ve)
                    SETFREE(xn_t, t_z2, t_st1)
                    SETFREE(st2, t_z2)
                    t_g1 = P.op("gpsimd", lambda e, z2_t=z2_t, xl_t=xl_t: e.tensor_tensor(
                        xl_t[:, :], z2_t[:, :], gam_rep[:, :], ALU.mult),
                        waits=[t_z2, FREE(xl_t)], inc=s_po)
                    t_g2 = P.op("gpsimd", lambda e, xl_t=xl_t: e.tensor_tensor(
                        xl_t[:, :], xl_t[:, :], bet_rep[:, :], ALU.add), waits=[t_g1], inc=s_po)
                    t_st2 = P.op("gpsimd", lambda e, r=r, xl_t=xl_t: e.dma_start(
                        d_xln[128 * r:128 * (r + 1), :], xl_t[:, :]), waits=[t_g2], inc=dsem(xl_t, True))
                    t_ss = P.op("scalar", lambda e, xl_t=xl_t, xb=xb, stC=stC: e.activation(
                        xb[:, :], xl_t[:, :], AF.Square, accum_out=stC[:, 0:1]),
                        waits=[t_g2, t_xn, t2_rs, t_l2prev, FREE(stC)], inc=s_se)
                    SETFREE(xb, t_ss)
                    t_sq = P.op("scalar", lambda e, stC=stC: e.activation(
                        stC[:, 1:2], stC[:, 0:1], AF.Sqrt), waits=[t_ss], inc=s_se)
                    t_mx = P.op("vector", lambda e, stC=stC: e.tensor_scalar_max(
                        stC[:, 2:3], stC[:, 1:2], 1e-12), waits=[t_sq], inc=s_ve)
                    t_rc = P.op("vector", lambda e, stC=stC: e.reciprocal(
                        stC[:, 3:4], stC[:, 2:3]), waits=[t_mx], inc=s_ve)
                    t_l2 = P.op("vector", lambda e, stC=stC: e.tensor_tensor(
                        stC[:, 4:5], stC[:, 3:4], sc_rep[:, 0:1], ALU.mult), waits=[t_rc], inc=s_ve)
                    t_l2prev = t_l2
                    SETFREE(st1, (s_ve, s_ve.n), (s_se, s_se.n))
                    pL = PS[4 + bi]
                    pR = PS[6 + bi]
                    t_tr = P.op("tensor", lambda e, stC=stC, pL=pL: e.transpose(
                        pL[0:1, 0:128], stC[:, 4:5], idf[:, :]),
                        waits=[t_l2, FREE(pL)], inc=s_pe)
                    t_lr = P.op("vector", lambda e, l2row=l2row, pL=pL: e.tensor_copy(
                        l2row[0:1, :], pL[0:1, 0:128]), waits=[t_tr, FREE(l2row)], inc=s_ve)
                    SETFREE(pL, t_lr)
                    SETFREE(stC, t_l2, t_tr)
                    t_repm = P.op("tensor", lambda e, l2row=l2row, pR=pR: e.matmul(
                        pR[:, 0:128], o1f[0:1, :], l2row[0:1, :], start=True, stop=True),
                        waits=[t_lr, FREE(pR)], inc=s_pe)
                    SETFREE(l2row, t_repm)
                    last_xst = None
                    for k in range(KD):
                        pf = PS[k % 4]
                        ust = ustg[k % 2]
                        tt = P.op("tensor", lambda e, k=k, pf=pf, z2_t=z2_t: e.transpose(
                            pf[:, 0:128], z2_t[:, 128 * k:128 * (k + 1)], idf[:, :]),
                            waits=[t_z2, FREE(pf)], inc=s_pe)
                        tu = P.op("vector", lambda e, k=k, r=r, pf=pf: e.tensor_scalar(
                            un_fm[k][:, 128 * r:128 * (r + 1)], pf[:, 0:128],
                            lnw_c[:, k:k + 1], lnb_c[:, k:k + 1], ALU.mult, ALU.add),
                            waits=[tt], inc=s_ve)
                        tx1 = P.op("vector", lambda e, k=k, pf=pf, ust=ust: e.tensor_scalar(
                            ust[:, :], pf[:, 0:128], gam_c[:, k:k + 1], bet_c[:, k:k + 1],
                            ALU.mult, ALU.add), waits=[tu, FREE(ust)], inc=s_ve)
                        SETFREE(pf, tx1)
                        tx2 = P.op("vector", lambda e, k=k, ust=ust, xst=xst, pR=pR: e.tensor_tensor(
                            xst[:, 128 * k:128 * (k + 1)], ust[:, :], pR[:, 0:128], ALU.mult),
                            waits=[tx1, t_repm, FREE(xst) if k == 0 else None], inc=s_ve)
                        SETFREE(ust, tx2)
                        last_xst = tx2
                    SETFREE(pR, last_xst)
                    SETFREE(z2_t, s_pe.tok(), t_g1)
                    dstap = d_xnorm.ap().rearrange("(k p) n -> p k n", p=128)
                    t_st3 = P.op("sync", lambda e, r=r, xst=xst, dstap=dstap: e.dma_start(
                        dstap[:, :, 128 * r:128 * (r + 1)],
                        xst[:, :].rearrange("p (k n) -> p k n", k=KD)),
                        waits=[last_xst], inc=dsem(xst))
                    SETFREE(xst, t_st3)
                    SETFREE(xl_t, t_ss, t_st2)
                P.emit(nc)


        if phases >= 2:
            with ExitStack() as S:
                e1 = S.enter_context
                wslot = [e1(nc.sbuf_tensor(f"ws{i}", [128, 16 * 512], BF16, side="right")) for i in range(2)]
                ostg = [e1(nc.sbuf_tensor(f"og{i}", [128, 512], BF16, side="right")) for i in range(4)]
                bsq = e1(nc.sbuf_tensor("bsq", [128, NTOK], F32, side="right"))
                ssrow = e1(nc.sbuf_tensor("ssrow", [1, NTOK], F32, side="right"))
                wdtn = e1(nc.sbuf_tensor("wdtn", [64, DM], BF16, side="right"))
                wdt = e1(nc.sbuf_tensor("wdt", [128, 16 * 64], BF16, side="right"))
                dtt = [e1(nc.sbuf_tensor(f"dtt{i}", [128, 64], F32, side="right")) for i in range(4)]
                acsb = e1(nc.sbuf_tensor("acsb", [128, 64], F32, side="right"))
                rowc = e1(nc.sbuf_tensor("rowc", [1, 128], F32, side="right"))
                rowcd = e1(nc.sbuf_tensor("rowcd", [1, 128], F32, side="right"))

                P = Prog("M1")
                guard(P)

                def load_wslot(ws, e0, width):
                    toks = None
                    nk = KD
                    for k in range(nk):
                        toks = P.op("sync", lambda e, ws=ws, e0=e0, k=k, width=width: e.dma_start(
                            ws[:, width * k:width * (k + 1)],
                            d_winb[e0:e0 + width, 128 * k:128 * (k + 1)], transpose=True),
                            waits=[FREE(ws) if k == 0 else None,
                                   (s_wc, s_wc.n) if k == 0 else None], inc=dsem(ws))
                    return toks

                do_dt = "dt" in m1_parts
                t_dn = P.op("gpsimd", lambda e: e.dma_start(
                    wdtn[:, :], w_in[2 * EIN + 2 * DS:EPROJ, :]), inc=dsem(wdtn, True)) if do_dt else None
                t_tp = None
                for k in (range(KD) if do_dt else []):
                    pq = PS[4 + k % 2]
                    t_t = P.op("tensor", lambda e, k=k, pq=pq: e.transpose(
                        pq[:, 0:32].bitcast(BF16),
                        wdtn[0:64, 128 * k:128 * (k + 1)], idb[0:64, 0:64]),
                        waits=[t_dn, FREE(pq)], inc=s_pe)
                    t_tp = P.op("vector", lambda e, k=k, pq=pq: e.tensor_copy(
                        wdt[:, 64 * k:64 * (k + 1)], pq[:, 0:32].bitcast(BF16)), waits=[t_t], inc=s_ve)
                    SETFREE(pq, t_tp)
                for r in (range(NT) if do_dt else []):
                    pp = PS[r % 4]
                    t_mm = None
                    for k in range(KD):
                        t_mm = P.op("tensor", lambda e, pp=pp, k=k, r=r: e.matmul(
                            pp[:, 0:64], un_fm[k][:, 128 * r:128 * (r + 1)],
                            wdt[:, 64 * k:64 * (k + 1)], start=(k == 0), stop=(k == KD - 1)),
                            waits=[(s_ve, s_ve.n), FREE(pp)] if k == 0 else [],
                            inc=s_pe if k == KD - 1 else None)
                    t_d0 = P.op("vector", lambda e, pp=pp: e.tensor_tensor(
                        dtt[0][:, :], pp[:, 0:64], dtb_rep[:, :], ALU.add),
                        waits=[t_mm, FREE(dtt[0])], inc=s_ve)
                    SETFREE(pp, t_d0)
                    t_ab = P.op("scalar", lambda e: e.activation(
                        dtt[1][:, :], dtt[0][:, :], AF.Abs),
                        waits=[t_d0, FREE(dtt[1])], inc=s_se)
                    t_e1 = P.op("scalar", lambda e: e.activation(
                        dtt[1][:, :], dtt[1][:, :], AF.Exp, scale=-1.0), waits=[t_ab], inc=s_se)
                    t_l1 = P.op("scalar", lambda e: e.activation(
                        dtt[1][:, :], dtt[1][:, :], AF.Ln, bias=1.0), waits=[t_e1], inc=s_se)
                    t_r0 = P.op("vector", lambda e: e.tensor_scalar_max(
                        dtt[2][:, :], dtt[0][:, :], 0.0), waits=[t_ab, FREE(dtt[2])], inc=s_ve)
                    SETFREE(dtt[0], t_r0)
                    t_dtp = P.op("vector", lambda e: e.tensor_tensor(
                        dtt[3][:, :], dtt[2][:, :], dtt[1][:, :], ALU.add),
                        waits=[t_l1, t_r0, FREE(dtt[3])], inc=s_ve)
                    SETFREE(dtt[1], t_dtp)
                    SETFREE(dtt[2], t_dtp)
                    t_dA = P.op("vector", lambda e, r=r: e.tensor_tensor(
                        dA[:, 64 * r:64 * (r + 1)], dtt[3][:, :], a_rep[:, :], ALU.mult),
                        waits=[t_dtp], inc=s_ve)
                    SETFREE(dtt[3], t_dA)
                    if dt_stop < 2:
                        continue
                    t_acs = P.op("tensor", lambda e, r=r: e.matmul(
                        PS[5][:, 0:64], u2[:, :], dA[:, 64 * r:64 * (r + 1)],
                        start=True, stop=True), waits=[t_dA, FREE(PS[5])], inc=s_pe)
                    t_eacs = P.op("scalar", lambda e, r=r: e.activation(
                        eacs[:, 64 * r:64 * (r + 1)], PS[5][:, 0:64], AF.Exp),
                        waits=[t_acs], inc=s_se)
                    t_acsb = P.op("vector", lambda e: e.tensor_copy(
                        acsb[:, :], PS[5][:, 0:64]), waits=[t_acs, t_eacs, FREE(acsb)], inc=s_ve)
                    SETFREE(PS[5], t_eacs, t_acsb)
                    if dt_stop < 3:
                        continue
                    t_c0 = P.op("tensor", lambda e, r=r: e.matmul(
                        PS[6][0:1, 0:64], half[:, 0:1], dA[:, 64 * r:64 * (r + 1)],
                        start=True, stop=True), waits=[t_dA, FREE(PS[6])], inc=s_pe)
                    t_c1 = P.op("tensor", lambda e, r=r: e.matmul(
                        PS[6][0:1, 64:128], half[:, 1:2], dA[:, 64 * r:64 * (r + 1)],
                        start=True, stop=True), waits=[], inc=s_pe)
                    t_crow = P.op("vector", lambda e: e.tensor_copy(
                        rowc[0:1, :], PS[6][0:1, 0:128]), waits=[t_c1, FREE(rowc)], inc=s_ve)
                    SETFREE(PS[6], t_crow)
                    t_cdr = P.op("scalar", lambda e: e.activation(
                        rowcd[0:1, :], rowc[0:1, :], AF.Exp), waits=[t_crow, FREE(rowcd)], inc=s_se)
                    if dt_stop < 4:
                        continue
                    t_b0 = P.op("tensor", lambda e: e.matmul(
                        PS[7][0:64, 0:64], o1f[0:1, 0:64], rowc[0:1, 0:64],
                        start=True, stop=True), waits=[t_crow, FREE(PS[7])], inc=s_pe)
                    t_b1 = P.op("tensor", lambda e: e.matmul(
                        PS[7][64:128, 0:64], o1f[0:1, 0:64], rowc[0:1, 64:128],
                        start=True, stop=True, tile_position=(0, 64)), waits=[], inc=s_pe)
                    SETFREE(rowc, t_b1)
                    t_sub = P.op("vector", lambda e: e.tensor_tensor(
                        acsb[:, :], PS[7][:, 0:64], acsb[:, :], ALU.subtract),
                        waits=[t_b1, t_acsb], inc=s_ve)
                    SETFREE(PS[7], t_sub)
                    t_dec = P.op("scalar", lambda e, r=r: e.activation(
                        dec[:, 64 * r:64 * (r + 1)], acsb[:, :], AF.Exp),
                        waits=[t_sub], inc=s_se)
                    SETFREE(acsb, t_dec)
                    if dt_stop < 5:
                        continue
                    t_k0 = P.op("tensor", lambda e: e.matmul(
                        PS[4][:, 0:64], o1f[0:1, :], rowcd[0:1, 0:64],
                        start=True, stop=True), waits=[t_cdr, FREE(PS[4])], inc=s_pe)
                    t_k1 = P.op("tensor", lambda e: e.matmul(
                        PS[4][:, 64:128], o1f[0:1, :], rowcd[0:1, 64:128],
                        start=True, stop=True), waits=[], inc=s_pe)
                    t_cdw = P.op("vector", lambda e, r=r: e.tensor_copy(
                        cdrep[:, 128 * r:128 * (r + 1)], PS[4][:, 0:128]),
                        waits=[t_k1], inc=s_ve)
                    SETFREE(PS[4], t_cdw)
                    SETFREE(rowcd, t_k1)
                wsbc = wslot[0]
                for bc in (range(2) if "bc" in m1_parts else []):
                    e0 = 2 * EIN + bc * DS
                    tgt = b_fm if bc == 0 else c_fm
                    wnorm = bnw_c if bc == 0 else cnw_c
                    t_wl = None
                    for k in range(KD):
                        t_wl = P.op("sync", lambda e, e0=e0, k=k: e.dma_start(
                            wsbc[:, 128 * k:128 * (k + 1)],
                            d_winb[e0:e0 + DS, 128 * k:128 * (k + 1)], transpose=True),
                            waits=[FREE(wsbc) if k == 0 else None,
                                   (s_wc, s_wc.n) if k == 0 else None], inc=dsem(wsbc))
                    t_raws = []
                    for ns in range(4):
                        pp = PS[ns % 4]
                        t_mm = None
                        for k in range(KD):
                            t_mm = P.op("tensor", lambda e, pp=pp, k=k, ns=ns: e.matmul(
                                pp[:, 0:512], wsbc[:, 128 * k:128 * (k + 1)],
                                un_fm[k][:, 512 * ns:512 * (ns + 1)],
                                start=(k == 0), stop=(k == KD - 1)),
                                waits=[t_wl, FREE(pp)] if k == 0 else [], inc=s_pe if k == KD - 1 else None)
                        t_sqv = P.op("scalar", lambda e, pp=pp, ns=ns: e.activation(
                            bsq[:, 512 * ns:512 * (ns + 1)], pp[:, 0:512], AF.Square),
                            waits=[t_mm, FREE(bsq) if ns == 0 else None], inc=s_se)
                        t_raw = P.op("vector", lambda e, pp=pp, ns=ns, tgt=tgt: e.tensor_copy(
                            tgt[:, 512 * ns:512 * (ns + 1)], pp[:, 0:512]),
                            waits=[t_mm, t_sqv], inc=s_ve)
                        t_raws.append(t_raw)
                        SETFREE(pp, t_sqv, t_raw)
                    SETFREE(wsbc, s_pe.tok())
                    t_evs = []
                    for ns in range(4):
                        pq = PS[4 + ns % 2]
                        t_nm = P.op("tensor", lambda e, pq=pq, ns=ns: e.matmul(
                            pq[0:1, 0:512], ocf[:, :], bsq[:, 512 * ns:512 * (ns + 1)],
                            start=True, stop=True),
                            waits=[(s_se, s_se.n), FREE(pq)], inc=s_pe)
                        t_ev = P.op("vector", lambda e, pq=pq, ns=ns: e.tensor_copy(
                            ssrow[0:1, 512 * ns:512 * (ns + 1)], pq[0:1, 0:512]),
                            waits=[t_nm, FREE(ssrow) if ns == 0 else None], inc=s_ve)
                        SETFREE(pq, t_ev)
                        t_evs.append(t_ev)
                    SETFREE(bsq, s_pe.tok())
                    t_ln = P.op("scalar", lambda e: e.activation(
                        ssrow[0:1, :], ssrow[0:1, :], AF.Ln, scale=0.0078125, bias=1e-5),
                        waits=t_evs, inc=s_se)
                    t_ex = P.op("scalar", lambda e: e.activation(
                        ssrow[0:1, :], ssrow[0:1, :], AF.Exp, scale=-0.5),
                        waits=[t_ln], inc=s_se)
                    for ns in range(4):
                        pq = PS[4 + ns % 2]
                        t_bc2 = P.op("tensor", lambda e, pq=pq, ns=ns: e.matmul(
                            pq[:, 0:512], o1f[0:1, :], ssrow[0:1, 512 * ns:512 * (ns + 1)],
                            start=True, stop=True), waits=[t_ex, FREE(pq)], inc=s_pe)
                        t_fin = P.op("vector", lambda e, pq=pq, ns=ns, tgt=tgt, wn=wnorm: e.scalar_tensor_tensor(
                            tgt[:, 512 * ns:512 * (ns + 1)], tgt[:, 512 * ns:512 * (ns + 1)],
                            wn[:, 0:1], pq[:, 0:512], ALU.mult, ALU.mult),
                            waits=[t_bc2, t_raws[ns]], inc=s_ve)
                        SETFREE(pq, t_fin)
                    SETFREE(ssrow, s_pe.tok())

                t_w = {0: load_wslot(wslot[0], 0, 512)} if "zx" in m1_parts else {}
                for blk in (range(16) if "zx" in m1_parts else []):
                    e0 = blk * 512
                    is_z = blk < 8
                    ws = wslot[blk % 2]
                    if blk + 1 < 16:
                        t_w[blk + 1] = load_wslot(wslot[(blk + 1) % 2], (blk + 1) * 512, 512)
                    last_mm_of_ws = None
                    for r in range(NT):
                        pp = PS[r % 4]
                        t_mm = None
                        for k in range(KD):
                            t_mm = P.op("tensor", lambda e, pp=pp, k=k, r=r, ws=ws: e.matmul(
                                pp[:, 0:512], un_fm[k][:, 128 * r:128 * (r + 1)],
                                ws[:, 512 * k:512 * (k + 1)], start=(k == 0), stop=(k == KD - 1)),
                                waits=[t_w[blk], FREE(pp)] if k == 0 else [], inc=s_pe if k == KD - 1 else None)
                        last_mm_of_ws = t_mm
                        og = ostg[r % 4]
                        if is_z:
                            t_e = P.op("scalar", lambda e, pp=pp, og=og: e.activation(
                                og[:, :], pp[:, 0:512], AF.Silu), waits=[t_mm, FREE(og)], inc=s_se)
                        else:
                            t_e = P.op("vector", lambda e, pp=pp, og=og: e.tensor_copy(
                                og[:, :], pp[:, 0:512]), waits=[t_mm, FREE(og)], inc=s_ve)
                        SETFREE(pp, t_e)
                        dst = d_sz if is_z else d_x
                        c0 = e0 if is_z else e0 - EIN
                        t_st = P.op("sync", lambda e, r=r, og=og, dst=dst, c0=c0: e.dma_start(
                            dst[128 * r:128 * (r + 1), c0:c0 + 512], og[:, :]),
                            waits=[t_e], inc=dsem(og))
                        SETFREE(og, t_st)
                    SETFREE(ws, last_mm_of_ws)

                P.emit(nc)
            UNS.close()


        if phases >= 3:
            with ExitStack() as S:
                e2 = S.enter_context
                xsl = [e2(nc.sbuf_tensor(f"xsl{i}", [128, EIN], BF16, side="right")) for i in range(2)]
                szl = [e2(nc.sbuf_tensor(f"szl{i}", [128, EIN], BF16, side="right")) for i in range(2)]
                sLT = e2(nc.sbuf_tensor("sLT", [128, EIN], BF16, side="right"))
                srun = e2(nc.sbuf_tensor("srun", [128, EIN], BF16, side="right"))
                ygs = [e2(nc.sbuf_tensor(f"ygs{i}", [128, EIN], BF16, side="right")) for i in range(2)]
                ytp = [e2(nc.sbuf_tensor(f"ytp{i}", [128, EIN], BF16, side="right")) for i in range(2)]
                dAmW = e2(nc.sbuf_tensor("dAmW", [128, EIN], BF16, side="right"))
                eDW = e2(nc.sbuf_tensor("eDW", [128, EIN], BF16, side="right"))
                xdsW = [e2(nc.sbuf_tensor(f"xdsW{i}", [128, EIN], BF16, side="right")) for i in range(2)]
                xDW = [e2(nc.sbuf_tensor(f"xDW{i}", [128, EIN], BF16, side="right")) for i in range(2)]
                stp = [e2(nc.sbuf_tensor(f"stp{i}", [128, 512], BF16, side="right")) for i in range(2)]
                gb1 = [e2(nc.sbuf_tensor(f"gb1{i}", [128, 512], F32, side="right")) for i in range(2)]
                btm = e2(nc.sbuf_tensor("btm", [128, 128], BF16, side="right"))

                P = Prog("M2")
                guard(P)
                t_srz = P.op("gpsimd", lambda e: e.memset(srun[:, :], 0.0), inc=s_po)
                SETFREE(srun, t_srz)
                sLT_free = {}
                srun_dep = {}

                def bcap(t, pstride, pcnt, blocks, off):
                    return bass.AP(t.tensor if isinstance(t, bass.AP) else t, off,
                                   [[pstride, pcnt]] + [list(b) for b in blocks])

                pend_tp = None
                ygap = d_yg.ap().rearrange("(k p) n -> p k n", p=128)

                def emit_transposes(pend):
                    r, yg, yt, t_gall = pend
                    last_yt = None
                    for k in range(KE):
                        pT = PS[6] if k % 2 == 0 else PS[7]
                        t_t = P.op("tensor", lambda e, k=k, pT=pT, yg=yg: e.transpose(
                            pT[:, 0:64].bitcast(BF16), yg[:, 128 * k:128 * (k + 1)], idb[:, :]),
                            waits=[t_gall if k == 0 else None, FREE(pT)], inc=s_pe)
                        t_ev = P.op("scalar", lambda e, k=k, pT=pT, yt=yt: e.copy(
                            yt[:, 128 * k:128 * (k + 1)], pT[:, 0:64].bitcast(BF16)),
                            waits=[t_t, FREE(yt) if k == 0 else None], inc=s_se)
                        SETFREE(pT, t_ev)
                        last_yt = t_ev
                    SETFREE(yg, s_pe.tok())
                    t_sty = P.op("sync", lambda e, r=r, yt=yt: e.dma_start(
                        ygap[:, :, 128 * r:128 * (r + 1)],
                        yt[:, :].rearrange("p (k n) -> p k n", k=KE)),
                        waits=[last_yt], inc=dsem(yt))
                    SETFREE(yt, t_sty)

                for r in range(NT):
                    xb, zb = xsl[r % 2], szl[r % 2]
                    yg, yt = ygs[r % 2], ytp[r % 2]
                    xds, xD = xdsW[r % 2], xDW[r % 2]
                    t_x = P.op("sync", lambda e, r=r, xb=xb: e.dma_start(
                        xb[:, :], d_x[128 * r:128 * (r + 1), :]), waits=[FREE(xb)], inc=dsem(xb))
                    t_z = P.op("sync", lambda e, r=r, zb=zb: e.dma_start(
                        zb[:, :], d_sz[128 * r:128 * (r + 1), :]), waits=[FREE(zb)], inc=dsem(zb))
                    t_sc0 = P.op("tensor", lambda e, r=r: e.matmul(
                        PS[6][0:64, 0:64], b_fm[:, 128 * r:128 * r + 64],
                        c_fm[:, 128 * r:128 * r + 64], start=True, stop=True),
                        waits=[FREE(PS[6])], inc=s_pe)
                    t_sc1 = P.op("tensor", lambda e, r=r: e.matmul(
                        PS[6][64:128, 0:64], b_fm[:, 128 * r + 64:128 * r + 128],
                        c_fm[:, 128 * r + 64:128 * r + 128], start=True, stop=True,
                        tile_position=(0, 64)), waits=[], inc=s_pe)
                    t_stm = P.op("vector", lambda e, r=r: e.tensor_tensor(
                        stm[:, 64 * r:64 * (r + 1)], PS[6][:, 0:64], tri2[:, :], ALU.mult),
                        waits=[t_sc1], inc=s_ve)
                    SETFREE(PS[6], t_stm)
                    t_bt0 = P.op("tensor", lambda e, r=r: e.transpose(
                        PS[7][0:64, 0:64].bitcast(BF16), b_fm[:, 128 * r:128 * r + 64], idb[:, :]),
                        waits=[FREE(PS[7])], inc=s_pe)
                    t_bt1 = P.op("tensor", lambda e, r=r: e.transpose(
                        PS[7][64:128, 0:64].bitcast(BF16), b_fm[:, 128 * r + 64:128 * r + 128],
                        idb[:, :], tile_position=(0, 64)), waits=[], inc=s_pe)
                    t_btm = P.op("vector", lambda e: e.tensor_copy(
                        btm[:, :], PS[7][:, 0:64].bitcast(BF16)), waits=[t_bt1, FREE(btm)], inc=s_ve)
                    SETFREE(PS[7], t_btm)
                    t_xds = P.op("gpsimd", lambda e, r=r, xb=xb, xds=xds: e.tensor_tensor(
                        xds[:, :], xb[:, :],
                        bcap(dec, NH * NT, 128, [[1, NH], [0, CH]], 64 * r),
                        ALU.mult), waits=[t_x, FREE(xds)], inc=s_po)
                    t_xD = P.op("gpsimd", lambda e, r=r, xb=xb, xD=xD: e.tensor_tensor(
                        xD[:, :], xb[:, :],
                        bcap(dd_rep, NH, 128, [[1, NH], [0, CH]], 0),
                        ALU.mult), waits=[t_x, FREE(xD)], inc=s_po)
                    t_dam = P.op("vector", lambda e, r=r: e.tensor_tensor(
                        dAmW[:, :],
                        bcap(dA, NH * NT, 128, [[1, NH], [0, CH]], 64 * r),
                        bcap(tri2, CH, 128, [[0, NH], [1, CH]], 0),
                        ALU.mult), waits=[FREE(dAmW)], inc=s_ve)
                    t_eDs = {}
                    for j in range(8):
                        pD = PS[6] if j % 2 == 0 else PS[7]
                        t_D = P.op("tensor", lambda e, j=j, pD=pD: e.matmul(
                            pD[:, 0:512], slowb[:, :], dAmW[:, 512 * j:512 * (j + 1)],
                            start=True, stop=True),
                            waits=[t_dam, FREE(pD)], inc=s_pe)
                        t_eD = P.op("scalar", lambda e, j=j, pD=pD: e.activation(
                            eDW[:, 512 * j:512 * (j + 1)], pD[:, 0:512], AF.Exp),
                            waits=[t_D, FREE(eDW) if j == 0 else None], inc=s_se)
                        SETFREE(pD, t_eD)
                        t_eDs[j] = t_eD
                    SETFREE(dAmW, s_pe.tok())
                    t_sLTh = {}
                    for half in range(2):
                        t_sLTh[half] = P.op("vector", lambda e, r=r, half=half: e.tensor_tensor(
                            sLT[:, 2048 * half:2048 * (half + 1)],
                            eDW[:, 2048 * half:2048 * (half + 1)],
                            bcap(stm, CH * NT, 128, [[0, NH // 2], [1, CH]], 64 * r),
                            ALU.mult),
                            waits=[t_eDs[4 * half + 3], t_stm,
                                   sLT_free.get(half)], inc=s_ve)
                    SETFREE(eDW, s_ve.tok())
                    t_ygl = []
                    for q in range(4):
                        pY = {0: PS[0], 1: PS[1]}
                        for cp in range(2):
                            for hh in range(16):
                                h = 16 * q + hh
                                pb = pY[hh // 8]
                                P.op("tensor", lambda e, cp=cp, h=h, hh=hh, pb=pb, xb=xb: e.matmul(
                                    pb[64 * cp:64 * (cp + 1), 64 * (hh % 8):64 * (hh % 8) + 64],
                                    sLT[64 * cp:64 * cp + 64, 64 * h:64 * h + 64],
                                    xb[64 * cp:64 * cp + 64, 64 * h:64 * h + 64],
                                    start=True, stop=True,
                                    tile_position=(64, 64) if cp == 1 else None),
                                    waits=[t_sLTh[q // 2], t_x,
                                           FREE(pb) if (cp == 0 and hh % 8 == 0) else None],
                                    inc=s_pe if (cp == 1 and hh in (7, 15)) else None)
                        t_yd1 = (s_pe, s_pe.n)
                        sLT_free[q // 2] = [sLT_free.get(q // 2), t_yd1]
                        pO = {2 * q: PS[2], 2 * q + 1: PS[3]}
                        pS = {2 * q: PS[4], 2 * q + 1: PS[5]}
                        t_upd = {}
                        t_yoff = {}
                        for cp in range(2):
                            for j in (2 * q, 2 * q + 1):
                                t_yo = P.op("tensor", lambda e, r=r, cp=cp, j=j, pb=pO[j]: e.matmul(
                                    pb[64 * cp:64 * (cp + 1), 0:512],
                                    c_fm[:, 128 * r + 64 * cp:128 * r + 64 * cp + 64],
                                    srun[:, 512 * j:512 * (j + 1)], start=True, stop=True,
                                    tile_position=(0, 64) if cp == 1 else None),
                                    waits=[FREE(pO[j]) if cp == 0 else None,
                                           t_upd.get((cp - 1, j)),
                                           srun_dep.get(j) if cp == 0 else None,
                                           t_srz if r == 0 else None],
                                    inc=s_pe)
                                t_yoff[(cp, j)] = t_yo
                                t_st = P.op("tensor", lambda e, cp=cp, j=j, pb=pS[j], xds=xds: e.matmul(
                                    pb[:, 0:512], btm[64 * cp:64 * cp + 64, :],
                                    xds[64 * cp:64 * cp + 64, 512 * j:512 * (j + 1)],
                                    start=True, stop=True),
                                    waits=[t_btm, t_xds, FREE(pS[j])], inc=s_pe)
                                ueng = "gpsimd"
                                usem = s_po
                                t_u1 = P.op(ueng, lambda e, r=r, cp=cp, j=j: e.tensor_tensor(
                                    stp[j % 2][:, :], srun[:, 512 * j:512 * (j + 1)],
                                    bcap(cdrep, 2 * NH * NT, 128, [[1, 8], [0, CH]],
                                         128 * r + 64 * cp + 8 * j),
                                    ALU.mult), waits=[t_yo, FREE(stp[j % 2])], inc=usem)
                                t_u2 = P.op("vector", lambda e, j=j, pb=pS[j]: e.tensor_tensor(
                                    srun[:, 512 * j:512 * (j + 1)], stp[j % 2][:, :],
                                    pb[:, 0:512], ALU.add), waits=[t_u1, t_st], inc=s_ve)
                                SETFREE(stp[j % 2], t_u2)
                                SETFREE(pS[j], t_u2)
                                t_upd[(cp, j)] = t_u2
                                if cp == 1:
                                    srun_dep[j] = t_u2
                        for j in (2 * q, 2 * q + 1):
                            g1 = gb1[j % 2]
                            t_g1 = P.op("vector", lambda e, r=r, j=j, g1=g1, pb=pO[j]: e.tensor_tensor(
                                g1[:, :], pb[:, 0:512],
                                bcap(eacs, NH * NT, 128, [[1, 8], [0, CH]], 64 * r + 8 * j),
                                ALU.mult), waits=[t_yoff[(1, j)], FREE(g1)], inc=s_ve)
                            SETFREE(pO[j], t_g1)
                            pb = pY[(j - 2 * q)]
                            t_g3 = P.op("vector", lambda e, g1=g1, pb=pb: e.scalar_tensor_tensor(
                                g1[:, :], pb[:, 0:512], epsb[:, 2:3], g1[:, :],
                                ALU.add, ALU.add), waits=[t_yd1, t_g1], inc=s_ve)
                            SETFREE(pb, t_g3)
                            t_g4 = P.op("gpsimd", lambda e, j=j, g1=g1, yg=yg, xD=xD: e.tensor_tensor(
                                yg[:, 512 * j:512 * (j + 1)], g1[:, :],
                                xD[:, 512 * j:512 * (j + 1)], ALU.add),
                                waits=[t_g3, t_xD, FREE(yg) if j == 0 else None], inc=s_po)
                            SETFREE(g1, t_g4)
                            t_ygl.append(t_g4)
                    SETFREE(xD, t_ygl[-1] if t_ygl else None)
                    t_g5 = P.op("gpsimd", lambda e, yg=yg, zb=zb: e.tensor_tensor(
                        yg[:, :], yg[:, :], zb[:, :], ALU.mult),
                        waits=[t_ygl[-1], t_z], inc=s_po)
                    SETFREE(xb, s_po.tok(), s_pe.tok())
                    SETFREE(zb, t_g5)
                    SETFREE(btm, s_pe.tok(), t_btm)
                    SETFREE(xds, s_pe.tok())
                    if pend_tp is not None:
                        emit_transposes(pend_tp)
                    pend_tp = (r, yg, yt, t_g5)
                emit_transposes(pend_tp)
                P.emit(nc)


        if phases >= 4:
            with ExitStack() as S:
                e3 = S.enter_context
                wohs = [e3(nc.sbuf_tensor(f"woh{i}", [128, KE * 1024], BF16, side="right")) for i in range(2)]
                ygc = [e3(nc.sbuf_tensor(f"ygc{i}", [128, EIN], BF16, side="right")) for i in range(2)]
                mstg = [e3(nc.sbuf_tensor(f"mstg{i}", [128, 1024], BF16, side="right")) for i in range(2)]

                P = Prog("M3")
                guard(P)
                ygap = d_yg.ap().rearrange("(k p) n -> p k n", p=128)
                def emit_wohload(dh, ec):
                    return P.op("sync", lambda e, dh=dh, ec=ec: e.dma_start(
                        wohs[dh][:, 1024 * ec:1024 * (ec + 1)],
                        d_woutb[1024 * dh:1024 * (dh + 1), 128 * ec:128 * (ec + 1)],
                        transpose=True),
                        waits=[FREE(wohs[dh]) if ec == 0 else None,
                               (s_wo, s_wo.n) if ec == 0 and dh == 0 else None],
                        inc=dsem(wohs[dh]))

                t_wls = {}
                for ec in range(KE):
                    t_wls[0] = emit_wohload(0, ec)
                for dh in range(2):
                    woh = wohs[dh]
                    t_wl = t_wls[dh]
                    for r in range(NT):
                        yc = ygc[r % 2]
                        t_y = P.op("sync", lambda e, r=r, yc=yc: e.dma_start(
                            yc[:, :].rearrange("p (k n) -> p k n", k=KE),
                            ygap[:, :, 128 * r:128 * (r + 1)]),
                            waits=[FREE(yc)], inc=dsem(yc))
                        if dh == 0:
                            for j2 in range(2):
                                t_wls[1] = emit_wohload(1, 2 * r + j2)
                        pa, pb = PS[(2 * r) % 4], PS[(2 * r + 1) % 4]
                        t_ma = t_mb = None
                        for ec in range(KE):
                            t_ma = P.op("tensor", lambda e, ec=ec, pa=pa, yc=yc, woh=woh: e.matmul(
                                pa[:, 0:512], yc[:, 128 * ec:128 * (ec + 1)],
                                woh[:, 1024 * ec:1024 * ec + 512],
                                start=(ec == 0), stop=(ec == KE - 1)),
                                waits=[t_y, t_wl, FREE(pa)] if ec == 0 else [],
                                inc=s_pe if ec == KE - 1 else None)
                            t_mb = P.op("tensor", lambda e, ec=ec, pb=pb, yc=yc, woh=woh: e.matmul(
                                pb[:, 0:512], yc[:, 128 * ec:128 * (ec + 1)],
                                woh[:, 1024 * ec + 512:1024 * (ec + 1)],
                                start=(ec == 0), stop=(ec == KE - 1)),
                                waits=[FREE(pb)] if ec == 0 else [],
                                inc=s_pe if ec == KE - 1 else None)
                        ms = mstg[r % 2]
                        t_e1 = P.op("vector", lambda e, pa=pa, ms=ms: e.tensor_copy(
                            ms[:, 0:512], pa[:, 0:512]), waits=[t_ma, FREE(ms)], inc=s_ve)
                        t_e2 = P.op("vector", lambda e, pb=pb, ms=ms: e.tensor_copy(
                            ms[:, 512:1024], pb[:, 0:512]), waits=[t_mb], inc=s_ve)
                        SETFREE(pa, t_e1)
                        SETFREE(pb, t_e2)
                        t_st = P.op("sync", lambda e, r=r, dh=dh, ms=ms: e.dma_start(
                            d_mam[128 * r:128 * (r + 1), 1024 * dh:1024 * (dh + 1)], ms[:, :]),
                            waits=[t_e1, t_e2], inc=dsem(ms))
                        SETFREE(ms, t_st)
                        SETFREE(yc, (s_pe, s_pe.n))
                    SETFREE(woh, (s_pe, s_pe.n))
                P.emit(nc)
            M2S.close()


        if phases >= 5:
            with ExitStack() as S:
                e4 = S.enter_context
                zdc = ent(nc.sbuf_tensor("zdc", [128, NE], F32))
                ccol = e4(nc.sbuf_tensor("ccol", [128, NE], F32, side="right"))
                xnf = e4(nc.sbuf_tensor("xnf", [128, KD * DM], BF16, side="right"))
                muf = [e4(nc.sbuf_tensor(f"muf{i}", [128, DM], F32, side="right")) for i in range(2)]
                msq = [e4(nc.sbuf_tensor(f"msq{i}", [128, DM], F32, side="right")) for i in range(2)]
                mut = [e4(nc.sbuf_tensor(f"mut{i}", [128, DM], BF16, side="right")) for i in range(2)]
                exl = [e4(nc.sbuf_tensor(f"exl{i}", [128, DM], BF16, side="right")) for i in range(2)]
                csr = e4(nc.sbuf_tensor("csr", [1, NE * SL], F32, side="right"))
                zd4 = e4(nc.sbuf_tensor("zd4", [128, 4], F32, side="right"))

                P = Prog("B5a")
                guard(P)
                for ei in range(NE):
                    P.op("gpsimd", lambda e, ei=ei: e.dma_start(d_w1b[ei], w1[ei]), inc=s_w12)
                    P.op("gpsimd", lambda e, ei=ei: e.dma_start(d_w2b[ei], w2[ei]), inc=s_w12)
                t_w12 = s_w12.tok()
                t_nm = {}
                for k in range(KD):
                    mf, mq = muf[k % 2], msq[k % 2]
                    t_l = P.op("sync", lambda e, k=k, mf=mf: e.dma_start(
                        mf[:, :], mu[128 * k:128 * (k + 1), :]), waits=[FREE(mf)], inc=dsem(mf))
                    t_s = P.op("scalar", lambda e, mf=mf, mq=mq: e.activation(
                        mq[:, :], mf[:, :], AF.Square), waits=[t_l, FREE(mq)], inc=s_se)
                    SETFREE(mf, t_s)
                    for ns in range(4):
                        t_nm[ns] = P.op("tensor", lambda e, ns=ns, k=k, mq=mq: e.matmul(
                            PS[4 + ns][0:1, 0:512],
                            ocf[:, :], mq[:, 512 * ns:512 * (ns + 1)],
                            start=(k == 0), stop=(k == KD - 1)),
                            waits=[t_s, FREE(PS[4 + ns]) if k == 0 else None],
                            inc=s_pe)
                    SETFREE(mq, (s_pe, s_pe.n))
                t_ev = []
                for ns in range(4):
                    t_ev.append(P.op("vector", lambda e, ns=ns: e.tensor_copy(
                        csr[0:1, 512 * ns:512 * (ns + 1)],
                        PS[4 + ns][0:1, 0:512]),
                        waits=[t_nm[ns], FREE(csr)], inc=s_ve))
                for ns in range(4):
                    SETFREE(PS[4 + ns], t_ev[ns])
                t_sq = P.op("scalar", lambda e: e.activation(
                    csr[0:1, :], csr[0:1, :], AF.Sqrt), waits=t_ev, inc=s_se)
                t_mx = P.op("vector", lambda e: e.tensor_scalar_max(
                    csr[0:1, :], csr[0:1, :], 1e-12), waits=[t_sq], inc=s_ve)
                t_rc = P.op("vector", lambda e: e.reciprocal(
                    csr[0:1, :], csr[0:1, :]), waits=[t_mx], inc=s_ve)
                SETFREE(csr, t_rc)
                for t in range(NE):
                    pq = PS[6 + t % 2]
                    t_t = P.op("tensor", lambda e, t=t, pq=pq: e.transpose(
                        pq[:, 0:1], csr[0:1, 128 * t:128 * (t + 1)], o1f[0:1, 0:1]),
                        waits=[t_rc, FREE(pq)], inc=s_pe)
                    t_c = P.op("vector", lambda e, t=t, pq=pq: e.tensor_copy(
                        ccol[:, t:t + 1], pq[:, 0:1]), waits=[t_t], inc=s_ve)
                    SETFREE(pq, t_c)
                t_ccol = (s_ve, s_ve.n)
                t_xnf = None
                for k in range(KD):
                    t_xnf = P.op("sync", lambda e, k=k: e.dma_start(
                        xnf[:, DM * k:DM * (k + 1)], d_xnorm[128 * k:128 * (k + 1), :]),
                        waits=[FREE(xnf) if k == 0 else None], inc=dsem(xnf))
                muap = mu.ap().rearrange("(k p) s -> p k s", p=128)
                for t in range(NE):
                    mt = mut[t % 2]
                    ex = exl[t % 2]
                    t_mu = P.op("gpsimd", lambda e, t=t, mt=mt: e.dma_start(
                        mt[:, :].rearrange("p (k s) -> p k s", k=KD),
                        muap[:, :, 128 * t:128 * (t + 1)]),
                        waits=[FREE(mt)], inc=dsem(mt, True))
                    t_es = []
                    for ns in range(4):
                        pp = PS[ns % 4]
                        t_mm = None
                        for k in range(KD):
                            t_mm = P.op("tensor", lambda e, ns=ns, k=k, pp=pp, mt=mt: e.matmul(
                                pp[:, 0:512], mt[:, 128 * k:128 * (k + 1)],
                                xnf[:, DM * k + 512 * ns:DM * k + 512 * (ns + 1)],
                                start=(k == 0), stop=(k == KD - 1)),
                                waits=[t_mu, t_xnf, FREE(pp)] if k == 0 else [],
                                inc=s_pe if k == KD - 1 else None)
                        t_e = P.op("scalar", lambda e, ns=ns, t=t, pp=pp, ex=ex: e.activation(
                            ex[:, 512 * ns:512 * (ns + 1)], pp[:, 0:512], AF.Exp,
                            scale=ccol[:, t:t + 1], accum_out=zd4[:, ns:ns + 1]),
                            waits=[t_mm, t_ccol, FREE(ex) if ns == 0 else None,
                                   FREE(zd4) if ns == 0 else None], inc=s_se)
                        SETFREE(pp, t_e)
                        t_es.append(t_e)
                    t_zd = P.op("vector", lambda e, t=t: e.reduce_sum(
                        zdc[:, t:t + 1], zd4[:, 0:4], axis=AX), waits=t_es, inc=s_ve)
                    SETFREE(zd4, t_zd)
                    t_st = P.op("sync", lambda e, t=t, ex=ex: e.dma_start(
                        d_expl[128 * t:128 * (t + 1), :], ex[:, :]), waits=t_es, inc=dsem(ex))
                    SETFREE(ex, t_st)
                    SETFREE(mt, (s_pe, s_pe.n))
                P.emit(nc)

        if phases >= 6:
            with ExitStack() as S:
                e5 = S.enter_context
                dtm = e5(nc.sbuf_tensor("dtm", [128, NT * DM], BF16, side="right"))
                exb = [e5(nc.sbuf_tensor(f"exb{i}", [128, DM], BF16, side="right")) for i in range(2)]
                zrep = e5(nc.sbuf_tensor("zrep", [128, 128], F32, side="right"))
                zrow = e5(nc.sbuf_tensor("zrow", [1, 128], F32, side="right"))
                zcolr = e5(nc.sbuf_tensor("zcolr", [128, 1], F32, side="right"))
                xlt = [e5(nc.sbuf_tensor(f"xlt{i}", [128, NT * 128], BF16, side="right")) for i in range(2)]
                sst = [e5(nc.sbuf_tensor(f"sst{i}", [128, 512], BF16, side="right")) for i in range(2)]

                P = Prog("B5b6")
                guard(P)
                for t in range(NE):
                    ex = exb[t % 2]
                    t_le = P.op("sync", lambda e, t=t, ex=ex: e.dma_start(
                        ex[:, :], d_expl[128 * t:128 * (t + 1), :]), waits=[FREE(ex)], inc=dsem(ex))
                    t_rc = P.op("vector", lambda e, t=t: e.reciprocal(
                        zcolr[:, 0:1], zdc[:, t:t + 1]), waits=[FREE(zcolr)], inc=s_ve)
                    t_tp = P.op("tensor", lambda e: e.transpose(
                        PS[6][0:1, 0:128], zcolr[:, 0:1], idf[:, :]),
                        waits=[t_rc, FREE(PS[6])], inc=s_pe)
                    t_zr = P.op("vector", lambda e: e.tensor_copy(
                        zrow[0:1, :], PS[6][0:1, 0:128]), waits=[t_tp, FREE(zrow)], inc=s_ve)
                    SETFREE(PS[6], t_zr)
                    SETFREE(zcolr, t_zr)
                    t_zb = P.op("tensor", lambda e: e.matmul(
                        PS[7][:, 0:128], o1f[0:1, :], zrow[0:1, :], start=True, stop=True),
                        waits=[t_zr, FREE(PS[7])], inc=s_pe)
                    t_ze = P.op("vector", lambda e: e.tensor_copy(
                        zrep[:, :], PS[7][:, 0:128]), waits=[t_zb, FREE(zrep)], inc=s_ve)
                    SETFREE(PS[7], t_ze)
                    SETFREE(zrow, t_zb)
                    last = None
                    for r in range(NT):
                        pq = PS[r % 2]
                        t_t = P.op("tensor", lambda e, r=r, pq=pq, ex=ex: e.transpose(
                            pq[:, 0:64].bitcast(BF16), ex[:, 128 * r:128 * (r + 1)], idb[:, :]),
                            waits=[t_le, FREE(pq)], inc=s_pe)
                        t_d = P.op("vector", lambda e, r=r, t=t, pq=pq: e.scalar_tensor_tensor(
                            dtm[:, DM * r + 128 * t:DM * r + 128 * (t + 1)],
                            pq[:, 0:64].bitcast(BF16), epsb[:, 2:3], zrep[:, :],
                            ALU.add, ALU.mult), waits=[t_t, t_ze], inc=s_ve)
                        SETFREE(pq, t_d)
                        last = t_d
                    SETFREE(ex, (s_pe, s_pe.n))
                    SETFREE(zrep, last)
                t_dtm = (s_ve, s_ve.n)
                xlap = d_xln.ap().rearrange("(r p) k -> p r k", p=128)
                for k in range(KD):
                    xt = xlt[k % 2]
                    t_lx = P.op("sync", lambda e, k=k, xt=xt: e.dma_start(
                        xt[:, :].rearrange("p (r c) -> p r c", r=NT),
                        xlap[:, :, 128 * k:128 * (k + 1)]),
                        waits=[FREE(xt)], inc=dsem(xt))
                    for ns in range(4):
                        pp = PS[2 + ns % 2]
                        t_mm = None
                        for r in range(NT):
                            t_mm = P.op("tensor", lambda e, ns=ns, r=r, pp=pp, xt=xt: e.matmul(
                                pp[:, 0:512], xt[:, 128 * r:128 * (r + 1)],
                                dtm[:, DM * r + 512 * ns:DM * r + 512 * (ns + 1)],
                                start=(r == 0), stop=(r == NT - 1)),
                                waits=[t_lx, t_dtm, FREE(pp)] if r == 0 else [],
                                inc=s_pe if r == NT - 1 else None)
                        ss = sst[ns % 2]
                        t_e = P.op("vector", lambda e, pp=pp, ss=ss: e.tensor_copy(
                            ss[:, :], pp[:, 0:512]), waits=[t_mm, FREE(ss)], inc=s_ve)
                        SETFREE(pp, t_e)
                        t_s = P.op("sync", lambda e, k=k, ns=ns, ss=ss: e.dma_start(
                            d_slotin[128 * k:128 * (k + 1), 512 * ns:512 * (ns + 1)], ss[:, :]),
                            waits=[t_e], inc=dsem(ss))
                        SETFREE(ss, t_s)
                    SETFREE(xt, (s_pe, s_pe.n))
                P.emit(nc)

        if phases >= 7:
            with ExitStack() as S:
                e6 = S.enter_context
                sos = e6(nc.sbuf_tensor("sos", [128, NE * DM], BF16, side="right"))
                B7S = ExitStack()
                e7 = B7S.enter_context
                w1t = [e7(nc.sbuf_tensor(f"w1t{i}", [128, KD * HH], BF16, side="right")) for i in range(2)]
                w2t = [e7(nc.sbuf_tensor(f"w2t{i}", [128, 4 * DM], BF16, side="right")) for i in range(2)]
                sit = [e7(nc.sbuf_tensor(f"sit{i}", [128, KD * 128], BF16, side="right")) for i in range(2)]
                b1r = [e7(nc.sbuf_tensor(f"b1r{i}", [1, HH], BF16, side="right")) for i in range(2)]
                b2r = [e7(nc.sbuf_tensor(f"b2r{i}", [1, DM], BF16, side="right")) for i in range(2)]
                hsm = e7(nc.sbuf_tensor("hsm", [128, HH], BF16, side="right"))
                hT = e7(nc.sbuf_tensor("hT", [128, HH], BF16, side="right"))

                P = Prog("B7")
                guard(P)
                w1ap = d_w1b.ap().rearrange("e (k p) h -> e p k h", p=128)
                w2ap = d_w2b.ap().rearrange("e (j p) d -> e p j d", p=128)
                siap = d_slotin.ap().rearrange("(k p) s -> p k s", p=128)
                for ei in range(NE):
                    wa, wb = w1t[ei % 2], w2t[ei % 2]
                    si = sit[ei % 2]
                    ba, bb = b1r[ei % 2], b2r[ei % 2]
                    t_w1 = P.op("sync", lambda e, ei=ei, wa=wa: e.dma_start(
                        wa[:, :].rearrange("p (k h) -> p k h", k=KD), w1ap[ei]),
                        waits=[FREE(wa), t_w12 if ei == 0 else None], inc=dsem(wa))
                    t_w2 = P.op("sync", lambda e, ei=ei, wb=wb: e.dma_start(
                        wb[:, :].rearrange("p (j d) -> p j d", j=4), w2ap[ei]),
                        waits=[FREE(wb), t_w12 if ei == 0 else None], inc=dsem(wb))
                    t_b1 = P.op("gpsimd", lambda e, ei=ei, ba=ba: e.dma_start(
                        ba[0:1, :], b1[ei:ei + 1, :]), waits=[FREE(ba)], inc=dsem(ba, True))
                    t_b2 = P.op("gpsimd", lambda e, ei=ei, bb=bb: e.dma_start(
                        bb[0:1, :], b2[ei:ei + 1, :]), waits=[FREE(bb)], inc=dsem(bb, True))
                    t_si = P.op("sync", lambda e, ei=ei, si=si: e.dma_start(
                        si[:, :].rearrange("p (k s) -> p k s", k=KD),
                        siap[:, :, 128 * ei:128 * (ei + 1)]),
                        waits=[FREE(si)], inc=dsem(si))
                    t_mm = None
                    for k in range(KD):
                        t_mm = P.op("tensor", lambda e, k=k, si=si, wa=wa: e.matmul(
                            PS[0][:, 0:512], si[:, 128 * k:128 * (k + 1)],
                            wa[:, HH * k:HH * (k + 1)], start=(k == 0), stop=False),
                            waits=[t_si, t_w1, FREE(PS[0])] if k == 0 else [], inc=None)
                    t_mm = P.op("tensor", lambda e, ba=ba: e.matmul(
                        PS[0][:, 0:512], o1b[0:1, :], ba[0:1, :], start=False, stop=True),
                        waits=[t_b1], inc=s_pe)
                    t_h = P.op("scalar", lambda e: e.activation(
                        hsm[:, :], PS[0][:, 0:512], AF.Gelu), waits=[t_mm, FREE(hsm)], inc=s_se)
                    SETFREE(PS[0], t_h)
                    t_hts = []
                    for j in range(4):
                        pq = PS[1 + j % 2]
                        t_t = P.op("tensor", lambda e, j=j, pq=pq: e.transpose(
                            pq[:, 0:64].bitcast(BF16), hsm[:, 128 * j:128 * (j + 1)], idb[:, :]),
                            waits=[t_h, FREE(pq)], inc=s_pe)
                        t_e = P.op("vector", lambda e, j=j, pq=pq: e.tensor_copy(
                            hT[:, 128 * j:128 * (j + 1)], pq[:, 0:64].bitcast(BF16)),
                            waits=[t_t, FREE(hT) if j == 0 else None], inc=s_ve)
                        SETFREE(pq, t_e)
                        t_hts.append(t_e)
                    SETFREE(hsm, s_pe.tok())
                    for dns in range(4):
                        pp = PS[3 + dns % 2]
                        t_mm = None
                        for j in range(4):
                            t_mm = P.op("tensor", lambda e, j=j, dns=dns, pp=pp, wb=wb: e.matmul(
                                pp[:, 0:512], hT[:, 128 * j:128 * (j + 1)],
                                wb[:, DM * j + 512 * dns:DM * j + 512 * (dns + 1)],
                                start=(j == 0), stop=False),
                                waits=[t_hts[j], t_w2, FREE(pp)] if j == 0 else [t_hts[j]], inc=None)
                        t_mm = P.op("tensor", lambda e, dns=dns, pp=pp, bb=bb: e.matmul(
                            pp[:, 0:512], o1b[0:1, :], bb[0:1, 512 * dns:512 * (dns + 1)],
                            start=False, stop=True), waits=[t_b2], inc=s_pe)
                        t_e = P.op("vector", lambda e, ei=ei, dns=dns, pp=pp: e.tensor_copy(
                            sos[:, DM * ei + 512 * dns:DM * ei + 512 * (dns + 1)], pp[:, 0:512]),
                            waits=[t_mm], inc=s_ve)
                        SETFREE(pp, t_e)
                    SETFREE(hT, s_pe.tok())
                    SETFREE(si, s_pe.tok())
                    SETFREE(wa, s_pe.tok())
                    SETFREE(wb, s_pe.tok())
                    SETFREE(ba, s_pe.tok())
                    SETFREE(bb, s_pe.tok())
                t_sos = (s_ve, s_ve.n)
                b7done = [s_pe.tok(), s_ve.tok(), s_se.tok(), s_po.tok()]
                B7S.close()

                exb2 = [e6(nc.sbuf_tensor(f"exb2{i}", [128, DM], BF16, side="right")) for i in range(2)]
                xfin = [e6(nc.sbuf_tensor(f"xfin{i}", [128, DM], F32, side="right")) for i in range(2)]
                xnb = [e6(nc.sbuf_tensor(f"xnb{i}", [128, DM], BF16, side="right")) for i in range(2)]
                mamb = [e6(nc.sbuf_tensor(f"mamb{i}", [128, DM], BF16, side="right")) for i in range(2)]
                obuf = [e6(nc.sbuf_tensor(f"obuf{i}", [128, DM], F32, side="right")) for i in range(2)]
                zcr = e6(nc.sbuf_tensor("zcr", [128, 1], F32, side="right"))

                exap = d_expl.ap().rearrange("(t p) n -> p t n", p=128)
                for r in range(NT):
                    ex = exb2[r % 2]
                    xf, xnl, mml, ob = xfin[r % 2], xnb[r % 2], mamb[r % 2], obuf[r % 2]
                    bg = b7done if r < 2 else []
                    t_le = P.op("sync", lambda e, r=r, ex=ex: e.dma_start(
                        ex[:, :].rearrange("p (t n) -> p t n", t=NE),
                        exap[:, :, 128 * r:128 * (r + 1)]), waits=[FREE(ex), bg], inc=dsem(ex))
                    t_lx = P.op("sync", lambda e, r=r, xf=xf: e.dma_start(
                        xf[:, :], x_in[128 * r:128 * (r + 1), :]), waits=[FREE(xf), bg], inc=dsem(xf))
                    t_ln = P.op("sync", lambda e, r=r, xnl=xnl: e.dma_start(
                        xnl[:, :], d_xn[128 * r:128 * (r + 1), :]), waits=[FREE(xnl), bg], inc=dsem(xnl))
                    t_lm = P.op("sync", lambda e, r=r, mml=mml: e.dma_start(
                        mml[:, :], d_mam[128 * r:128 * (r + 1), :]), waits=[FREE(mml), bg], inc=dsem(mml))
                    t_md = {}
                    t_zc = None
                    for t in range(NE):
                        for dns in range(4):
                            t_md[dns] = P.op("tensor", lambda e, t=t, dns=dns, ex=ex: e.matmul(
                                PS[dns][:, 0:512], ex[:, 128 * t:128 * (t + 1)],
                                sos[:, DM * t + 512 * dns:DM * t + 512 * (dns + 1)],
                                start=(t == 0), stop=(t == NE - 1)),
                                waits=[t_le, t_sos, FREE(PS[dns])] if t == 0 else [],
                                inc=s_pe if t == NE - 1 else None)
                        t_zc = P.op("tensor", lambda e, t=t, ex=ex: e.matmul(
                            PS[4][:, 0:1], ex[:, 128 * t:128 * (t + 1)], ocb[:, 0:1],
                            start=(t == 0), stop=(t == NE - 1)),
                            waits=[FREE(PS[4])] if t == 0 else [],
                            inc=s_pe if t == NE - 1 else None)
                    t_rc = P.op("vector", lambda e: e.reciprocal(
                        zcr[:, 0:1], PS[4][:, 0:1]), waits=[t_zc, FREE(zcr)], inc=s_ve)
                    SETFREE(PS[4], t_rc)
                    t_a1 = P.op("gpsimd", lambda e, xf=xf, xnl=xnl: e.tensor_tensor(
                        xf[:, :], xf[:, :], xnl[:, :], ALU.add), waits=[t_lx, t_ln], inc=s_po)
                    t_a2 = P.op("gpsimd", lambda e, xf=xf, mml=mml: e.tensor_tensor(
                        xf[:, :], xf[:, :], mml[:, :], ALU.add), waits=[t_a1, t_lm], inc=s_po)
                    SETFREE(xnl, t_a2)
                    SETFREE(mml, t_a2)
                    t_os = []
                    for dns in range(4):
                        t_o = P.op("vector", lambda e, dns=dns, xf=xf, ob=ob: e.scalar_tensor_tensor(
                            ob[:, 512 * dns:512 * (dns + 1)], PS[dns][:, 0:512],
                            zcr[:, 0:1], xf[:, 512 * dns:512 * (dns + 1)],
                            ALU.mult, ALU.add),
                            waits=[t_md[dns], t_rc, t_a2, FREE(ob) if dns == 0 else None], inc=s_ve)
                        SETFREE(PS[dns], t_o)
                        t_os.append(t_o)
                    SETFREE(zcr, t_os[-1])
                    SETFREE(xf, t_os[-1])
                    SETFREE(ex, (s_pe, s_pe.n))
                    t_st = P.op("sync", lambda e, r=r, ob=ob: e.dma_start(
                        out[128 * r:128 * (r + 1), :], ob[:, :]), waits=t_os, inc=dsem(ob))
                    SETFREE(ob, t_st)
                P.emit(nc)

        if d_m1 is not None and 2 <= phases < 4:
            P = Prog("dbg_m1")
            sdbg2 = Sem(nc, "sdbg2", 16)
            P.op("sync", lambda e: e.dma_start(d_m1[:, 0:1024], dA[:, :]), inc=sdbg2)
            P.op("sync", lambda e: e.dma_start(d_m1[:, 1024:2048], eacs[:, :]), inc=sdbg2)
            P.op("sync", lambda e: e.dma_start(d_m1[:, 2048:3072], dec[:, :]), inc=sdbg2)
            P.op("sync", lambda e: e.dma_start(d_m1[:, 3072:5120], cdrep[:, :]), inc=sdbg2)
            sdbg3 = Sem(nc, "sdbg3", 16)
            P.op("gpsimd", lambda e: e.dma_start(d_bc[0:128, :], b_fm[:, :]), inc=sdbg3)
            P.op("gpsimd", lambda e: e.dma_start(d_bc[128:256, :], c_fm[:, :]), inc=sdbg3)
            P.op("sync", lambda e: e.nop(), waits=[sdbg2.tok(), sdbg3.tok()])
            P.emit(nc)

        if d_un is not None and 1 <= phases < 3:
            P = Prog("dbg_un")
            sdbg = Sem(nc, "sdbg", 16)
            for k in range(KD):
                P.op("sync", lambda e, k=k: e.dma_start(
                    d_un[128 * k:128 * (k + 1), :], un_fm[k][:, :]), inc=sdbg)
            P.op("sync", lambda e: e.nop(), waits=[sdbg.tok()])
            P.emit(nc)

        UNS.close()
        M2S.close()
        P = Prog("drain")
        for sm in list(_dma_sems.values()) + [s_ld, s_wc]:
            if sm.n > 0:
                P.op("sync", lambda e, sm=sm: e.nop(), waits=[sm.tok()])
        P.emit(nc)

        _ = (d_x, d_sz, d_yg, d_mam, d_slotin, d_expl, d_bc, mu, w1, b1, w2, b2,
             b_fm, c_fm, dA, eacs, dec, stm, u2, slow, tri2, half, ocb, o1b, idb,
             bnw_c, cnw_c, a_rep, dtb_rep, dd_rep, t_wc_in, t_wc_out, out, d_winb, d_woutb)

    return nc


def host_consts():
    idn = np.eye(128)
    u2 = np.zeros((128, 128), np.float32)
    u2[0:64, 0:64] = np.triu(np.ones((64, 64)))
    u2[64:128, 64:128] = u2[0:64, 0:64]
    sl_ = np.zeros((128, 128), np.float32)
    sl_[0:64, 0:64] = np.tril(np.ones((64, 64)), -1)
    sl_[64:128, 64:128] = sl_[0:64, 0:64]
    tri2 = np.zeros((128, 64), np.float32)
    tri2[0:64, :] = np.triu(np.ones((64, 64)))
    tri2[64:128, :] = tri2[0:64, :]
    half = np.zeros((128, 2), np.float32)
    half[0:64, 0] = 1.0
    half[64:128, 1] = 1.0
    return {
        "c_idf": idn.astype(np.float32),
        "c_idb": idn.astype(BF),
        "c_u2": u2,
        "c_slow": sl_,
        "c_slowb": sl_.astype(BF),
        "c_tri2": tri2,
        "c_half": half,
        "c_o1f": np.ones((1, 128), np.float32),
        "c_o1b": np.ones((1, 128), BF),
        "c_ocb": np.ones((128, 1), BF),
        "c_ocf": np.ones((128, 1), np.float32),
        "c_eps": np.tile(np.pad(np.array([CONST_VALS], np.float32), ((0,0),(0,16-len(CONST_VALS)))), (128, 1)),
    }


def make_in_maps(inputs):
    cst = host_consts()
    f32c = lambda a: np.ascontiguousarray(np.asarray(a, np.float32))
    shared = dict(
        w_in=f32c(inputs["L0_attn_in_proj_w"]),
        w_out=f32c(inputs["L0_attn_out_proj_w"]),
        p_D=f32c(inputs["L0_attn_D"]).reshape(1, -1),
        p_dtb=f32c(inputs["L0_attn_dt_bias"]).reshape(1, -1),
        p_alog=f32c(inputs["L0_attn_A_log"]).reshape(1, -1),
        p_bnw=f32c(inputs["L0_attn_bnorm_w"]).reshape(1, -1),
        p_cnw=f32c(inputs["L0_attn_cnorm_w"]).reshape(1, -1),
        p_lnw=f32c(inputs["L0_attn_ln_w"]).reshape(1, -1),
        p_lnb=f32c(inputs["L0_attn_ln_b"]).reshape(1, -1),
        p_gam=f32c(inputs["L0_ffn_gamma"]).reshape(1, -1),
        p_bet=f32c(inputs["L0_ffn_beta"]).reshape(1, -1),
        p_scale=f32c(inputs["L0_ffn_scale"]).reshape(1, 1),
        mu=f32c(inputs["L0_ffn_mu"]).reshape(DM, NE * SL),
        w1=f32c(inputs["L0_ffn_W1"]),
        b1=f32c(inputs["L0_ffn_b1"]),
        w2=f32c(inputs["L0_ffn_W2"]),
        b2=f32c(inputs["L0_ffn_b2"]),
        **cst,
    )
    x = np.asarray(inputs["L0_attn_u"], np.float32)
    return [dict(shared, x_in=np.ascontiguousarray(x[c])) for c in range(NCORES)]


def kernel(**inputs):
    nc = build()
    in_maps = make_in_maps(inputs)
    res = run_bass_kernel_spmd(nc, in_maps, core_ids=list(range(NCORES)))
    return np.stack([np.asarray(r["out"], np.float32) for r in res.results], axis=0)
